# version 23
# baseline (speedup 1.0000x reference)
import contextlib
import numpy as np
import ml_dtypes

import concourse.bass as bass
import concourse.mybir as mybir
from concourse.bass_utils import run_bass_kernel_spmd

bf16 = ml_dtypes.bfloat16
F32 = mybir.dt.float32
BF = mybir.dt.bfloat16
AF = mybir.ActivationFunctionType
ALU = mybir.AluOpType
AX = mybir.AxisListType

B_CORE = 2
S = 2048
T = B_CORE * S
C = 1024
H = 16
HD = 64
BLK = 32
NBH = 32
NUM_BLOCK = 1024
BIGNEG = -239616.0
LN32 = float(np.log(32.0))
LNEPS = float(np.log(1e-6))
N_BISECT = 58
QT = 512

ENGS = ("tensor", "vector", "scalar", "sync", "gpsimd")
ESEM = {"tensor": "sPE", "vector": "sDV", "scalar": "sAC",
        "sync": "sSY", "gpsimd": "sGP", "gpsimdc": "sGC"}


class Rec:
    def __init__(self, nc, stack):
        self.nc = nc
        self.ops = {e: [] for e in ENGS}
        self.sem = {}
        self.cnt = {}
        self.seq = 0
        self.marks = {}
        for n in ("sPE", "sDV", "sAC", "sSY", "sGP", "sGC"):
            self.sem[n] = stack.enter_context(nc.semaphore(n))
            self.cnt[n] = 0

    def _push(self, eng, fn):
        self.seq += 1
        if eng == "gpsimdc":
            eng = "gpsimd"
        self.ops[eng].append((self.seq, fn))

    def mark(self, name):
        self.marks[name] = self.seq

    def emit(self, eng, fn):
        sname = ESEM[eng]
        n = 16 if eng in ("sync", "gpsimd") else 1
        if eng != "tensor" and self.cnt[sname]:
            self.wait(eng, sname, self.cnt[sname])
        self.cnt[sname] += n
        h = self.sem[sname]
        self._push(eng, lambda e, fn=fn, h=h, n=n: fn(e).then_inc(h, n))
        return self.cnt[sname]

    def wait(self, eng, sname, val):
        if not val:
            return
        h = self.sem[sname]
        self._push(eng, lambda e, h=h, v=val: e.wait_ge(h, v))

    def barrier(self, engs=ENGS):
        snap = dict(self.cnt)
        for eng in engs:
            for sname, v in snap.items():
                if sname != ESEM[eng] and v:
                    self.wait(eng, sname, v)


def _host_consts():
    cst = {}
    cst["c_id_bf"] = np.eye(128, dtype=np.float32).astype(bf16)
    cst["c_id_f32"] = np.eye(128, dtype=np.float32)
    ii = np.arange(64)
    cst["c_band"] = (np.abs(ii[:, None] - ii[None, :]) <= 1).astype(np.float32) * 5000.0
    e = np.zeros((64, 2048), np.float32)
    for cc in range(16):
        for j in range(128):
            e[4 * cc + j // 32, 128 * cc + j] = 1.0
    cst["c_eall"] = e.astype(bf16)
    cst["c_ones_row"] = np.ones((1, 128), np.float32).astype(bf16)
    return cst


def build_program(debug=False, trunc=None):
    nc = bass.Bass("TRN2", target_bir_lowering=False, debug=False)

    x_in = nc.dram_tensor("x", [T, C], F32, kind="ExternalInput").ap()
    w_in = {w: nc.dram_tensor(f"w_{w}", [C, C], F32, kind="ExternalInput").ap()
            for w in ("q", "k", "v", "r", "k2", "v2")}
    b_in = {w: nc.dram_tensor(f"b_{w}", [1, C], F32, kind="ExternalInput").ap()
            for w in ("q", "k", "v", "mur", "muk")}
    c_id_bf = nc.dram_tensor("c_id_bf", [128, 128], BF, kind="ExternalInput").ap()
    c_id_f32 = nc.dram_tensor("c_id_f32", [128, 128], F32, kind="ExternalInput").ap()
    c_band = nc.dram_tensor("c_band", [64, 64], F32, kind="ExternalInput").ap()
    c_eall = nc.dram_tensor("c_eall", [64, 2048], BF, kind="ExternalInput").ap()
    c_ones_row = nc.dram_tensor("c_ones_row", [1, 128], BF, kind="ExternalInput").ap()

    out = nc.dram_tensor("out", [T, C], F32, kind="ExternalOutput").ap()

    wT_sc = {w: nc.dram_tensor(f"wT_{w}", [8, C, 128], BF).ap()
             for w in ("q", "k", "v", "r", "k2", "v2")}
    part_sc = nc.dram_tensor("part_sc", [T, C], BF).ap()
    bsm_dram = nc.dram_tensor("bsm_dram", [NBH, 4, 16, 64], F32).ap()
    rm_dram = nc.dram_tensor("rm_dram", [NBH, 4, 16], F32).ap()
    bsm_raw = nc.dram_tensor("bsm_raw", [64, NBH, 64], F32).ap()
    selT_dram = nc.dram_tensor("selT_dram", [64, NBH * 64], F32).ap()
    rm_raw = nc.dram_tensor("rm_raw", [64, NBH], F32).ap()
    attn_sc = nc.dram_tensor("attn_sc", [B_CORE, S, C], F32).ap()

    dbg = {}
    if debug:
        for name, shape in [("d_xnT", [128, 512]), ("d_qhT", [128, 1024]),
                            ("d_norm", [64, 128]), ("d_thr", [128, 32]),
                            ("d_mv", [128, 32]), ("d_qt", [128, 512]),
                            ("d_at", [128, 1024]), ("d_ot", [65, 1024]),
                            ("d_eps", [128, 32]), ("d_la", [64, 64]),
                            ("d_stage", [128, 1024])]:
            dbg[name] = nc.dram_tensor(name, shape, F32, kind="ExternalOutput").ap()

    stack = contextlib.ExitStack()
    with stack:
        r = Rec(nc, stack)

        def sb(name, shape, dt):
            return stack.enter_context(nc.sbuf_tensor(name, shape, dt))

        xnT = sb("xnT", [128, 8 * T], BF)
        id_bf = sb("id_bf", [128, 128], BF)
        id_f = sb("id_f", [128, 128], F32)
        band = sb("band", [64, 64], F32)
        eall = sb("eall", [64, 2048], BF)
        ones_row = sb("ones_row", [1, 128], BF)
        ones128 = sb("ones128", [128, 128], F32)
        bcol = {w: sb(f"bcol_{w}", [128, 8], F32) for w in ("q", "k", "v")}
        mucol = {w: sb(f"mucol_{w}", [128, 8], F32) for w in ("r", "k")}
        mucol1m = {w: sb(f"mucol1m_{w}", [128, 8], F32) for w in ("r", "k")}
        bvrow = sb("bvrow", [1, C], BF)
        norm_all = sb("norm_all", [128, NBH * 64], F32)
        lowT_all = sb("lowT_all", [64, NBH * 64], F32)
        rm_all = sb("rm_all", [64, NBH], F32)
        selT_all = sb("selT_all", [64, NBH * 64], F32)
        bsm_all = sb("bsm_all", [64, NBH * 64], F32)
        vhp_all = sb("vhp_all", [64, NBH * 65], BF)
        thr = sb("thr", [128, 32], F32)
        mv_all = sb("mv_all", [128, NBH * 16], F32)
        eps_all = sb("eps_all", [128, NBH * 16], F32)
        cln32 = sb("cln32", [128, 1], F32)
        clneps = sb("clneps", [128, 1], F32)

        pA = stack.enter_context(nc.psum_tensor("pA", [128, 1024], F32))

        xtile = [sb("xtile0", [128, C], F32), sb("xtile1", [128, C], F32)]
        xnrow = sb("xnrow", [128, C], BF)
        sq_dump = sb("sq_dump", [128, C], F32)
        col_a = sb("col_a", [128, 2], F32)
        col_b = sb("col_b", [128, 2], F32)
        col_c = sb("col_c", [128, 2], F32)
        xnrow2 = sb("xnrow2", [128, C], BF)
        xnh = sb("xnh", [128, 1024], BF)
        small_f = sb("small_f", [64, 64], F32)
        sel_buf = sb("sel_buf", [64, 64], F32)

        for dst, src in [(id_bf[:, :], c_id_bf), (id_f[:, :], c_id_f32),
                         (band[:, :], c_band), (eall[:, :], c_eall),
                         (ones_row[:, :], c_ones_row)]:
            r.emit("sync", lambda e, d=dst, s=src: e.dma_start(d, s))
        ngp_bv = r.emit("gpsimd", lambda e: e.dma_start(bvrow[:, :], b_in["v"]))
        for i, w in enumerate(["q", "k", "v", "mur", "muk"]):
            r.emit("sync", lambda e, i=i, s=b_in[w]: e.dma_start(
                brows[:, i * 128:(i + 1) * 128],
                s.rearrange("o (p j) -> (o p) j", p=8)))
        n_sy_const = r.cnt["sSY"]

        r.emit("vector", lambda e: e.memset(ones128[:, :], 1.0))
        r.emit("vector", lambda e: e.memset(cln32[:, :], LN32))
        r.emit("vector", lambda e: e.memset(clneps[:, :], LNEPS))

        r.wait("tensor", "sSY", n_sy_const)
        for i in range(5):
            r.emit("tensor", lambda e, i=i: e.transpose(
                pA[:, i * 8:(i + 1) * 8], brows[:, i * 128:(i + 1) * 128],
                id_f[0:8, 0:8]))
        npe = r.cnt["sPE"]
        r.wait("vector", "sPE", npe)
        for i, dst in enumerate([bcol["q"], bcol["k"], bcol["v"],
                                 mucol["r"], mucol["k"]]):
            r.emit("vector", lambda e, i=i, d=dst: e.tensor_copy(
                d[:, :], pA[:, i * 8:(i + 1) * 8]))
        for w in ("r", "k"):
            r.emit("vector", lambda e, w=w: e.tensor_scalar(
                mucol1m[w][:, :], mucol[w][:, :], -1.0, 1.0, ALU.mult, ALU.add))
        r.barrier()

        with contextlib.ExitStack() as st_p0:
            pT = st_p0.enter_context(nc.psum_tensor("pT", [128, 1024], BF))
            pTb = st_p0.enter_context(nc.psum_tensor("pTb", [128, 1024], BF))
            pT2 = [pT, pTb]
            brows = st_p0.enter_context(nc.sbuf_tensor("brows", [8, 5 * 128], F32))
            qhT = st_p0.enter_context(nc.sbuf_tensor("qhT", [128, 1024], F32))
            khT = st_p0.enter_context(nc.sbuf_tensor("khT", [128, 1024], F32))
            vhT = st_p0.enter_context(nc.sbuf_tensor("vhT", [128, 1024], F32))
            prev_consume = [0, 0]
            prev_pt_dv = [0, 0]
            xnrows = None
            for t in range(32):
                par = t % 2
                ca = col_a[:, par:par + 1]
                cb = col_b[:, par:par + 1]
                cc_ = col_c[:, par:par + 1]
                xr_ = [xnrow, xnrow2][par]
                sq_ = sq_dump[:, par * 512:par * 512 + 512]
                r.wait("sync", "sAC", prev_consume[par])
                nld = r.emit("sync", lambda e, t=t, p=par: e.dma_start(
                    xtile[p][:, :], x_in[t * 128:(t + 1) * 128, :]))
                r.wait("scalar", "sSY", nld)
                r.emit("scalar", lambda e, p=par, ca=ca: e.activation(
                    sq_dump[:, :], xtile[p][:, :], AF.Square, accum_out=ca))
                nac = r.cnt["sAC"]
                r.wait("vector", "sAC", nac)
                r.emit("vector", lambda e, p=par, cb=cb: e.tensor_reduce(
                    cb, xtile[p][:, :], axis=AX.X, op=ALU.add))
                r.emit("vector", lambda e, cb=cb: e.tensor_scalar(
                    cb, cb, 1.0 / C, None, ALU.mult))
                r.emit("vector", lambda e, ca=ca: e.tensor_scalar(
                    ca, ca, 1.0 / C, None, ALU.mult))
                r.emit("vector", lambda e, cb=cb, cc_=cc_: e.tensor_tensor(
                    cc_, cb, cb, ALU.mult))
                r.emit("vector", lambda e, ca=ca, cc_=cc_: e.tensor_tensor(
                    ca, ca, cc_, ALU.subtract))
                ndv = r.emit("vector", lambda e, ca=ca: e.tensor_scalar(
                    ca, ca, 1e-5, None, ALU.add))
                r.wait("scalar", "sDV", ndv)
                nac2 = r.emit("scalar", lambda e, ca=ca: e.activation(
                    ca, ca, AF.Sqrt))
                r.wait("vector", "sAC", nac2)
                r.emit("vector", lambda e, ca=ca, cc_=cc_: e.reciprocal(cc_, ca))
                r.emit("vector", lambda e, cb=cb, cc_=cc_: e.tensor_tensor(
                    cb, cb, cc_, ALU.mult))
                ndv2 = r.emit("vector", lambda e, cb=cb: e.tensor_scalar(
                    cb, cb, -1.0, None, ALU.mult))
                r.wait("scalar", "sDV", ndv2)
                r.wait("scalar", "sPE", prev_pt_dv[par] and 0 or 0)
                nac3 = r.emit("scalar", lambda e, p=par, xr_=xr_, cb=cb, cc_=cc_:
                              e.activation(xr_[:, :], xtile[p][:, :], AF.Identity,
                                           bias=cb, scale=cc_))
                prev_consume[par] = nac3
                r.wait("tensor", "sAC", nac3)
                r.wait("tensor", "sDV", prev_pt_dv[par])
                for c in range(8):
                    r.emit("tensor", lambda e, c=c, xr_=xr_, p=par: e.transpose(
                        pT2[p][:, c * 128:(c + 1) * 128],
                        xr_[:, c * 128:(c + 1) * 128], id_bf[:, :]))
                npe_t = r.cnt["sPE"]
                r.wait("vector", "sPE", npe_t)
                prev_pt_dv[par] = r.emit("vector", lambda e, t=t, p=par: e.tensor_copy(
                    xnT.rearrange("p (c t) -> p c t", c=8)[:, :, t * 128:(t + 1) * 128],
                    pT2[p].ap().rearrange("p (c x) -> p c x", c=8)))
                r.wait("scalar", "sPE", npe_t)
            r.barrier()

            r.mark("P0b")
            for c in range(8):
                r.emit("vector", lambda e, c=c: e.tensor_reduce(
                    sq_dump[:, c * 128:(c + 1) * 128],
                    xnT[:, c * T:(c + 1) * T].rearrange("p (n b) -> p n b", b=BLK),
                    axis=AX.X, op=ALU.add))
            ndvh = r.emit("vector", lambda e: e.tensor_scalar(
                xnh[:, :], sq_dump[:, :], 1.0 / BLK, None, ALU.mult))

            wbuf2 = [st_p0.enter_context(nc.sbuf_tensor(f"wbuf{j}", [128, C], BF))
                     for j in range(2)]
            wTsb2 = st_p0.enter_context(nc.sbuf_tensor("wTsb2", [128, C], BF))
            wp = {"nld": {}, "npe": {}, "nst": 0, "wc": [0, 0], "g": 0}

            def wpd_load(tag, d):
                g = wp["g"]
                wp["g"] = g + 1
                r.wait("gpsimd", "sPE", wp["npe"].get(g - 2, 0))
                wp["nld"][(tag, d)] = r.emit(
                    "gpsimd", lambda e, tag=tag, d=d, g=g: e.dma_start(
                        wbuf2[g % 2][:, :], w_in[tag][d * 128:(d + 1) * 128, :]))
                wp["nld"][(tag, d)] = (wp["nld"][(tag, d)], g)

            def wpd_proc(tag, d):
                nld, g = wp["nld"][(tag, d)]
                r.wait("tensor", "sGP", nld)
                r.wait("tensor", "sAC", r.cnt["sAC"])
                r.wait("tensor", "sDV", r.cnt["sDV"])
                wpt = pT2[g % 2]
                for cc in range(8):
                    r.emit("tensor", lambda e, cc=cc, wpt=wpt, g=g: e.transpose(
                        wpt[:, cc * 128:(cc + 1) * 128],
                        wbuf2[g % 2][:, cc * 128:(cc + 1) * 128],
                        id_bf[:, :]))
                npe_w = r.cnt["sPE"]
                wp["npe"][g] = npe_w
                r.wait("vector", "sPE", npe_w)
                r.wait("vector", "sSY", wp["nst"])
                nwc = r.emit("vector", lambda e, wpt=wpt: e.tensor_copy(
                    wTsb2[:, :], wpt[:, :]))
                r.wait("sync", "sDV", nwc)
                wp["nst"] = r.emit("sync", lambda e, tag=tag, d=d: e.dma_start(
                    wT_sc[tag][d].rearrange("(cc p) j -> p cc j", p=128),
                    wTsb2.rearrange("p (cc x) -> p cc x", cc=8)))
                if tag in ("q", "k", "v"):
                    dstT = {"q": qhT, "k": khT, "v": vhT}[tag]
                    r.wait("tensor", "sDV", max(ndvh, nwc))
                    for cc in range(8):
                        r.emit("tensor", lambda e, cc=cc: e.matmul(
                            pA[:, 0:128],
                            wTsb2[:, cc * 128:(cc + 1) * 128],
                            xnh[:, cc * 128:(cc + 1) * 128],
                            start=(cc == 0), stop=(cc == 7)))
                    npe_m = r.cnt["sPE"]
                    r.wait("vector", "sPE", npe_m)
                    r.emit("vector", lambda e, d=d, dstT=dstT, tag=tag:
                           e.tensor_scalar(
                               dstT[:, d * 128:(d + 1) * 128], pA[:, 0:128],
                               bcol[tag][:, d:d + 1], None, ALU.add))
                r.wait("tensor", "sDV", r.cnt["sDV"])

            for tag in ("r", "k2", "v2"):
                wpd_load(tag, 0)
                wpd_load(tag, 1)
                for d in range(8):
                    wpd_proc(tag, d)
                    if d + 2 < 8:
                        wpd_load(tag, d + 2)
            r.mark("P0c")
            nb128 = st_p0.enter_context(nc.sbuf_tensor("nb128", [128, 1024], F32))
            bis_lo = st_p0.enter_context(nc.sbuf_tensor("bis_lo", [128, 32], F32))
            bis_hi = st_p0.enter_context(nc.sbuf_tensor("bis_hi", [128, 32], F32))
            bis_mid = st_p0.enter_context(nc.sbuf_tensor("bis_mid", [128, 32], F32))
            bis_ge = st_p0.enter_context(nc.sbuf_tensor("bis_ge", [128, 1024], BF))
            bis_cnt = st_p0.enter_context(nc.sbuf_tensor("bis_cnt", [128, 32], F32))
            bis_t1 = st_p0.enter_context(nc.sbuf_tensor("bis_t1", [128, 32], F32))
            bis_t2 = st_p0.enter_context(nc.sbuf_tensor("bis_t2", [128, 32], F32))

            def bh_idx(p, hip, s):
                return (2 * p + hip) * 2 + s

            def emit_low_bh(p, hip, sx):
                b = bh_idx(p, hip, sx)
                p0 = 64 * hip
                cl = p * 128 + sx * 64
                r.wait("tensor", "sAC", r.cnt["sAC"])
                r.wait("tensor", "sDV", r.cnt["sDV"])
                r.emit("tensor", lambda e, p0=p0, cl=cl: e.matmul(
                    pA[0:64, 0:64], qhT[p0:p0 + 64, cl:cl + 64],
                    khT[p0:p0 + 64, cl:cl + 64], start=True, stop=True))
                npe_l = r.emit("tensor", lambda e, p0=p0, cl=cl: e.matmul(
                    pA[0:64, 64:128], khT[p0:p0 + 64, cl:cl + 64],
                    qhT[p0:p0 + 64, cl:cl + 64], start=True, stop=True))
                r.wait("vector", "sPE", npe_l)
                r.emit("vector", lambda e: e.tensor_reduce(
                    col_a[0:64, 0:1], pA[0:64, 0:64], axis=AX.X, op=ALU.max))
                r.emit("vector", lambda e: e.tensor_scalar(
                    small_f[:, :], pA[0:64, 0:64], col_a[0:64, 0:1], 0.125,
                    ALU.subtract, ALU.mult))
                r.emit("vector", lambda e, b=b: e.tensor_tensor(
                    norm_all[0:64, b * 64:(b + 1) * 64], small_f[:, :],
                    band[:, :], ALU.add))
                r.emit("vector", lambda e, b=b: e.tensor_copy(
                    lowT_all[:, b * 64:(b + 1) * 64], pA[0:64, 64:128]))
                ndv_l = r.emit("vector", lambda e, b=b: e.tensor_scalar(
                    rm_all[:, b:b + 1], col_a[0:64, 0:1], 0.125, None, ALU.mult))
                r.wait("tensor", "sDV", ndv_l)

            def emit_vhp_bh(p, hip, sx):
                b = bh_idx(p, hip, sx)
                p0 = 64 * hip
                cl = p * 128 + sx * 64
                r.wait("tensor", "sAC", r.cnt["sAC"])
                r.wait("tensor", "sDV", r.cnt["sDV"])
                idsl = id_f[0:64, 0:64] if hip == 0 else id_f[64:128, 64:128]
                npe_v = r.emit("tensor", lambda e, p0=p0, cl=cl, idsl=idsl:
                               e.transpose(pA[0:64, 0:64],
                                           vhT[p0:p0 + 64, cl:cl + 64], idsl))
                r.wait("vector", "sPE", npe_v)
                r.emit("vector", lambda e, b=b: e.tensor_copy(
                    vhp_all[:, b * 65:b * 65 + 64], pA[0:64, 0:64]))
                ndv_v = r.emit("vector", lambda e, b=b: e.memset(
                    vhp_all[:, b * 65 + 64:b * 65 + 65], 1.0))
                r.wait("tensor", "sDV", ndv_v)

            def emit_nb128():
                ndv_n = r.cnt["sDV"]
                r.wait("gpsimd", "sDV", ndv_n)
                for b in range(NBH):
                    r.emit("gpsimd", lambda e, b=b: e.dma_start(
                        nb128[0:64, b * 32:(b + 1) * 32],
                        norm_all[0:64, b * 64:b * 64 + 32]))
                    r.emit("gpsimd", lambda e, b=b: e.dma_start(
                        nb128[64:128, b * 32:(b + 1) * 32],
                        norm_all[0:64, b * 64 + 32:b * 64 + 64]))
                ngp_n = r.cnt["sGP"]
                r.emit("vector", lambda e: e.memset(bis_lo[:, :], -100000.0))
                r.emit("vector", lambda e: e.memset(bis_hi[:, :], 100000.0))
                r.wait("vector", "sGP", ngp_n)

            pB1 = st_p0.enter_context(nc.psum_tensor("pB1", [128, 32], F32))
            pB1s = st_p0.enter_context(nc.psum_tensor("pB1s", [64, 64], F32))

            def emit_bisect_iter(it):
                r.emit("vector", lambda e: e.tensor_tensor(
                    bis_mid[:, :], bis_lo[:, :], bis_hi[:, :], ALU.add))
                r.emit("vector", lambda e: e.tensor_scalar(
                    bis_mid[:, :], bis_mid[:, :], 0.5, None, ALU.mult))
                r.emit("vector", lambda e: e.tensor_tensor(
                    bis_ge[:, :], nb128[:, :],
                    bis_mid.ap().rearrange("p (h o) -> p h o", o=1)
                    .broadcast_to([128, 32, 32]), ALU.is_ge))
                ndv_c = r.emit("vector", lambda e: e.tensor_reduce(
                    bis_cnt[:, :], bis_ge.ap().rearrange("p (h e) -> p h e", e=32),
                    axis=AX.X, op=ALU.add))
                r.wait("tensor", "sDV", ndv_c)
                npe_c = r.emit("tensor", lambda e: e.matmul(
                    pB1[:, :], ones128[:, :], bis_cnt[:, :], start=True, stop=True))
                r.wait("vector", "sPE", npe_c)
                r.emit("vector", lambda e: e.tensor_scalar(
                    bis_t1[:, :], pB1[:, :], float(NUM_BLOCK), None, ALU.is_ge))
                r.emit("vector", lambda e: e.tensor_scalar(
                    bis_t2[:, :], pB1[:, :], float(NUM_BLOCK), None, ALU.is_lt))
                r.emit("vector", lambda e: e.tensor_tensor(
                    bis_cnt[:, :], bis_t1[:, :], bis_mid[:, :], ALU.mult))
                r.emit("vector", lambda e: e.tensor_tensor(
                    bis_lo[:, :], bis_t2[:, :], bis_lo[:, :], ALU.mult))
                r.emit("vector", lambda e: e.tensor_tensor(
                    bis_lo[:, :], bis_lo[:, :], bis_cnt[:, :], ALU.add))
                r.emit("vector", lambda e: e.tensor_tensor(
                    bis_cnt[:, :], bis_t2[:, :], bis_mid[:, :], ALU.mult))
                r.emit("vector", lambda e: e.tensor_tensor(
                    bis_hi[:, :], bis_t1[:, :], bis_hi[:, :], ALU.mult))
                ndv_u = r.emit("vector", lambda e: e.tensor_tensor(
                    bis_hi[:, :], bis_hi[:, :], bis_cnt[:, :], ALU.add))
                r.wait("tensor", "sDV", ndv_u)

            def emit_sel_all():
                r.emit("vector", lambda e: e.tensor_copy(thr[:, :], bis_lo[:, :]))
                for b in range(NBH):
                    r.emit("vector", lambda e, b=b: e.tensor_scalar(
                        sel_buf[:, :], norm_all[0:64, b * 64:(b + 1) * 64],
                        thr[0:64, b:b + 1], None, ALU.is_ge))
                    ndv_s = r.emit("vector", lambda e, b=b: e.tensor_scalar(
                        bsm_all[:, b * 64:(b + 1) * 64], sel_buf[:, :], 1.0, -BIGNEG,
                        ALU.subtract, ALU.mult))
                    r.wait("tensor", "sDV", ndv_s)
                    npe_s = r.emit("tensor", lambda e: e.transpose(
                        pB1s[0:64, 0:64], sel_buf[:, :], id_f[0:64, 0:64]))
                    r.wait("vector", "sPE", npe_s)
                    ndv_s2 = r.emit("vector", lambda e, b=b: e.tensor_copy(
                        selT_all[:, b * 64:(b + 1) * 64], pB1s[0:64, 0:64]))
                    r.wait("tensor", "sDV", ndv_s2)
                r.wait("sync", "sDV", r.cnt["sDV"])
                nT1 = r.emit("sync", lambda e: e.dma_start(selT_dram, selT_all[:, :]))
                r.wait("sync", "sSY", nT1)
                r.emit("sync", lambda e: e.dma_start(norm_all[64:128, :], selT_dram))
                nraw = r.emit("sync", lambda e: e.dma_start(
                    bsm_raw, bsm_all.ap().rearrange("q (b k) -> q b k", b=NBH)))
                nraw2 = r.emit("sync", lambda e: e.dma_start(rm_raw, rm_all[:, :]))
                r.wait("sync", "sSY", nraw2)
                for j in range(4):
                    r.emit("sync", lambda e, j=j: e.dma_start(
                        bsm_dram[:, j],
                        bsm_raw.rearrange("(t j) b k -> j b t k", j=4)[j]))
                    r.emit("sync", lambda e, j=j: e.dma_start(
                        rm_dram[:, j],
                        rm_raw.rearrange("(t j) b -> j b t", j=4)[j],
                        allow_slow_non_contiguous=True))


            r.mark("P05")
            fA = st_p0.enter_context(nc.sbuf_tensor("fA", [128, 8 * QT], BF))
            fB = st_p0.enter_context(nc.sbuf_tensor("fB", [128, 8 * QT], BF))
            fR = st_p0.enter_context(nc.sbuf_tensor("fR", [128, 8 * QT], BF))
            fF = st_p0.enter_context(nc.sbuf_tensor("fF", [128, 8 * QT], BF))
            fW = [st_p0.enter_context(nc.sbuf_tensor(f"fW{j}", [128, 8 * 128], BF))
                  for j in range(3)]
            ptile = st_p0.enter_context(nc.sbuf_tensor("ptile", [128, C], BF))
            ptileb = st_p0.enter_context(nc.sbuf_tensor("ptileb", [128, C], BF))
            ptile2 = [ptile, ptileb]

            def mu_mix(dst, mu_key, t0):
                for c in range(8):
                    base = c * T + t0
                    if t0 % S == 0:
                        r.emit("vector", lambda e, d=dst, c=c: e.memset(
                            d[:, c * QT:c * QT + 1], 0.0))
                        r.emit("vector", lambda e, d=dst, c=c, base=base, m=mu_key:
                               e.tensor_scalar(
                                   d[:, c * QT + 1:(c + 1) * QT],
                                   xnT[:, base:base + QT - 1],
                                   mucol1m[m][:, c:c + 1], None, ALU.mult))
                    else:
                        r.emit("vector", lambda e, d=dst, c=c, base=base, m=mu_key:
                               e.tensor_scalar(
                                   d[:, c * QT:(c + 1) * QT],
                                   xnT[:, base - 1:base + QT - 1],
                                   mucol1m[m][:, c:c + 1], None, ALU.mult))
                    r.emit("vector", lambda e, d=dst, c=c, base=base, m=mu_key:
                           e.scalar_tensor_tensor(
                               d[:, c * QT:(c + 1) * QT],
                               xnT[:, base:base + QT],
                               mucol[m][:, c:c + 1],
                               d[:, c * QT:(c + 1) * QT],
                               ALU.mult, ALU.add))

            def load_wslice(buf, tag, d):
                return r.emit("sync", lambda e, buf=buf, tag=tag, d=d: e.dma_start(
                    buf.rearrange("p (cc j) -> p cc j", cc=8),
                    wT_sc[tag][d].rearrange("(cc p) j -> p cc j", p=128)))

            def mm_quarter(tag, rhs_buf, d, wbuf_par, slot):
                for cc in range(8):
                    r.emit("tensor", lambda e, cc=cc, w=wbuf_par, rb=rhs_buf, sl=slot:
                           e.matmul(pA[:, sl * 512:sl * 512 + QT],
                                    w[:, cc * 128:(cc + 1) * 128],
                                    rb[:, cc * QT:(cc + 1) * QT],
                                    start=(cc == 0), stop=(cc == 7)))
                return r.cnt["sPE"]

            wsched = [(tag, d) for _Q in range(8)
                      for tag in ("r", "k2", "v2") for d in range(8)]
            wst = {"ld": {}, "mm": {}}

            def wload(i):
                if i >= len(wsched):
                    return
                if i - 3 >= 0:
                    r.wait("sync", "sPE", wst["mm"][i - 3])
                tag, d = wsched[i]
                wst["ld"][i] = load_wslice(fW[i % 3], tag, d)

            for i in range(3):
                wload(i)

            bh_list = [(p, hip, sx) for p in range(8) for hip in range(2)
                       for sx in range(B_CORE)]
            site_work = {}

            def addw(si, fn):
                site_work.setdefault(si, []).append(fn)

            for j, tag in enumerate(("q", "k", "v")):
                base = 2 + j * 4
                for ch in range(4):
                    for d2 in (0, 1):
                        d = ch * 2 + d2
                        addw(base + ch,
                             lambda tag=tag, d=d: wpd_proc(tag, d))
                        if d + 2 < 8:
                            addw(base + ch,
                                 lambda tag=tag, d=d: wpd_load(tag, d + 2))
                nxt = {"q": "k", "k": "v", "v": None}[tag]
                if nxt is not None:
                    addw(base + 3, lambda nxt=nxt: wpd_load(nxt, 0))
                    addw(base + 3, lambda nxt=nxt: wpd_load(nxt, 1))
            for j, si in enumerate((11, 12, 13, 14)):
                for bb in bh_list[j * 8:(j + 1) * 8]:
                    addw(si, lambda bb=bb: emit_low_bh(*bb))
            for bb in bh_list:
                addw(15, lambda bb=bb: emit_vhp_bh(*bb))
            addw(15, emit_nb128)
            for si in range(16, 24):
                for it in range((si - 16) * 8, (si - 16) * 8 + 8):
                    if it < N_BISECT:
                        addw(si, lambda it=it: emit_bisect_iter(it))

            def do_site(si):
                for fn in site_work.get(si, ()):
                    fn()

            wpd_load("q", 0)
            wpd_load("q", 1)
            r.wait("vector", "sPE", r.cnt["sPE"])
            for Q in range(8):
                t0 = Q * QT
                mu_mix(fA, "r", t0)
                ndv_mixr = r.cnt["sDV"]
                cons = [r.cnt["sAC"], r.cnt["sAC"]]
                r.wait("tensor", "sDV", r.cnt["sDV"])
                for d in range(8):
                    par = d % 2
                    gi = Q * 24 + d
                    r.wait("tensor", "sSY", wst["ld"][gi])
                    if d == 0:
                        r.wait("tensor", "sDV", ndv_mixr)
                    r.wait("tensor", "sAC", cons[par])
                    npe_q = mm_quarter("r", fA, d, fW[gi % 3], par)
                    wst["mm"][gi] = npe_q
                    wload(gi + 3)
                    r.wait("scalar", "sPE", npe_q)
                    cons[par] = r.emit("scalar", lambda e, d=d, par=par: e.activation(
                        fR[:, d * QT:(d + 1) * QT], pA[:, par * 512:par * 512 + QT],
                        AF.Sigmoid))
                do_site(Q * 3 + 0)
                r.wait("vector", "sPE", r.cnt["sPE"])
                mu_mix(fA, "k", t0)
                ndv_mixk = r.cnt["sDV"]
                consd = [r.cnt["sDV"], r.cnt["sDV"]]
                r.wait("tensor", "sAC", r.cnt["sAC"])
                for d in range(8):
                    par = d % 2
                    gi = Q * 24 + 8 + d
                    r.wait("tensor", "sSY", wst["ld"][gi])
                    if d == 0:
                        r.wait("tensor", "sDV", ndv_mixk)
                    r.wait("tensor", "sDV", consd[par])
                    npe_q = mm_quarter("k2", fA, d, fW[gi % 3], par)
                    wst["mm"][gi] = npe_q
                    wload(gi + 3)
                    r.wait("vector", "sPE", npe_q)
                    consd[par] = r.emit("vector", lambda e, d=d, par=par: e.tensor_scalar(
                        fB[:, d * QT:(d + 1) * QT], pA[:, par * 512:par * 512 + QT],
                        0.0, None, ALU.max))
                do_site(Q * 3 + 1)
                r.wait("scalar", "sDV", r.cnt["sDV"])
                r.wait("scalar", "sPE", r.cnt["sPE"])
                nac_sq = None
                for d in range(8):
                    nac_sq = r.emit("scalar", lambda e, d=d: e.activation(
                        fA[:, d * QT:(d + 1) * QT], fB[:, d * QT:(d + 1) * QT],
                        AF.Square))
                r.wait("tensor", "sAC", nac_sq)
                consd = [r.cnt["sDV"], r.cnt["sDV"]]
                r.wait("tensor", "sAC", r.cnt["sAC"])
                for d in range(8):
                    par = d % 2
                    gi = Q * 24 + 16 + d
                    r.wait("tensor", "sSY", wst["ld"][gi])
                    r.wait("tensor", "sDV", consd[par])
                    npe_q = mm_quarter("v2", fA, d, fW[gi % 3], par)
                    wst["mm"][gi] = npe_q
                    wload(gi + 3)
                    r.wait("vector", "sPE", npe_q)
                    consd[par] = r.emit("vector", lambda e, d=d, par=par: e.tensor_tensor(
                        fF[:, d * QT:(d + 1) * QT], pA[:, par * 512:par * 512 + QT],
                        fR[:, d * QT:(d + 1) * QT], ALU.mult))
                r.wait("tensor", "sDV", r.cnt["sDV"])
                prev_dma = [0, 0]
                prev_add = [0, 0]
                for tt in range(QT // 128):
                    par = tt % 2
                    wpt = pT2[par]
                    r.wait("tensor", "sDV", prev_add[par])
                    for d in range(8):
                        r.emit("tensor", lambda e, d=d, tt=tt, wpt=wpt: e.transpose(
                            wpt[:, d * 128:(d + 1) * 128],
                            fF[:, d * QT + tt * 128:d * QT + (tt + 1) * 128],
                            id_bf[:, :]))
                    npe_tt = r.cnt["sPE"]
                    r.wait("vector", "sPE", npe_tt)
                    r.wait("vector", "sGP", prev_dma[par])
                    ndv_tt = r.emit("vector", lambda e, p=par, wpt=wpt:
                                    e.tensor_copy(ptile2[p][:, :], wpt[:, :]))
                    prev_add[par] = ndv_tt
                    r.wait("gpsimd", "sDV", ndv_tt)
                    prev_dma[par] = r.emit("gpsimd", lambda e, t0=t0, tt=tt, p=par:
                                           e.dma_start(part_sc[t0 + tt * 128:
                                                               t0 + (tt + 1) * 128, :],
                                                       ptile2[p][:, :]))
                do_site(Q * 3 + 2)
                r.barrier()
            emit_sel_all()
            r.barrier()
            if debug:
                r.emit("gpsimd", lambda e: e.dma_start(dbg["d_norm"], norm_all[0:64, 0:128]))
                r.emit("gpsimd", lambda e: e.dma_start(dbg["d_thr"], thr[:, :]))
                r.emit("gpsimd", lambda e: e.dma_start(dbg["d_xnT"], xnT[:, 0:512]))
                r.emit("gpsimd", lambda e: e.dma_start(dbg["d_qhT"], qhT[:, :]))
                r.barrier()

        r.mark("P1")
        with contextlib.ExitStack() as st_p2:
            def ps2(name, shape, dt=F32):
                return st_p2.enter_context(nc.psum_tensor(name, shape, dt))
            pS = [ps2("pS0", [128, 1024]), ps2("pS1", [128, 1024])]
            pO = ps2("pO", [128, 1024])

            qtp = sb("qtp", [128, T], BF)
            ktp = sb("ktp", [128, T], BF)
            vtm = sb("vtm", [128, 32 * 130], BF)
            wsl = {w: sb(f"wsl_{w}", [128, 8 * 128], BF) for w in ("q", "k", "v")}
            lhsc = sb("lhsc", [128, 2048], BF)
            rhsc = sb("rhsc", [128, 2048], BF)
            aT = [sb("aT0", [128, 1024], BF), sb("aT1", [128, 1024], BF)]
            laT = sb("laT", [64, 64], BF)
            la_t1 = sb("la_t1", [64, 64], F32)
            la_t2 = sb("la_t2", [64, 64], F32)
            bm = sb("bm", [128, 64], F32)
            bm2c = [bm, sb("bm1", [128, 64], F32)]
            bmm = sb("bmm", [128, 64], F32)
            bx2 = [sb("bx0", [128, 1024], BF), sb("bx1", [128, 1024], BF)]
            rmb2 = [sb("rmb0", [128, 16], F32), sb("rmb1", [128, 16], F32)]
            epst = sb("epst", [128, 16], F32)
            outT = sb("outT", [65, 2048], F32)
            stage = sb("stage", [128, 1024], F32)
            stage2 = [stage, sb("stage1", [128, 1024], F32)]
            rden = sb("rden", [128, 1], F32)
            rdenw16 = sb("rdenw16", [128, 16], F32)
            vTf = sb("vTf", [128, 2048], BF)
            fslice = [vTf[:, 0:1024], vTf[:, 1024:2048]]
            dcol = sb("dcol", [128, 1], F32)

            r.emit("vector", lambda e: e.memset(
                vtm.ap().rearrange("p (k h o) -> p k h o", k=32, h=2)[:, :, :, 64:65], 1.0))
            ndv_pre = r.emit("gpsimd", lambda e: e.dma_start(lhsc[64:128, :], eall[:, :]))
            r.barrier()

            def bh_idx(p, hip, s):
                return (2 * p + hip) * 2 + s

            rhs2 = [rhsc, st_p2.enter_context(nc.sbuf_tensor("rhsc1", [128, 2048], BF))]
            lhs2 = [lhsc, st_p2.enter_context(nc.sbuf_tensor("lhsc1", [128, 2048], BF))]
            laT2 = [laT, st_p2.enter_context(nc.sbuf_tensor("laT1", [64, 64], BF))]
            r.emit("gpsimd", lambda e: e.dma_start(lhs2[1][64:128, :], eall[:, :]))
            r.barrier()

            state = {"ndv_ep": 0, "prev_evac": 0, "chunk_pe_end": 0,
                     "nac_la": [0, 0], "ngp_prep": [0, 0], "ndv_prep": [0, 0],
                     "ngp_bx": [0, 0], "nac_eps": 0, "ngc_mv": [0, 0],
                     "nac_la_last": 0, "ngp_prep_last": 0,
                     "prev_exp": [0, 0], "phA_dv": 0, "phA_done_dv": [0, 0], "pO_dv": 0,
                     "nst_a": 0, "ngp_rmb": [0, 0], "prev_exp2": [0, 0], "phA_dv4": [0, 0, 0, 0], "nac_ep": 0,
                     "pcons": [0, 0], "nac_qk": 0, "nsy_part": [0, 0], "ngp_add": [0, 0],
                     "nst_a2": [0, 0], "pA_free_ac": [0, 0], "pending_w1": None, "npe_vtr": 0,
                     "nld_f": [0, 0]}

            def emit_prep_bx(p, hip, sq):
                b = bh_idx(p, hip, sq)
                bpar = b % 2
                r.wait("gpsimd", "sDV", state["phA_done_dv"][bpar])
                state["ngp_bx"][bpar] = r.emit(
                    "gpsimd", lambda e, b=b, bpar=bpar: e.dma_start(
                        bx2[bpar][:, :],
                        bsm_dram[b].rearrange("j t k -> j (t k)")
                        .unsqueeze(1).broadcast_to([4, 32, 1024])))
                state["ngp_rmb"][bpar] = r.emit(
                    "gpsimd", lambda e, b=b, bpar=bpar: e.dma_start(
                        rmb2[bpar][:, :],
                        rm_dram[b].unsqueeze(1).broadcast_to([4, 32, 16])))

            def emit_prep(p, hip, sq, par, prev_chunk_pe, with_bx=False):
                b = bh_idx(p, hip, sq)
                p0 = 64 * hip
                ts0 = sq * S
                r.wait("gpsimd", "sPE", prev_chunk_pe)
                r.emit("gpsimd", lambda e, p0=p0, ts0=ts0, par=par: e.dma_start(
                    rhs2[par][0:64, :], qtp[p0:p0 + 64, ts0:ts0 + S]))
                r.emit("gpsimd", lambda e, p0=p0, ts0=ts0, par=par: e.dma_start(
                    lhs2[par][0:64, :], ktp[p0:p0 + 64, ts0:ts0 + S]))
                ngp_qk = r.cnt["sGP"]
                if with_bx:
                    emit_prep_bx(p, hip, sq)
                r.wait("vector", "sPE", prev_chunk_pe)
                ngp5 = r.emit("vector", lambda e, b=b, par=par: e.tensor_scalar(
                    rhs2[par][64:128, :],
                    norm_all[64:128, b * 64:(b + 1) * 64]
                    .rearrange("p (q o) -> p q o", o=1)
                    .broadcast_to([64, 64, 32]),
                    1.0, -BIGNEG, ALU.subtract, ALU.mult))
                r.wait("vector", "sAC", state["nac_la_last"])
                r.emit("vector", lambda e, b=b: e.tensor_scalar(
                    la_t1[:, :], lowT_all[:, b * 64:(b + 1) * 64],
                    0.125, None, ALU.mult))
                r.emit("vector", lambda e, b=b: e.tensor_scalar(
                    la_t2[:, :], selT_all[:, b * 64:(b + 1) * 64],
                    10000.0, None, ALU.mult))
                ndv_la = r.emit("vector", lambda e: e.tensor_tensor(
                    la_t1[:, :], la_t1[:, :], la_t2[:, :], ALU.subtract))
                r.wait("scalar", "sDV", ndv_la)
                state["nac_la"][par] = r.emit("scalar", lambda e, par=par: e.activation(
                    laT2[par][:, :], la_t1[:, :], AF.Exp, bias=cln32[0:64, :]))
                state["nac_la_last"] = state["nac_la"][par]
                state["ngp_prep"][par] = ngp_qk
                state["ndv_prep"][par] = ngp5

            def emit_phaseA_half(p, hip, sq, u, slots=(0, 1)):
                b = bh_idx(p, hip, sq)
                bpar = b % 2
                bmp = bm2c[bpar]
                p0 = 64 * hip
                ts0 = sq * S
                t, kq = u // 4, u % 4
                sl4 = slots[u % len(slots)]
                pap = [pA[:, 0:512], pA[:, 512:1024],
                       pO[:, 0:512], pO[:, 512:1024]][sl4]
                r.wait("tensor", "sDV", state["phA_dv4"][sl4])
                if sl4 < 2:
                    r.wait("tensor", "sAC", state["pcons"][sl4])
                    r.wait("tensor", "sAC", state["pA_free_ac"][sl4])
                else:
                    r.wait("tensor", "sAC", state["prev_evac"])
                r.wait("tensor", "sDV", state["ndv_ep"])
                if u == 0:
                    r.wait("tensor", "sGP", state["ngp_bx"][bpar])
                    r.wait("tensor", "sAC", state["nac_qk"])
                r.emit("tensor", lambda e, p0=p0, ts0=ts0, t=t, kq=kq, pap=pap:
                       e.matmul(pap,
                                qtp[p0:p0 + 64,
                                    ts0 + t * 128:ts0 + (t + 1) * 128],
                                ktp[p0:p0 + 64,
                                    ts0 + kq * 512:ts0 + (kq + 1) * 512],
                                start=True, stop=True))
                npe_a = r.cnt["sPE"]
                r.wait("vector", "sPE", npe_a)
                if kq == 0:
                    r.wait("vector", "sGC", state["ngc_mv"][bpar])
                nred = r.emit("vector", lambda e, kq=kq, pap=pap, bmp=bmp:
                              e.tensor_reduce(
                                  bmp[:, kq * 16:(kq + 1) * 16],
                                  pap.rearrange("p (k e) -> p k e", e=32),
                                  axis=AX.X, op=ALU.max))
                state["phA_dv4"][sl4] = nred
                if sl4 >= 2:
                    state["pO_dv"] = nred
                if kq == 3:
                    r.emit("vector", lambda e, t=t, bmp=bmp, bpar=bpar:
                           e.tensor_tensor(bmm[:, :], bmp[:, :],
                                           bx2[bpar][:, t * 64:(t + 1) * 64],
                                           ALU.add))
                    state["ngc_mv"][bpar] = 0
                    r.emit("vector", lambda e, b=b, t=t:
                           e.tensor_reduce(
                               mv_all[:, b * 16 + t:b * 16 + t + 1],
                               bmm[:, :], axis=AX.X, op=ALU.max))
                if u == 63:
                    r.wait("vector", "sGP", state["ngp_rmb"][bpar])
                    r.wait("vector", "sGC", state["ngc_mv"][bpar])
                    r.wait("vector", "sAC", state["nac_eps"])
                    r.emit("vector", lambda e, b=b: e.tensor_scalar(
                        epst[:, :], mv_all[:, b * 16:(b + 1) * 16],
                        0.125, None, ALU.mult))
                    ndv_e = r.emit("vector", lambda e, bpar=bpar: e.tensor_tensor(
                        epst[:, :], epst[:, :], rmb2[bpar][:, :], ALU.max))
                    r.wait("scalar", "sDV", ndv_e)
                    state["nac_eps"] = r.emit("scalar", lambda e, b=b: e.activation(
                        eps_all[:, b * 16:(b + 1) * 16], epst[:, :],
                        AF.Exp, bias=clneps[:, :]))
                    state["phA_done_dv"][bpar] = ndv_e

            def emit_av(p, hip, sq, par, kg):
                b = bh_idx(p, hip, sq)
                qh, k = kg // 16, kg % 16
                kpar = kg % 2
                r.wait("tensor", "sAC", state["prev_exp2"][kpar])
                if k == 0:
                    r.wait("tensor", "sAC", state["prev_evac"])
                    r.wait("tensor", "sDV", state["pO_dv"])
                for sl in range(2):
                    r.emit("tensor", lambda e, sq=sq, k=k, sl=sl, bpar=kpar, hip=hip:
                           e.matmul(pO[0:65, sl * 512:(sl + 1) * 512],
                                    vtm[:, (sq * 16 + k) * 130 + hip * 65:
                                        (sq * 16 + k) * 130 + hip * 65 + 65],
                                    aT[bpar][:, sl * 512:(sl + 1) * 512],
                                    start=(k == 0), stop=False))
                if k == 15:
                    r.wait("tensor", "sAC", state["nac_la"][par])
                    for sl in range(2):
                        r.emit("tensor", lambda e, b=b, qh=qh, sl=sl, par=par:
                               e.matmul(pO[0:65, sl * 512:(sl + 1) * 512],
                                        vhp_all[:, b * 65:(b + 1) * 65],
                                        laT2[par][:, qh * 32 + sl * 16:
                                                  qh * 32 + (sl + 1) * 16]
                                        .rearrange("p (q o) -> p q o", o=1)
                                        .broadcast_to([64, 16, 32]),
                                        start=False, stop=True))
                    npe_o = r.cnt["sPE"]
                    r.wait("scalar", "sPE", npe_o)
                    state["prev_evac"] = r.emit("scalar", lambda e, qh=qh:
                                                e.activation(
                                                    outT[:, qh * 1024:(qh + 1) * 1024],
                                                    pO[0:65, :], AF.Copy))

            def emit_chunk_iter(p, hip, sq, par, kg, units):
                qh, k = kg // 16, kg % 16
                kpar = kg % 2
                r.wait("tensor", "sAC", state["prev_exp"][kpar])
                if kg == 0:
                    r.wait("tensor", "sGP", state["ngp_prep"][par])
                    r.wait("tensor", "sDV", state["ndv_prep"][par])
                for sl in range(2):
                    r.emit("tensor", lambda e, k=k, qh=qh, sl=sl, par=par, bpar=kpar:
                           e.matmul(pS[bpar][:, sl * 512:(sl + 1) * 512],
                                    lhs2[par][:, k * 128:(k + 1) * 128],
                                    rhs2[par][:, qh * 1024 + sl * 512:
                                              qh * 1024 + (sl + 1) * 512],
                                    start=True, stop=True))
                npe_s = r.cnt["sPE"]
                if kg > 0:
                    emit_av(p, hip, sq, par, kg - 1)
                if kg == 4 and state["pending_w1"] is not None:
                    state["pending_w1"]()
                    state["pending_w1"] = None
                for bhu, u in units:
                    emit_phaseA_half(*bhu, u)
                r.wait("scalar", "sPE", npe_s)
                newexp = r.emit("scalar", lambda e, bpar=kpar:
                                e.activation(aT[bpar][:, :], pS[bpar][:, :],
                                             AF.Exp, scale=0.125))
                state["prev_exp2"][kpar] = newexp
                state["prev_exp"][kpar] = newexp

            def emit_epilogue(p, hip, sq, par, last):
                b = bh_idx(p, hip, sq)
                stg = stage2[par]
                h = 2 * p + hip
                r.wait("tensor", "sAC", state["prev_evac"])
                if last:
                    r.wait("tensor", "sAC", max(state["prev_exp"][0],
                                                state["prev_exp"][1]))
                    slot_off = [0, 128, 256, 384, 512, 640, 768, 896]
                    for i in range(16):
                        so = slot_off[i % 8]
                        r.emit("tensor", lambda e, i=i, so=so: e.transpose(
                            pS[i // 8][:, so:so + 65],
                            outT[:, i * 128:(i + 1) * 128], id_f[0:65, 0:65]))
                    npe_tr = r.cnt["sPE"]
                    r.wait("vector", "sPE", npe_tr)
                    r.wait("vector", "sAC", state["nac_ep"])
                    for h2 in range(2):
                        r.emit("vector", lambda e, b=b, h2=h2: e.tensor_tensor(
                            rdenw16[:, h2 * 8:(h2 + 1) * 8],
                            pS[h2].ap().rearrange("p (a c) -> p a c", a=8)
                            [:, :, 64:65].rearrange("p a c -> p (a c)"),
                            eps_all[:, b * 16 + h2 * 8:b * 16 + (h2 + 1) * 8],
                            ALU.add))
                    ndv_rd = r.emit("vector", lambda e: e.reciprocal(
                        rdenw16[:, :], rdenw16[:, :]))
                    r.wait("scalar", "sDV", ndv_rd)
                    r.wait("scalar", "sSY", state["nst_a2"][par])
                    nac_ep = 0
                    for i in range(16):
                        so = slot_off[i % 8]
                        nac_ep = r.emit("scalar", lambda e, i=i, so=so, stg=stg:
                                        e.activation(
                                            stg[:, i * 64:(i + 1) * 64],
                                            pS[i // 8][:, so:so + 64], AF.Identity,
                                            scale=rdenw16[:, i:i + 1]))
                    state["ndv_ep"] = ndv_rd
                    state["nac_ep"] = nac_ep
                else:
                    r.wait("tensor", "sDV", max(state["phA_dv4"][0],
                                                state["phA_dv4"][1]))

                    def emit_wave(w, gate0=0, gate1=0, npe_tr=None):
                        acts = []
                        if npe_tr is None:
                            if gate0:
                                r.wait("tensor", "sAC", gate0)
                            for sl8 in range(8):
                                i = w * 8 + sl8
                                if gate1 and sl8 == 4:
                                    r.wait("tensor", "sAC", gate1)
                                r.emit("tensor", lambda e, i=i, sl8=sl8:
                                       e.transpose(
                                           pA[:, sl8 * 128:sl8 * 128 + 65],
                                           outT[:, i * 128:(i + 1) * 128],
                                           id_f[0:65, 0:65]))
                            npe_tr = r.cnt["sPE"]
                        r.wait("vector", "sPE", npe_tr)
                        if w == 0:
                            r.wait("vector", "sAC", state["nac_ep"])
                        r.emit("vector", lambda e, b=b, w=w: e.tensor_tensor(
                            rdenw16[:, w * 8:(w + 1) * 8],
                            pA.ap().rearrange("p (a c) -> p a c", a=8)
                            [:, :, 64:65].rearrange("p a c -> p (a c)"),
                            eps_all[:, b * 16 + w * 8:b * 16 + (w + 1) * 8],
                            ALU.add))
                        ndv_rd = r.emit("vector", lambda e, w=w: e.reciprocal(
                            rdenw16[:, w * 8:(w + 1) * 8],
                            rdenw16[:, w * 8:(w + 1) * 8]))
                        r.wait("scalar", "sDV", ndv_rd)
                        if w == 0:
                            r.wait("scalar", "sSY", state["nst_a2"][par])
                        for sl8 in range(8):
                            i = w * 8 + sl8
                            acts.append(r.emit(
                                "scalar", lambda e, i=i, sl8=sl8, stg=stg:
                                e.activation(stg[:, i * 64:(i + 1) * 64],
                                             pA[:, sl8 * 128:sl8 * 128 + 64],
                                             AF.Identity,
                                             scale=rdenw16[:, i:i + 1])))
                        return ndv_rd, acts

                    for sl8 in range(8):
                        r.emit("tensor", lambda e, sl8=sl8: e.transpose(
                            pA[:, sl8 * 128:sl8 * 128 + 65],
                            outT[:, sl8 * 128:(sl8 + 1) * 128], id_f[0:65, 0:65]))
                    npe_tr0 = r.cnt["sPE"]

                    def w1fn(b=b, par=par, stg=stg, h=h, sq=sq, npe_tr0=npe_tr0):
                        _, acts0 = emit_wave(0, npe_tr=npe_tr0)
                        ndv_rd, acts1 = emit_wave(1, gate0=acts0[3],
                                                  gate1=acts0[7])
                        state["ndv_ep"] = ndv_rd
                        state["nac_ep"] = acts1[7]
                        state["pA_free_ac"] = [acts1[3], acts1[7]]
                        finish(b, par, stg, h, sq, acts1[7])

                    state["pending_w1"] = w1fn
                    return

                finish(b, par, stg, h, sq, state["nac_ep"])

            def emit_xf_load(p, hip, sq, par):
                h = 2 * p + hip
                r.wait("sync", "sGC", state["ngp_add"][par])
                r.emit("sync", lambda e, h=h, sq=sq, par=par: e.dma_start(
                    xtile[par].ap().rearrange("p (i d) -> p i d", i=16),
                    x_in.rearrange("(s i pp) c -> s pp i c", i=16, pp=128)
                    [sq][:, :, h * 64:(h + 1) * 64]))
                r.wait("sync", "sPE", state["npe_vtr"])
                state["nld_f"][par] = r.emit(
                    "sync", lambda e, h=h, sq=sq, par=par: e.dma_start(
                        fslice[par].rearrange("p (i d) -> p i d", i=16),
                        part_sc.rearrange("(s i pp) c -> s pp i c", i=16, pp=128)
                        [sq][:, :, h * 64:(h + 1) * 64]))

            def finish(b, par, stg, h, sq, nac_ep):
                r.wait("gpsimdc", "sAC", nac_ep)
                r.wait("gpsimdc", "sSY", state["nld_f"][par])
                r.emit("gpsimdc", lambda e, par=par, stg=stg: e.tensor_tensor(
                    stg[:, :], stg[:, :], xtile[par][:, :], ALU.add))
                state["ngp_add"][par] = r.emit(
                    "gpsimdc", lambda e, par=par, stg=stg: e.tensor_tensor(
                        stg[:, :], stg[:, :], fslice[par], ALU.add))
                r.wait("sync", "sGC", state["ngp_add"][par])
                state["nst_a2"][par] = r.emit(
                    "sync", lambda e, h=h, sq=sq, stg=stg: e.dma_start(
                        out.rearrange("(s i pp) c -> s pp i c", i=16, pp=128)
                        [sq][:, :, h * 64:(h + 1) * 64],
                        stg.ap().rearrange("p (i d) -> p i d", i=16)))

            def emit_projqk_group(p, tag, n):
                pcons = state["pcons"]
                dst = qtp if tag == "q" else ktp
                slot = n % 2
                r.wait("tensor", "sAC",
                       max(pcons[slot], state["pA_free_ac"][slot]))
                r.wait("tensor", "sDV",
                       max(state["phA_dv4"][slot], state["ndv_ep"]))
                for cc in range(8):
                    r.emit("tensor", lambda e, tag=tag, n=n, cc=cc, slot=slot:
                           e.matmul(pA[:, slot * 512:(slot + 1) * 512],
                                    wsl[tag][:, cc * 128:(cc + 1) * 128],
                                    xnT[:, cc * T + n * 512:cc * T + (n + 1) * 512],
                                    start=(cc == 0), stop=(cc == 7)))
                npe_p = r.cnt["sPE"]
                r.wait("scalar", "sPE", npe_p)
                pcons[slot] = r.emit("scalar", lambda e, tag=tag, n=n, slot=slot,
                                     dst=dst, p=p: e.activation(
                                         dst[:, n * 512:(n + 1) * 512],
                                         pA[:, slot * 512:(slot + 1) * 512],
                                         AF.Identity, bias=bcol[tag][:, p:p + 1]))

            state["nls_next"] = 0
            for p in range(8):
                if p == 0:
                    for tag in ("q", "k", "v"):
                        r.emit("sync", lambda e, tag=tag: e.dma_start(
                            wsl[tag].rearrange("p (cc j) -> p cc j", cc=8),
                            wT_sc[tag][0].rearrange("(cc p) j -> p cc j", p=128)))
                    state["nls_next"] = r.cnt["sSY"]
                nls = state["nls_next"]
                pcons = state["pcons"]
                if p == 0:
                    r.wait("scalar", "sPE", r.cnt["sPE"])
                    r.wait("scalar", "sGP", r.cnt["sGP"])
                    r.wait("tensor", "sSY", nls)
                    for tag in ("q", "k"):
                        for n in range(8):
                            emit_projqk_group(0, tag, n)
                    state["nac_qk"] = r.cnt["sAC"]
                bhs = [(p, hip, sq) for hip in range(2) for sq in range(B_CORE)]
                r.wait("gpsimd", "sAC", r.cnt["sAC"])
                emit_prep(*bhs[0], 0, 0, with_bx=True)
                emit_prep_bx(*bhs[1])
                vsl = [(pS[0], 0), (pS[0], 512), (pS[1], 0), (pS[1], 512)]
                vcons = [0] * 4
                pre_units = [(bhs[0], u) for u in range(64)] + \
                            [(bhs[1], u) for u in range(16)]
                upos = [0]

                def pre_phA(slots=(0, 1, 2, 3)):
                    for _ in range(2):
                        if upos[0] < 80:
                            emit_phaseA_half(*pre_units[upos[0]][0],
                                             pre_units[upos[0]][1], slots=slots)
                            upos[0] += 1
                r.wait("tensor", "sDV", state["ndv_ep"])
                r.wait("tensor", "sAC", state["nac_ep"])
                r.wait("scalar", "sGC", max(state["ngp_add"]))
                r.wait("sync", "sGC", max(state["ngp_add"]))
                for half in range(2):
                    vev = [0] * 4
                    for n4 in range(4):
                        n = half * 4 + n4
                        slot = n % 2
                        r.wait("tensor", "sAC",
                               max(pcons[slot], state["pA_free_ac"][slot]))
                        r.wait("tensor", "sDV",
                               max(state["phA_dv4"][slot], state["ndv_ep"]))
                        for cc in range(8):
                            r.emit("tensor", lambda e, n=n, cc=cc, slot=slot:
                                   e.matmul(pA[:, slot * 512:(slot + 1) * 512],
                                            wsl["v"][:, cc * 128:(cc + 1) * 128],
                                            xnT[:, cc * T + n * 512:
                                                cc * T + (n + 1) * 512],
                                            start=(cc == 0), stop=(cc == 7)))
                        npe_p = r.cnt["sPE"]
                        r.wait("scalar", "sPE", npe_p)
                        pcons[slot] = r.emit("scalar", lambda e, n4=n4, slot=slot, p=p:
                                             e.activation(
                                                 vTf[:, n4 * 512:(n4 + 1) * 512],
                                                 pA[:, slot * 512:(slot + 1) * 512],
                                                 AF.Identity,
                                                 bias=bcol["v"][:, p:p + 1]))
                        vev[n4] = pcons[slot]
                        pre_phA(slots=(2, 3))
                    if half == 1 and p < 7:
                        r.wait("sync", "sPE", r.cnt["sPE"])
                        for tag in ("q", "k", "v"):
                            r.emit("sync", lambda e, tag=tag, p=p: e.dma_start(
                                wsl[tag].rearrange("p (cc j) -> p cc j", cc=8),
                                wT_sc[tag][p + 1].rearrange("(cc p) j -> p cc j",
                                                            p=128)))
                        state["nls_next"] = r.cnt["sSY"]
                    for tt4 in range(16):
                        tt = half * 16 + tt4
                        slot = tt % 4
                        vps, voff = vsl[slot]
                        r.wait("tensor", "sAC", max(vcons[slot], vev[tt4 // 4]))
                        npe_v = r.emit("tensor", lambda e, tt4=tt4, vps=vps, voff=voff:
                                       e.matmul(vps[:, voff:voff + 128],
                                                vTf[:, tt4 * 128:(tt4 + 1) * 128],
                                                id_bf[:, :],
                                                start=True, stop=True))
                        pre_phA()
                        r.wait("scalar", "sPE", npe_v)
                        r.emit("scalar", lambda e, tt=tt, vps=vps, voff=voff:
                               e.activation(vtm[:, tt * 130:tt * 130 + 64],
                                            vps[:, voff:voff + 64], AF.Copy))
                        vcons[slot] = r.emit(
                            "scalar", lambda e, tt=tt, vps=vps, voff=voff:
                            e.activation(vtm[:, tt * 130 + 65:tt * 130 + 129],
                                         vps[:, voff + 64:voff + 128], AF.Copy))
                state["npe_vtr"] = r.cnt["sPE"]
                r.wait("tensor", "sAC", r.cnt["sAC"])
                r.wait("gpsimd", "sAC", r.cnt["sAC"])

                for i in range(4):
                    par = i % 2
                    emit_xf_load(*bhs[i], par)
                    if i < 3:
                        emit_prep(*bhs[i + 1], (i + 1) % 2, state["chunk_pe_end"])
                    for kg in range(32):
                        units = []
                        if i == 0:
                            if kg == 16:
                                emit_prep_bx(*bhs[2])
                            if kg < 24:
                                u0 = 16 + 2 * kg
                                units = [(bhs[1], u0), (bhs[1], u0 + 1)]
                            else:
                                u0 = 2 * (kg - 24)
                                units = [(bhs[2], u0), (bhs[2], u0 + 1)]
                        elif i == 1:
                            if kg == 0:
                                emit_prep_bx(*bhs[3])
                            if kg >= 8:
                                u0 = 16 + 2 * (kg - 8)
                                units = [(bhs[2], u0), (bhs[2], u0 + 1)]
                        elif i == 2:
                            if kg >= 8:
                                u0 = 2 * (kg - 8)
                                units = [(bhs[3], u0), (bhs[3], u0 + 1)]
                        else:
                            if 8 <= kg < 16:
                                u0 = 48 + 2 * (kg - 8)
                                units = [(bhs[3], u0), (bhs[3], u0 + 1)]
                            elif kg >= 16 and p < 7:
                                if kg == 16:
                                    r.wait("tensor", "sSY", state["nls_next"])
                                    r.wait("scalar", "sGP", r.cnt["sGP"])
                                tag = "q" if kg < 24 else "k"
                                emit_projqk_group(p + 1, tag, kg % 8)
                                if kg == 31:
                                    state["nac_qk"] = r.cnt["sAC"]
                        emit_chunk_iter(*bhs[i], par, kg, units)
                    emit_av(*bhs[i], par, 31)
                    state["chunk_pe_end"] = r.cnt["sPE"]
                    emit_epilogue(*bhs[i], par, i == 3)
                if debug and p == 0:
                    r.barrier()
                    r.emit("gpsimd", lambda e: e.dma_start(dbg["d_mv"], mv_all[:, 0:32]))
                    r.emit("gpsimd", lambda e: e.dma_start(dbg["d_eps"], eps_all[:, 0:32]))
                    r.emit("gpsimd", lambda e: e.dma_start(dbg["d_qt"], qtp[:, 0:512]))
                    r.emit("gpsimd", lambda e: e.dma_start(dbg["d_ot"], outT[:, 0:1024]))
                    r.emit("gpsimd", lambda e: e.dma_start(dbg["d_stage"], stage[:, :]))
                    r.emit("gpsimd", lambda e: e.dma_start(dbg["d_la"], laT[:, :]))
                r.barrier(("sync", "tensor", "vector"))
            r.barrier()

        r.mark("P2")
        r.wait("gpsimd", "sGP", r.cnt["sGP"])
        r.wait("gpsimd", "sSY", r.cnt["sSY"])

        r.mark("ALL")
        lim = r.seq + 1
        if trunc is not None:
            lim = r.marks.get(trunc, trunc if isinstance(trunc, int) else r.seq + 1)
        with nc.Block() as block:
            @block.sync
            def _(e):
                for sq, f in r.ops["sync"]:
                    if sq <= lim:
                        f(e)

            @block.tensor
            def _(e):
                for sq, f in r.ops["tensor"]:
                    if sq <= lim:
                        f(e)

            @block.vector
            def _(e):
                for sq, f in r.ops["vector"]:
                    if sq <= lim:
                        f(e)

            @block.scalar
            def _(e):
                for sq, f in r.ops["scalar"]:
                    if sq <= lim:
                        f(e)

            @block.gpsimd
            def _(e):
                for sq, f in r.ops["gpsimd"]:
                    if sq <= lim:
                        f(e)

    return nc


def kernel(**inputs):
    X = np.ascontiguousarray(np.asarray(inputs["L0_attn_X"], np.float32))
    cst = _host_consts()
    base = {
        "w_q": np.ascontiguousarray(inputs["L0_attn_Wq"], dtype=np.float32),
        "w_k": np.ascontiguousarray(inputs["L0_attn_Wk"], dtype=np.float32),
        "w_v": np.ascontiguousarray(inputs["L0_attn_Wv"], dtype=np.float32),
        "w_r": np.ascontiguousarray(inputs["L0_ffn_W_r"], dtype=np.float32),
        "w_k2": np.ascontiguousarray(inputs["L0_ffn_W_k"], dtype=np.float32),
        "w_v2": np.ascontiguousarray(inputs["L0_ffn_W_v"], dtype=np.float32),
        "b_q": np.ascontiguousarray(inputs["L0_attn_bq"], dtype=np.float32).reshape(1, C),
        "b_k": np.ascontiguousarray(inputs["L0_attn_bk"], dtype=np.float32).reshape(1, C),
        "b_v": np.ascontiguousarray(inputs["L0_attn_bv"], dtype=np.float32).reshape(1, C),
        "b_mur": np.ascontiguousarray(inputs["L0_ffn_mu_r"], dtype=np.float32).reshape(1, C),
        "b_muk": np.ascontiguousarray(inputs["L0_ffn_mu_k"], dtype=np.float32).reshape(1, C),
    }
    base.update(cst)
    in_maps = []
    for core in range(8):
        m = dict(base)
        m["x"] = np.ascontiguousarray(
            X[2 * core:2 * core + 2].reshape(T, C))
        in_maps.append(m)
    nc = build_program(debug=False)
    res = run_bass_kernel_spmd(nc, in_maps, core_ids=list(range(8)))
    outs = [np.asarray(r["out"], np.float32).reshape(2, S, C) for r in res.results]
    return np.concatenate(outs, axis=0)


# version 24 (final)
# speedup vs baseline: 1.0129x; 1.0129x over previous
import contextlib
import numpy as np
import ml_dtypes

import concourse.bass as bass
import concourse.mybir as mybir
from concourse.bass_utils import run_bass_kernel_spmd

bf16 = ml_dtypes.bfloat16
F32 = mybir.dt.float32
BF = mybir.dt.bfloat16
AF = mybir.ActivationFunctionType
ALU = mybir.AluOpType
AX = mybir.AxisListType

B_CORE = 2
S = 2048
T = B_CORE * S
C = 1024
H = 16
HD = 64
BLK = 32
NBH = 32
NUM_BLOCK = 1024
BIGNEG = -239616.0
LN32 = float(np.log(32.0))
LNEPS = float(np.log(1e-6))
N_BISECT = 58
QT = 512

ENGS = ("tensor", "vector", "scalar", "sync", "gpsimd")
ESEM = {"tensor": "sPE", "vector": "sDV", "scalar": "sAC",
        "sync": "sSY", "gpsimd": "sGP", "gpsimdc": "sGC"}


class Rec:
    def __init__(self, nc, stack):
        self.nc = nc
        self.ops = {e: [] for e in ENGS}
        self.sem = {}
        self.cnt = {}
        self.seq = 0
        self.marks = {}
        for n in ("sPE", "sDV", "sAC", "sSY", "sGP", "sGC"):
            self.sem[n] = stack.enter_context(nc.semaphore(n))
            self.cnt[n] = 0

    def _push(self, eng, fn):
        self.seq += 1
        if eng == "gpsimdc":
            eng = "gpsimd"
        self.ops[eng].append((self.seq, fn))

    def mark(self, name):
        self.marks[name] = self.seq

    def emit(self, eng, fn):
        sname = ESEM[eng]
        n = 16 if eng in ("sync", "gpsimd") else 1
        if eng != "tensor" and self.cnt[sname]:
            self.wait(eng, sname, self.cnt[sname])
        self.cnt[sname] += n
        h = self.sem[sname]
        self._push(eng, lambda e, fn=fn, h=h, n=n: fn(e).then_inc(h, n))
        return self.cnt[sname]

    def wait(self, eng, sname, val):
        if not val:
            return
        h = self.sem[sname]
        self._push(eng, lambda e, h=h, v=val: e.wait_ge(h, v))

    def barrier(self, engs=ENGS):
        snap = dict(self.cnt)
        for eng in engs:
            for sname, v in snap.items():
                if sname != ESEM[eng] and v:
                    self.wait(eng, sname, v)


def _host_consts():
    cst = {}
    cst["c_id_bf"] = np.eye(128, dtype=np.float32).astype(bf16)
    cst["c_id_f32"] = np.eye(128, dtype=np.float32)
    ii = np.arange(64)
    cst["c_band"] = (np.abs(ii[:, None] - ii[None, :]) <= 1).astype(np.float32) * 5000.0
    e = np.zeros((64, 2048), np.float32)
    for cc in range(16):
        for j in range(128):
            e[4 * cc + j // 32, 128 * cc + j] = 1.0
    cst["c_eall"] = e.astype(bf16)
    cst["c_ones_row"] = np.ones((1, 128), np.float32).astype(bf16)
    return cst


def build_program(debug=False, trunc=None):
    nc = bass.Bass("TRN2", target_bir_lowering=False, debug=False)

    x_in = nc.dram_tensor("x", [T, C], F32, kind="ExternalInput").ap()
    w_in = {w: nc.dram_tensor(f"w_{w}", [C, C], F32, kind="ExternalInput").ap()
            for w in ("q", "k", "v", "r", "k2", "v2")}
    b_in = {w: nc.dram_tensor(f"b_{w}", [1, C], F32, kind="ExternalInput").ap()
            for w in ("q", "k", "v", "mur", "muk")}
    c_id_bf = nc.dram_tensor("c_id_bf", [128, 128], BF, kind="ExternalInput").ap()
    c_id_f32 = nc.dram_tensor("c_id_f32", [128, 128], F32, kind="ExternalInput").ap()
    c_band = nc.dram_tensor("c_band", [64, 64], F32, kind="ExternalInput").ap()
    c_eall = nc.dram_tensor("c_eall", [64, 2048], BF, kind="ExternalInput").ap()
    c_ones_row = nc.dram_tensor("c_ones_row", [1, 128], BF, kind="ExternalInput").ap()

    out = nc.dram_tensor("out", [T, C], F32, kind="ExternalOutput").ap()

    wT_sc = {w: nc.dram_tensor(f"wT_{w}", [8, C, 128], BF).ap()
             for w in ("q", "k", "v", "r", "k2", "v2")}
    part_sc = nc.dram_tensor("part_sc", [T, C], BF).ap()
    bsm_dram = nc.dram_tensor("bsm_dram", [NBH, 4, 16, 64], F32).ap()
    rm_dram = nc.dram_tensor("rm_dram", [NBH, 4, 16], F32).ap()
    bsm_raw = nc.dram_tensor("bsm_raw", [64, NBH, 64], F32).ap()
    selT_dram = nc.dram_tensor("selT_dram", [64, NBH * 64], F32).ap()
    rm_raw = nc.dram_tensor("rm_raw", [64, NBH], F32).ap()
    attn_sc = nc.dram_tensor("attn_sc", [B_CORE, S, C], F32).ap()

    dbg = {}
    if debug:
        for name, shape in [("d_xnT", [128, 512]), ("d_qhT", [128, 1024]),
                            ("d_norm", [64, 128]), ("d_thr", [128, 32]),
                            ("d_mv", [128, 32]), ("d_qt", [128, 512]),
                            ("d_at", [128, 1024]), ("d_ot", [65, 1024]),
                            ("d_eps", [128, 32]), ("d_la", [64, 64]),
                            ("d_stage", [128, 1024])]:
            dbg[name] = nc.dram_tensor(name, shape, F32, kind="ExternalOutput").ap()

    stack = contextlib.ExitStack()
    with stack:
        r = Rec(nc, stack)

        def sb(name, shape, dt):
            return stack.enter_context(nc.sbuf_tensor(name, shape, dt))

        xnT = sb("xnT", [128, 8 * T], BF)
        id_bf = sb("id_bf", [128, 128], BF)
        id_f = sb("id_f", [128, 128], F32)
        band = sb("band", [64, 64], F32)
        eall = sb("eall", [64, 2048], BF)
        ones_row = sb("ones_row", [1, 128], BF)
        ones128 = sb("ones128", [128, 128], F32)
        bcol = {w: sb(f"bcol_{w}", [128, 8], F32) for w in ("q", "k", "v")}
        mucol = {w: sb(f"mucol_{w}", [128, 8], F32) for w in ("r", "k")}
        mucol1m = {w: sb(f"mucol1m_{w}", [128, 8], F32) for w in ("r", "k")}
        bvrow = sb("bvrow", [1, C], BF)
        norm_all = sb("norm_all", [128, NBH * 64], F32)
        lowT_all = sb("lowT_all", [64, NBH * 64], F32)
        rm_all = sb("rm_all", [64, NBH], F32)
        selT_all = sb("selT_all", [64, NBH * 64], F32)
        bsm_all = sb("bsm_all", [64, NBH * 64], F32)
        vhp_all = sb("vhp_all", [64, NBH * 65], BF)
        thr = sb("thr", [128, 32], F32)
        mv_all = sb("mv_all", [128, NBH * 16], F32)
        eps_all = sb("eps_all", [128, NBH * 16], F32)
        cln32 = sb("cln32", [128, 1], F32)
        clneps = sb("clneps", [128, 1], F32)

        pA = stack.enter_context(nc.psum_tensor("pA", [128, 1024], F32))

        xtile = [sb("xtile0", [128, C], F32), sb("xtile1", [128, C], F32)]
        xnrow = sb("xnrow", [128, C], BF)
        sq_dump = sb("sq_dump", [128, C], F32)
        col_a = sb("col_a", [128, 2], F32)
        col_b = sb("col_b", [128, 2], F32)
        col_c = sb("col_c", [128, 2], F32)
        xnrow2 = sb("xnrow2", [128, C], BF)
        xnh = sb("xnh", [128, 1024], BF)
        small_f = sb("small_f", [64, 64], F32)
        sel_buf = sb("sel_buf", [64, 64], F32)

        for dst, src in [(id_bf[:, :], c_id_bf), (id_f[:, :], c_id_f32),
                         (band[:, :], c_band), (eall[:, :], c_eall),
                         (ones_row[:, :], c_ones_row)]:
            r.emit("sync", lambda e, d=dst, s=src: e.dma_start(d, s))
        ngp_bv = r.emit("gpsimd", lambda e: e.dma_start(bvrow[:, :], b_in["v"]))
        for i, w in enumerate(["q", "k", "v", "mur", "muk"]):
            r.emit("sync", lambda e, i=i, s=b_in[w]: e.dma_start(
                brows[:, i * 128:(i + 1) * 128],
                s.rearrange("o (p j) -> (o p) j", p=8)))
        n_sy_const = r.cnt["sSY"]

        r.emit("vector", lambda e: e.memset(ones128[:, :], 1.0))
        r.emit("vector", lambda e: e.memset(cln32[:, :], LN32))
        r.emit("vector", lambda e: e.memset(clneps[:, :], LNEPS))

        r.wait("tensor", "sSY", n_sy_const)
        for i in range(5):
            r.emit("tensor", lambda e, i=i: e.transpose(
                pA[:, i * 8:(i + 1) * 8], brows[:, i * 128:(i + 1) * 128],
                id_f[0:8, 0:8]))
        npe = r.cnt["sPE"]
        r.wait("vector", "sPE", npe)
        for i, dst in enumerate([bcol["q"], bcol["k"], bcol["v"],
                                 mucol["r"], mucol["k"]]):
            r.emit("vector", lambda e, i=i, d=dst: e.tensor_copy(
                d[:, :], pA[:, i * 8:(i + 1) * 8]))
        for w in ("r", "k"):
            r.emit("vector", lambda e, w=w: e.tensor_scalar(
                mucol1m[w][:, :], mucol[w][:, :], -1.0, 1.0, ALU.mult, ALU.add))
        r.barrier()

        with contextlib.ExitStack() as st_p0:
            pT = st_p0.enter_context(nc.psum_tensor("pT", [128, 1024], BF))
            pTb = st_p0.enter_context(nc.psum_tensor("pTb", [128, 1024], BF))
            pT2 = [pT, pTb]
            brows = st_p0.enter_context(nc.sbuf_tensor("brows", [8, 5 * 128], F32))
            qhT = st_p0.enter_context(nc.sbuf_tensor("qhT", [128, 1024], F32))
            khT = st_p0.enter_context(nc.sbuf_tensor("khT", [128, 1024], F32))
            vhT = st_p0.enter_context(nc.sbuf_tensor("vhT", [128, 1024], F32))
            prev_consume = [0, 0]
            prev_pt_dv = [0, 0]
            xnrows = None
            for t in range(32):
                par = t % 2
                ca = col_a[:, par:par + 1]
                cb = col_b[:, par:par + 1]
                cc_ = col_c[:, par:par + 1]
                xr_ = [xnrow, xnrow2][par]
                sq_ = sq_dump[:, par * 512:par * 512 + 512]
                r.wait("sync", "sAC", prev_consume[par])
                nld = r.emit("sync", lambda e, t=t, p=par: e.dma_start(
                    xtile[p][:, :], x_in[t * 128:(t + 1) * 128, :]))
                r.wait("scalar", "sSY", nld)
                r.emit("scalar", lambda e, p=par, ca=ca: e.activation(
                    sq_dump[:, :], xtile[p][:, :], AF.Square, accum_out=ca))
                nac = r.cnt["sAC"]
                r.wait("vector", "sAC", nac)
                r.emit("vector", lambda e, p=par, cb=cb: e.tensor_reduce(
                    cb, xtile[p][:, :], axis=AX.X, op=ALU.add))
                r.emit("vector", lambda e, cb=cb: e.tensor_scalar(
                    cb, cb, 1.0 / C, None, ALU.mult))
                r.emit("vector", lambda e, ca=ca: e.tensor_scalar(
                    ca, ca, 1.0 / C, None, ALU.mult))
                r.emit("vector", lambda e, cb=cb, cc_=cc_: e.tensor_tensor(
                    cc_, cb, cb, ALU.mult))
                r.emit("vector", lambda e, ca=ca, cc_=cc_: e.tensor_tensor(
                    ca, ca, cc_, ALU.subtract))
                ndv = r.emit("vector", lambda e, ca=ca: e.tensor_scalar(
                    ca, ca, 1e-5, None, ALU.add))
                r.wait("scalar", "sDV", ndv)
                nac2 = r.emit("scalar", lambda e, ca=ca: e.activation(
                    ca, ca, AF.Sqrt))
                r.wait("vector", "sAC", nac2)
                r.emit("vector", lambda e, ca=ca, cc_=cc_: e.reciprocal(cc_, ca))
                r.emit("vector", lambda e, cb=cb, cc_=cc_: e.tensor_tensor(
                    cb, cb, cc_, ALU.mult))
                ndv2 = r.emit("vector", lambda e, cb=cb: e.tensor_scalar(
                    cb, cb, -1.0, None, ALU.mult))
                r.wait("scalar", "sDV", ndv2)
                r.wait("scalar", "sPE", prev_pt_dv[par] and 0 or 0)
                nac3 = r.emit("scalar", lambda e, p=par, xr_=xr_, cb=cb, cc_=cc_:
                              e.activation(xr_[:, :], xtile[p][:, :], AF.Identity,
                                           bias=cb, scale=cc_))
                prev_consume[par] = nac3
                r.wait("tensor", "sAC", nac3)
                r.wait("tensor", "sDV", prev_pt_dv[par])
                for c in range(8):
                    r.emit("tensor", lambda e, c=c, xr_=xr_, p=par: e.transpose(
                        pT2[p][:, c * 128:(c + 1) * 128],
                        xr_[:, c * 128:(c + 1) * 128], id_bf[:, :]))
                npe_t = r.cnt["sPE"]
                r.wait("vector", "sPE", npe_t)
                prev_pt_dv[par] = r.emit("vector", lambda e, t=t, p=par: e.tensor_copy(
                    xnT.rearrange("p (c t) -> p c t", c=8)[:, :, t * 128:(t + 1) * 128],
                    pT2[p].ap().rearrange("p (c x) -> p c x", c=8)))
                r.wait("scalar", "sPE", npe_t)
            r.barrier()

            r.mark("P0b")
            for c in range(8):
                r.emit("vector", lambda e, c=c: e.tensor_reduce(
                    sq_dump[:, c * 128:(c + 1) * 128],
                    xnT[:, c * T:(c + 1) * T].rearrange("p (n b) -> p n b", b=BLK),
                    axis=AX.X, op=ALU.add))
            ndvh = r.emit("vector", lambda e: e.tensor_scalar(
                xnh[:, :], sq_dump[:, :], 1.0 / BLK, None, ALU.mult))

            wbuf2 = [st_p0.enter_context(nc.sbuf_tensor(f"wbuf{j}", [128, C], BF))
                     for j in range(2)]
            wTsb2 = st_p0.enter_context(nc.sbuf_tensor("wTsb2", [128, C], BF))
            wp = {"nld": {}, "npe": {}, "nst": 0, "wc": [0, 0], "g": 0}

            def wpd_load(tag, d):
                g = wp["g"]
                wp["g"] = g + 1
                r.wait("gpsimd", "sPE", wp["npe"].get(g - 2, 0))
                wp["nld"][(tag, d)] = r.emit(
                    "gpsimd", lambda e, tag=tag, d=d, g=g: e.dma_start(
                        wbuf2[g % 2][:, :], w_in[tag][d * 128:(d + 1) * 128, :]))
                wp["nld"][(tag, d)] = (wp["nld"][(tag, d)], g)

            def wpd_proc(tag, d):
                nld, g = wp["nld"][(tag, d)]
                r.wait("tensor", "sGP", nld)
                r.wait("tensor", "sAC", r.cnt["sAC"])
                r.wait("tensor", "sDV", r.cnt["sDV"])
                wpt = pT2[g % 2]
                for cc in range(8):
                    r.emit("tensor", lambda e, cc=cc, wpt=wpt, g=g: e.transpose(
                        wpt[:, cc * 128:(cc + 1) * 128],
                        wbuf2[g % 2][:, cc * 128:(cc + 1) * 128],
                        id_bf[:, :]))
                npe_w = r.cnt["sPE"]
                wp["npe"][g] = npe_w
                r.wait("vector", "sPE", npe_w)
                r.wait("vector", "sSY", wp["nst"])
                nwc = r.emit("vector", lambda e, wpt=wpt: e.tensor_copy(
                    wTsb2[:, :], wpt[:, :]))
                r.wait("sync", "sDV", nwc)
                wp["nst"] = r.emit("sync", lambda e, tag=tag, d=d: e.dma_start(
                    wT_sc[tag][d].rearrange("(cc p) j -> p cc j", p=128),
                    wTsb2.rearrange("p (cc x) -> p cc x", cc=8)))
                if tag in ("q", "k", "v"):
                    dstT = {"q": qhT, "k": khT, "v": vhT}[tag]
                    r.wait("tensor", "sDV", max(ndvh, nwc))
                    for cc in range(8):
                        r.emit("tensor", lambda e, cc=cc: e.matmul(
                            pA[:, 0:128],
                            wTsb2[:, cc * 128:(cc + 1) * 128],
                            xnh[:, cc * 128:(cc + 1) * 128],
                            start=(cc == 0), stop=(cc == 7)))
                    npe_m = r.cnt["sPE"]
                    r.wait("vector", "sPE", npe_m)
                    r.emit("vector", lambda e, d=d, dstT=dstT, tag=tag:
                           e.tensor_scalar(
                               dstT[:, d * 128:(d + 1) * 128], pA[:, 0:128],
                               bcol[tag][:, d:d + 1], None, ALU.add))
                r.wait("tensor", "sDV", r.cnt["sDV"])

            for tag in ("r", "k2", "v2"):
                wpd_load(tag, 0)
                wpd_load(tag, 1)
                for d in range(8):
                    wpd_proc(tag, d)
                    if d + 2 < 8:
                        wpd_load(tag, d + 2)
            r.mark("P0c")
            nb128 = st_p0.enter_context(nc.sbuf_tensor("nb128", [128, 1024], F32))
            bis_lo = st_p0.enter_context(nc.sbuf_tensor("bis_lo", [128, 32], F32))
            bis_hi = st_p0.enter_context(nc.sbuf_tensor("bis_hi", [128, 32], F32))
            bis_mid = st_p0.enter_context(nc.sbuf_tensor("bis_mid", [128, 32], F32))
            bis_ge = st_p0.enter_context(nc.sbuf_tensor("bis_ge", [128, 1024], BF))
            bis_cnt = st_p0.enter_context(nc.sbuf_tensor("bis_cnt", [128, 32], F32))
            bis_t1 = st_p0.enter_context(nc.sbuf_tensor("bis_t1", [128, 32], F32))
            bis_t2 = st_p0.enter_context(nc.sbuf_tensor("bis_t2", [128, 32], F32))

            def bh_idx(p, hip, s):
                return (2 * p + hip) * 2 + s

            def emit_low_bh(p, hip, sx):
                b = bh_idx(p, hip, sx)
                p0 = 64 * hip
                cl = p * 128 + sx * 64
                r.wait("tensor", "sAC", r.cnt["sAC"])
                r.wait("tensor", "sDV", r.cnt["sDV"])
                r.emit("tensor", lambda e, p0=p0, cl=cl: e.matmul(
                    pA[0:64, 0:64], qhT[p0:p0 + 64, cl:cl + 64],
                    khT[p0:p0 + 64, cl:cl + 64], start=True, stop=True))
                npe_l = r.emit("tensor", lambda e, p0=p0, cl=cl: e.matmul(
                    pA[0:64, 64:128], khT[p0:p0 + 64, cl:cl + 64],
                    qhT[p0:p0 + 64, cl:cl + 64], start=True, stop=True))
                r.wait("vector", "sPE", npe_l)
                r.emit("vector", lambda e: e.tensor_reduce(
                    col_a[0:64, 0:1], pA[0:64, 0:64], axis=AX.X, op=ALU.max))
                r.emit("vector", lambda e: e.tensor_scalar(
                    small_f[:, :], pA[0:64, 0:64], col_a[0:64, 0:1], 0.125,
                    ALU.subtract, ALU.mult))
                r.emit("vector", lambda e, b=b: e.tensor_tensor(
                    norm_all[0:64, b * 64:(b + 1) * 64], small_f[:, :],
                    band[:, :], ALU.add))
                r.emit("vector", lambda e, b=b: e.tensor_copy(
                    lowT_all[:, b * 64:(b + 1) * 64], pA[0:64, 64:128]))
                ndv_l = r.emit("vector", lambda e, b=b: e.tensor_scalar(
                    rm_all[:, b:b + 1], col_a[0:64, 0:1], 0.125, None, ALU.mult))
                r.wait("tensor", "sDV", ndv_l)

            def emit_vhp_bh(p, hip, sx):
                b = bh_idx(p, hip, sx)
                p0 = 64 * hip
                cl = p * 128 + sx * 64
                r.wait("tensor", "sAC", r.cnt["sAC"])
                r.wait("tensor", "sDV", r.cnt["sDV"])
                idsl = id_f[0:64, 0:64] if hip == 0 else id_f[64:128, 64:128]
                npe_v = r.emit("tensor", lambda e, p0=p0, cl=cl, idsl=idsl:
                               e.transpose(pA[0:64, 0:64],
                                           vhT[p0:p0 + 64, cl:cl + 64], idsl))
                r.wait("vector", "sPE", npe_v)
                r.emit("vector", lambda e, b=b: e.tensor_copy(
                    vhp_all[:, b * 65:b * 65 + 64], pA[0:64, 0:64]))
                ndv_v = r.emit("vector", lambda e, b=b: e.memset(
                    vhp_all[:, b * 65 + 64:b * 65 + 65], 1.0))
                r.wait("tensor", "sDV", ndv_v)

            def emit_nb128():
                ndv_n = r.cnt["sDV"]
                r.wait("gpsimd", "sDV", ndv_n)
                for b in range(NBH):
                    r.emit("gpsimd", lambda e, b=b: e.dma_start(
                        nb128[0:64, b * 32:(b + 1) * 32],
                        norm_all[0:64, b * 64:b * 64 + 32]))
                    r.emit("gpsimd", lambda e, b=b: e.dma_start(
                        nb128[64:128, b * 32:(b + 1) * 32],
                        norm_all[0:64, b * 64 + 32:b * 64 + 64]))
                ngp_n = r.cnt["sGP"]
                r.emit("vector", lambda e: e.memset(bis_lo[:, :], -100000.0))
                r.emit("vector", lambda e: e.memset(bis_hi[:, :], 100000.0))
                r.wait("vector", "sGP", ngp_n)

            pB1 = st_p0.enter_context(nc.psum_tensor("pB1", [128, 32], F32))
            pB1s = st_p0.enter_context(nc.psum_tensor("pB1s", [64, 64], F32))

            def emit_bisect_iter(it):
                r.emit("vector", lambda e: e.tensor_tensor(
                    bis_mid[:, :], bis_lo[:, :], bis_hi[:, :], ALU.add))
                r.emit("vector", lambda e: e.tensor_scalar(
                    bis_mid[:, :], bis_mid[:, :], 0.5, None, ALU.mult))
                r.emit("vector", lambda e: e.tensor_tensor(
                    bis_ge[:, :], nb128[:, :],
                    bis_mid.ap().rearrange("p (h o) -> p h o", o=1)
                    .broadcast_to([128, 32, 32]), ALU.is_ge))
                ndv_c = r.emit("vector", lambda e: e.tensor_reduce(
                    bis_cnt[:, :], bis_ge.ap().rearrange("p (h e) -> p h e", e=32),
                    axis=AX.X, op=ALU.add))
                r.wait("tensor", "sDV", ndv_c)
                npe_c = r.emit("tensor", lambda e: e.matmul(
                    pB1[:, :], ones128[:, :], bis_cnt[:, :], start=True, stop=True))
                r.wait("vector", "sPE", npe_c)
                r.emit("vector", lambda e: e.tensor_scalar(
                    bis_t1[:, :], pB1[:, :], float(NUM_BLOCK), None, ALU.is_ge))
                r.emit("vector", lambda e: e.tensor_scalar(
                    bis_t2[:, :], pB1[:, :], float(NUM_BLOCK), None, ALU.is_lt))
                r.emit("vector", lambda e: e.tensor_tensor(
                    bis_cnt[:, :], bis_t1[:, :], bis_mid[:, :], ALU.mult))
                r.emit("vector", lambda e: e.tensor_tensor(
                    bis_lo[:, :], bis_t2[:, :], bis_lo[:, :], ALU.mult))
                r.emit("vector", lambda e: e.tensor_tensor(
                    bis_lo[:, :], bis_lo[:, :], bis_cnt[:, :], ALU.add))
                r.emit("vector", lambda e: e.tensor_tensor(
                    bis_cnt[:, :], bis_t2[:, :], bis_mid[:, :], ALU.mult))
                r.emit("vector", lambda e: e.tensor_tensor(
                    bis_hi[:, :], bis_t1[:, :], bis_hi[:, :], ALU.mult))
                ndv_u = r.emit("vector", lambda e: e.tensor_tensor(
                    bis_hi[:, :], bis_hi[:, :], bis_cnt[:, :], ALU.add))
                r.wait("tensor", "sDV", ndv_u)

            def emit_sel_thr():
                r.emit("vector", lambda e: e.tensor_copy(thr[:, :], bis_lo[:, :]))

            def emit_sel_b(b):
                r.emit("vector", lambda e, b=b: e.tensor_scalar(
                    sel_buf[:, :], norm_all[0:64, b * 64:(b + 1) * 64],
                    thr[0:64, b:b + 1], None, ALU.is_ge))
                ndv_s = r.emit("vector", lambda e, b=b: e.tensor_scalar(
                    bsm_all[:, b * 64:(b + 1) * 64], sel_buf[:, :], 1.0, -BIGNEG,
                    ALU.subtract, ALU.mult))
                r.wait("tensor", "sDV", ndv_s)
                r.wait("tensor", "sAC", r.cnt["sAC"])
                npe_s = r.emit("tensor", lambda e: e.transpose(
                    pB1s[0:64, 0:64], sel_buf[:, :], id_f[0:64, 0:64]))
                r.wait("vector", "sPE", npe_s)
                ndv_s2 = r.emit("vector", lambda e, b=b: e.tensor_copy(
                    selT_all[:, b * 64:(b + 1) * 64], pB1s[0:64, 0:64]))
                r.wait("tensor", "sDV", ndv_s2)

            def emit_sel_post():
                r.wait("sync", "sDV", r.cnt["sDV"])
                nT1 = r.emit("sync", lambda e: e.dma_start(selT_dram, selT_all[:, :]))
                r.wait("sync", "sSY", nT1)
                r.emit("sync", lambda e: e.dma_start(norm_all[64:128, :], selT_dram))
                nraw = r.emit("sync", lambda e: e.dma_start(
                    bsm_raw, bsm_all.ap().rearrange("q (b k) -> q b k", b=NBH)))
                nraw2 = r.emit("sync", lambda e: e.dma_start(rm_raw, rm_all[:, :]))
                r.wait("sync", "sSY", nraw2)
                for j in range(4):
                    r.emit("sync", lambda e, j=j: e.dma_start(
                        bsm_dram[:, j],
                        bsm_raw.rearrange("(t j) b k -> j b t k", j=4)[j]))
                    r.emit("sync", lambda e, j=j: e.dma_start(
                        rm_dram[:, j],
                        rm_raw.rearrange("(t j) b -> j b t", j=4)[j],
                        allow_slow_non_contiguous=True))

            r.mark("P05")
            fA = st_p0.enter_context(nc.sbuf_tensor("fA", [128, 8 * QT], BF))
            fB = st_p0.enter_context(nc.sbuf_tensor("fB", [128, 8 * QT], BF))
            fR = st_p0.enter_context(nc.sbuf_tensor("fR", [128, 8 * QT], BF))
            fF = st_p0.enter_context(nc.sbuf_tensor("fF", [128, 8 * QT], BF))
            fW = [st_p0.enter_context(nc.sbuf_tensor(f"fW{j}", [128, 8 * 128], BF))
                  for j in range(3)]
            ptile = st_p0.enter_context(nc.sbuf_tensor("ptile", [128, C], BF))
            ptileb = st_p0.enter_context(nc.sbuf_tensor("ptileb", [128, C], BF))
            ptile2 = [ptile, ptileb]

            def mu_mix(dst, mu_key, t0):
                for c in range(8):
                    base = c * T + t0
                    if t0 % S == 0:
                        r.emit("vector", lambda e, d=dst, c=c: e.memset(
                            d[:, c * QT:c * QT + 1], 0.0))
                        r.emit("vector", lambda e, d=dst, c=c, base=base, m=mu_key:
                               e.tensor_scalar(
                                   d[:, c * QT + 1:(c + 1) * QT],
                                   xnT[:, base:base + QT - 1],
                                   mucol1m[m][:, c:c + 1], None, ALU.mult))
                    else:
                        r.emit("vector", lambda e, d=dst, c=c, base=base, m=mu_key:
                               e.tensor_scalar(
                                   d[:, c * QT:(c + 1) * QT],
                                   xnT[:, base - 1:base + QT - 1],
                                   mucol1m[m][:, c:c + 1], None, ALU.mult))
                    r.emit("vector", lambda e, d=dst, c=c, base=base, m=mu_key:
                           e.scalar_tensor_tensor(
                               d[:, c * QT:(c + 1) * QT],
                               xnT[:, base:base + QT],
                               mucol[m][:, c:c + 1],
                               d[:, c * QT:(c + 1) * QT],
                               ALU.mult, ALU.add))

            def load_wslice(buf, tag, d):
                return r.emit("sync", lambda e, buf=buf, tag=tag, d=d: e.dma_start(
                    buf.rearrange("p (cc j) -> p cc j", cc=8),
                    wT_sc[tag][d].rearrange("(cc p) j -> p cc j", p=128)))

            def mm_quarter(tag, rhs_buf, d, wbuf_par, slot):
                for cc in range(8):
                    r.emit("tensor", lambda e, cc=cc, w=wbuf_par, rb=rhs_buf, sl=slot:
                           e.matmul(pA[:, sl * 512:sl * 512 + QT],
                                    w[:, cc * 128:(cc + 1) * 128],
                                    rb[:, cc * QT:(cc + 1) * QT],
                                    start=(cc == 0), stop=(cc == 7)))
                return r.cnt["sPE"]

            wsched = [(tag, d) for _Q in range(8)
                      for tag in ("r", "k2", "v2") for d in range(8)]
            wst = {"ld": {}, "mm": {}}

            def wload(i):
                if i >= len(wsched):
                    return
                if i - 3 >= 0:
                    r.wait("sync", "sPE", wst["mm"][i - 3])
                tag, d = wsched[i]
                wst["ld"][i] = load_wslice(fW[i % 3], tag, d)

            for i in range(3):
                wload(i)

            bh_list = [(p, hip, sx) for p in range(8) for hip in range(2)
                       for sx in range(B_CORE)]
            site_work = {}

            def addw(si, fn):
                site_work.setdefault(si, []).append(fn)

            for j, tag in enumerate(("q", "k", "v")):
                base = 2 + j * 2
                for ch in range(8):
                    d = ch
                    addw(base + ch // 4, lambda tag=tag, d=d: wpd_proc(tag, d))
                    if d + 2 < 8:
                        addw(base + ch // 4,
                             lambda tag=tag, d=d: wpd_load(tag, d + 2))
                nxt = {"q": "k", "k": "v", "v": None}[tag]
                if nxt is not None:
                    addw(base + 1, lambda nxt=nxt: wpd_load(nxt, 0))
                    addw(base + 1, lambda nxt=nxt: wpd_load(nxt, 1))
            for j, si in enumerate((6, 7, 8, 9)):
                for bb in bh_list[j * 8:(j + 1) * 8]:
                    addw(si, lambda bb=bb: emit_low_bh(*bb))
            for bb in bh_list:
                addw(10, lambda bb=bb: emit_vhp_bh(*bb))
            addw(10, emit_nb128)
            for it in range(N_BISECT):
                addw(11 + it // 8, lambda it=it: emit_bisect_iter(it))
            addw(18, emit_sel_thr)
            for b2 in range(NBH):
                addw(18 + (2 + b2) // 6, lambda b2=b2: emit_sel_b(b2))

            def do_site(si):
                for fn in site_work.get(si, ()):
                    fn()

            wpd_load("q", 0)
            wpd_load("q", 1)
            r.wait("vector", "sPE", r.cnt["sPE"])
            for Q in range(8):
                t0 = Q * QT
                mu_mix(fA, "r", t0)
                ndv_mixr = r.cnt["sDV"]
                cons = [r.cnt["sAC"], r.cnt["sAC"]]
                r.wait("tensor", "sDV", r.cnt["sDV"])
                for d in range(8):
                    par = d % 2
                    gi = Q * 24 + d
                    r.wait("tensor", "sSY", wst["ld"][gi])
                    if d == 0:
                        r.wait("tensor", "sDV", ndv_mixr)
                    r.wait("tensor", "sAC", cons[par])
                    npe_q = mm_quarter("r", fA, d, fW[gi % 3], par)
                    wst["mm"][gi] = npe_q
                    wload(gi + 3)
                    r.wait("scalar", "sPE", npe_q)
                    cons[par] = r.emit("scalar", lambda e, d=d, par=par: e.activation(
                        fR[:, d * QT:(d + 1) * QT], pA[:, par * 512:par * 512 + QT],
                        AF.Sigmoid))
                do_site(Q * 3 + 0)
                r.wait("vector", "sPE", r.cnt["sPE"])
                mu_mix(fA, "k", t0)
                ndv_mixk = r.cnt["sDV"]
                consd = [r.cnt["sDV"], r.cnt["sDV"]]
                r.wait("tensor", "sAC", r.cnt["sAC"])
                for d in range(8):
                    par = d % 2
                    gi = Q * 24 + 8 + d
                    r.wait("tensor", "sSY", wst["ld"][gi])
                    if d == 0:
                        r.wait("tensor", "sDV", ndv_mixk)
                    r.wait("tensor", "sDV", consd[par])
                    npe_q = mm_quarter("k2", fA, d, fW[gi % 3], par)
                    wst["mm"][gi] = npe_q
                    wload(gi + 3)
                    r.wait("vector", "sPE", npe_q)
                    consd[par] = r.emit("vector", lambda e, d=d, par=par: e.tensor_scalar(
                        fB[:, d * QT:(d + 1) * QT], pA[:, par * 512:par * 512 + QT],
                        0.0, None, ALU.max))
                do_site(Q * 3 + 1)
                r.wait("scalar", "sDV", r.cnt["sDV"])
                r.wait("scalar", "sPE", r.cnt["sPE"])
                nac_sq = None
                for d in range(8):
                    nac_sq = r.emit("scalar", lambda e, d=d: e.activation(
                        fA[:, d * QT:(d + 1) * QT], fB[:, d * QT:(d + 1) * QT],
                        AF.Square))
                r.wait("tensor", "sAC", nac_sq)
                consd = [r.cnt["sDV"], r.cnt["sDV"]]
                r.wait("tensor", "sAC", r.cnt["sAC"])
                for d in range(8):
                    par = d % 2
                    gi = Q * 24 + 16 + d
                    r.wait("tensor", "sSY", wst["ld"][gi])
                    r.wait("tensor", "sDV", consd[par])
                    npe_q = mm_quarter("v2", fA, d, fW[gi % 3], par)
                    wst["mm"][gi] = npe_q
                    wload(gi + 3)
                    r.wait("vector", "sPE", npe_q)
                    consd[par] = r.emit("vector", lambda e, d=d, par=par: e.tensor_tensor(
                        fF[:, d * QT:(d + 1) * QT], pA[:, par * 512:par * 512 + QT],
                        fR[:, d * QT:(d + 1) * QT], ALU.mult))
                r.wait("tensor", "sDV", r.cnt["sDV"])
                prev_dma = [0, 0]
                prev_add = [0, 0]
                for tt in range(QT // 128):
                    par = tt % 2
                    wpt = pT2[par]
                    r.wait("tensor", "sDV", prev_add[par])
                    for d in range(8):
                        r.emit("tensor", lambda e, d=d, tt=tt, wpt=wpt: e.transpose(
                            wpt[:, d * 128:(d + 1) * 128],
                            fF[:, d * QT + tt * 128:d * QT + (tt + 1) * 128],
                            id_bf[:, :]))
                    npe_tt = r.cnt["sPE"]
                    r.wait("vector", "sPE", npe_tt)
                    r.wait("vector", "sGP", prev_dma[par])
                    ndv_tt = r.emit("vector", lambda e, p=par, wpt=wpt:
                                    e.tensor_copy(ptile2[p][:, :], wpt[:, :]))
                    prev_add[par] = ndv_tt
                    r.wait("gpsimd", "sDV", ndv_tt)
                    prev_dma[par] = r.emit("gpsimd", lambda e, t0=t0, tt=tt, p=par:
                                           e.dma_start(part_sc[t0 + tt * 128:
                                                               t0 + (tt + 1) * 128, :],
                                                       ptile2[p][:, :]))
                do_site(Q * 3 + 2)
                r.barrier()
            emit_sel_post()
            r.barrier()
            if debug:
                r.emit("gpsimd", lambda e: e.dma_start(dbg["d_norm"], norm_all[0:64, 0:128]))
                r.emit("gpsimd", lambda e: e.dma_start(dbg["d_thr"], thr[:, :]))
                r.emit("gpsimd", lambda e: e.dma_start(dbg["d_xnT"], xnT[:, 0:512]))
                r.emit("gpsimd", lambda e: e.dma_start(dbg["d_qhT"], qhT[:, :]))
                r.barrier()

        r.mark("P1")
        with contextlib.ExitStack() as st_p2:
            def ps2(name, shape, dt=F32):
                return st_p2.enter_context(nc.psum_tensor(name, shape, dt))
            pS = [ps2("pS0", [128, 1024]), ps2("pS1", [128, 1024])]
            pO = ps2("pO", [128, 1024])

            qtp = sb("qtp", [128, T], BF)
            ktp = sb("ktp", [128, T], BF)
            vtm = sb("vtm", [128, 32 * 130], BF)
            wsl = {w: sb(f"wsl_{w}", [128, 8 * 128], BF) for w in ("q", "k", "v")}
            lhsc = sb("lhsc", [128, 2048], BF)
            rhsc = sb("rhsc", [128, 2048], BF)
            aT = [sb("aT0", [128, 1024], BF), sb("aT1", [128, 1024], BF)]
            laT = sb("laT", [64, 64], BF)
            la_t1 = sb("la_t1", [64, 64], F32)
            la_t2 = sb("la_t2", [64, 64], F32)
            bm = sb("bm", [128, 64], F32)
            bm2c = [bm, sb("bm1", [128, 64], F32)]
            bmm = sb("bmm", [128, 64], F32)
            bx2 = [sb("bx0", [128, 1024], BF), sb("bx1", [128, 1024], BF)]
            rmb2 = [sb("rmb0", [128, 16], F32), sb("rmb1", [128, 16], F32)]
            epst = sb("epst", [128, 16], F32)
            outT = sb("outT", [65, 2048], F32)
            stage = sb("stage", [128, 1024], F32)
            stage2 = [stage, sb("stage1", [128, 1024], F32)]
            rden = sb("rden", [128, 1], F32)
            rdenw16 = sb("rdenw16", [128, 16], F32)
            vTf = sb("vTf", [128, 2048], BF)
            fslice = [vTf[:, 0:1024], vTf[:, 1024:2048]]
            dcol = sb("dcol", [128, 1], F32)

            r.emit("vector", lambda e: e.memset(
                vtm.ap().rearrange("p (k h o) -> p k h o", k=32, h=2)[:, :, :, 64:65], 1.0))
            ndv_pre = r.emit("gpsimd", lambda e: e.dma_start(lhsc[64:128, :], eall[:, :]))
            r.barrier()

            def bh_idx(p, hip, s):
                return (2 * p + hip) * 2 + s

            rhs2 = [rhsc, st_p2.enter_context(nc.sbuf_tensor("rhsc1", [128, 2048], BF))]
            lhs2 = [lhsc, st_p2.enter_context(nc.sbuf_tensor("lhsc1", [128, 2048], BF))]
            laT2 = [laT, st_p2.enter_context(nc.sbuf_tensor("laT1", [64, 64], BF))]
            r.emit("gpsimd", lambda e: e.dma_start(lhs2[1][64:128, :], eall[:, :]))
            r.barrier()

            state = {"ndv_ep": 0, "prev_evac": 0, "chunk_pe_end": 0,
                     "nac_la": [0, 0], "ngp_prep": [0, 0], "ndv_prep": [0, 0],
                     "ngp_bx": [0, 0], "nac_eps": 0, "ngc_mv": [0, 0],
                     "nac_la_last": 0, "ngp_prep_last": 0,
                     "prev_exp": [0, 0], "phA_dv": 0, "phA_done_dv": [0, 0], "pO_dv": 0,
                     "nst_a": 0, "ngp_rmb": [0, 0], "prev_exp2": [0, 0], "phA_dv4": [0, 0, 0, 0], "nac_ep": 0,
                     "pcons": [0, 0], "nac_qk": 0, "nsy_part": [0, 0], "ngp_add": [0, 0],
                     "nst_a2": [0, 0], "pA_free_ac": [0, 0], "pending_w1": None, "npe_vtr": 0,
                     "nld_f": [0, 0]}

            def emit_prep_bx(p, hip, sq):
                b = bh_idx(p, hip, sq)
                bpar = b % 2
                r.wait("gpsimd", "sDV", state["phA_done_dv"][bpar])
                state["ngp_bx"][bpar] = r.emit(
                    "gpsimd", lambda e, b=b, bpar=bpar: e.dma_start(
                        bx2[bpar][:, :],
                        bsm_dram[b].rearrange("j t k -> j (t k)")
                        .unsqueeze(1).broadcast_to([4, 32, 1024])))
                state["ngp_rmb"][bpar] = r.emit(
                    "gpsimd", lambda e, b=b, bpar=bpar: e.dma_start(
                        rmb2[bpar][:, :],
                        rm_dram[b].unsqueeze(1).broadcast_to([4, 32, 16])))

            def emit_prep(p, hip, sq, par, prev_chunk_pe, with_bx=False):
                b = bh_idx(p, hip, sq)
                p0 = 64 * hip
                ts0 = sq * S
                r.wait("gpsimd", "sPE", prev_chunk_pe)
                r.emit("gpsimd", lambda e, p0=p0, ts0=ts0, par=par: e.dma_start(
                    rhs2[par][0:64, :], qtp[p0:p0 + 64, ts0:ts0 + S]))
                r.emit("gpsimd", lambda e, p0=p0, ts0=ts0, par=par: e.dma_start(
                    lhs2[par][0:64, :], ktp[p0:p0 + 64, ts0:ts0 + S]))
                ngp_qk = r.cnt["sGP"]
                if with_bx:
                    emit_prep_bx(p, hip, sq)
                r.wait("vector", "sPE", prev_chunk_pe)
                ngp5 = r.emit("vector", lambda e, b=b, par=par: e.tensor_scalar(
                    rhs2[par][64:128, :],
                    norm_all[64:128, b * 64:(b + 1) * 64]
                    .rearrange("p (q o) -> p q o", o=1)
                    .broadcast_to([64, 64, 32]),
                    1.0, -BIGNEG, ALU.subtract, ALU.mult))
                r.wait("vector", "sAC", state["nac_la_last"])
                r.emit("vector", lambda e, b=b: e.tensor_scalar(
                    la_t1[:, :], lowT_all[:, b * 64:(b + 1) * 64],
                    0.125, None, ALU.mult))
                r.emit("vector", lambda e, b=b: e.tensor_scalar(
                    la_t2[:, :], selT_all[:, b * 64:(b + 1) * 64],
                    10000.0, None, ALU.mult))
                ndv_la = r.emit("vector", lambda e: e.tensor_tensor(
                    la_t1[:, :], la_t1[:, :], la_t2[:, :], ALU.subtract))
                r.wait("scalar", "sDV", ndv_la)
                state["nac_la"][par] = r.emit("scalar", lambda e, par=par: e.activation(
                    laT2[par][:, :], la_t1[:, :], AF.Exp, bias=cln32[0:64, :]))
                state["nac_la_last"] = state["nac_la"][par]
                state["ngp_prep"][par] = ngp_qk
                state["ndv_prep"][par] = ngp5

            def emit_phaseA_half(p, hip, sq, u, slots=(0, 1)):
                b = bh_idx(p, hip, sq)
                bpar = b % 2
                bmp = bm2c[bpar]
                p0 = 64 * hip
                ts0 = sq * S
                t, kq = u // 4, u % 4
                sl4 = slots[u % len(slots)]
                pap = [pA[:, 0:512], pA[:, 512:1024],
                       pO[:, 0:512], pO[:, 512:1024]][sl4]
                r.wait("tensor", "sDV", state["phA_dv4"][sl4])
                if sl4 < 2:
                    r.wait("tensor", "sAC", state["pcons"][sl4])
                    r.wait("tensor", "sAC", state["pA_free_ac"][sl4])
                else:
                    r.wait("tensor", "sAC", state["prev_evac"])
                r.wait("tensor", "sDV", state["ndv_ep"])
                if u == 0:
                    r.wait("tensor", "sGP", state["ngp_bx"][bpar])
                    r.wait("tensor", "sAC", state["nac_qk"])
                r.emit("tensor", lambda e, p0=p0, ts0=ts0, t=t, kq=kq, pap=pap:
                       e.matmul(pap,
                                qtp[p0:p0 + 64,
                                    ts0 + t * 128:ts0 + (t + 1) * 128],
                                ktp[p0:p0 + 64,
                                    ts0 + kq * 512:ts0 + (kq + 1) * 512],
                                start=True, stop=True))
                npe_a = r.cnt["sPE"]
                r.wait("vector", "sPE", npe_a)
                if kq == 0:
                    r.wait("vector", "sGC", state["ngc_mv"][bpar])
                nred = r.emit("vector", lambda e, kq=kq, pap=pap, bmp=bmp:
                              e.tensor_reduce(
                                  bmp[:, kq * 16:(kq + 1) * 16],
                                  pap.rearrange("p (k e) -> p k e", e=32),
                                  axis=AX.X, op=ALU.max))
                state["phA_dv4"][sl4] = nred
                if sl4 >= 2:
                    state["pO_dv"] = nred
                if kq == 3:
                    r.emit("vector", lambda e, t=t, bmp=bmp, bpar=bpar:
                           e.tensor_tensor(bmm[:, :], bmp[:, :],
                                           bx2[bpar][:, t * 64:(t + 1) * 64],
                                           ALU.add))
                    state["ngc_mv"][bpar] = 0
                    r.emit("vector", lambda e, b=b, t=t:
                           e.tensor_reduce(
                               mv_all[:, b * 16 + t:b * 16 + t + 1],
                               bmm[:, :], axis=AX.X, op=ALU.max))
                if u == 63:
                    r.wait("vector", "sGP", state["ngp_rmb"][bpar])
                    r.wait("vector", "sGC", state["ngc_mv"][bpar])
                    r.wait("vector", "sAC", state["nac_eps"])
                    r.emit("vector", lambda e, b=b: e.tensor_scalar(
                        epst[:, :], mv_all[:, b * 16:(b + 1) * 16],
                        0.125, None, ALU.mult))
                    ndv_e = r.emit("vector", lambda e, bpar=bpar: e.tensor_tensor(
                        epst[:, :], epst[:, :], rmb2[bpar][:, :], ALU.max))
                    r.wait("scalar", "sDV", ndv_e)
                    state["nac_eps"] = r.emit("scalar", lambda e, b=b: e.activation(
                        eps_all[:, b * 16:(b + 1) * 16], epst[:, :],
                        AF.Exp, bias=clneps[:, :]))
                    state["phA_done_dv"][bpar] = ndv_e

            def emit_av(p, hip, sq, par, kg):
                b = bh_idx(p, hip, sq)
                qh, k = kg // 16, kg % 16
                kpar = kg % 2
                r.wait("tensor", "sAC", state["prev_exp2"][kpar])
                if k == 0:
                    r.wait("tensor", "sAC", state["prev_evac"])
                    r.wait("tensor", "sDV", state["pO_dv"])
                for sl in range(2):
                    r.emit("tensor", lambda e, sq=sq, k=k, sl=sl, bpar=kpar, hip=hip:
                           e.matmul(pO[0:65, sl * 512:(sl + 1) * 512],
                                    vtm[:, (sq * 16 + k) * 130 + hip * 65:
                                        (sq * 16 + k) * 130 + hip * 65 + 65],
                                    aT[bpar][:, sl * 512:(sl + 1) * 512],
                                    start=(k == 0), stop=False))
                if k == 15:
                    r.wait("tensor", "sAC", state["nac_la"][par])
                    for sl in range(2):
                        r.emit("tensor", lambda e, b=b, qh=qh, sl=sl, par=par:
                               e.matmul(pO[0:65, sl * 512:(sl + 1) * 512],
                                        vhp_all[:, b * 65:(b + 1) * 65],
                                        laT2[par][:, qh * 32 + sl * 16:
                                                  qh * 32 + (sl + 1) * 16]
                                        .rearrange("p (q o) -> p q o", o=1)
                                        .broadcast_to([64, 16, 32]),
                                        start=False, stop=True))
                    npe_o = r.cnt["sPE"]
                    r.wait("scalar", "sPE", npe_o)
                    state["prev_evac"] = r.emit("scalar", lambda e, qh=qh:
                                                e.activation(
                                                    outT[:, qh * 1024:(qh + 1) * 1024],
                                                    pO[0:65, :], AF.Copy))

            def emit_chunk_iter(p, hip, sq, par, kg, units):
                qh, k = kg // 16, kg % 16
                kpar = kg % 2
                r.wait("tensor", "sAC", state["prev_exp"][kpar])
                if kg == 0:
                    r.wait("tensor", "sGP", state["ngp_prep"][par])
                    r.wait("tensor", "sDV", state["ndv_prep"][par])
                for sl in range(2):
                    r.emit("tensor", lambda e, k=k, qh=qh, sl=sl, par=par, bpar=kpar:
                           e.matmul(pS[bpar][:, sl * 512:(sl + 1) * 512],
                                    lhs2[par][:, k * 128:(k + 1) * 128],
                                    rhs2[par][:, qh * 1024 + sl * 512:
                                              qh * 1024 + (sl + 1) * 512],
                                    start=True, stop=True))
                npe_s = r.cnt["sPE"]
                if kg > 0:
                    emit_av(p, hip, sq, par, kg - 1)
                if kg == 4 and state["pending_w1"] is not None:
                    state["pending_w1"]()
                    state["pending_w1"] = None
                for bhu, u in units:
                    emit_phaseA_half(*bhu, u)
                r.wait("scalar", "sPE", npe_s)
                newexp = r.emit("scalar", lambda e, bpar=kpar:
                                e.activation(aT[bpar][:, :], pS[bpar][:, :],
                                             AF.Exp, scale=0.125))
                state["prev_exp2"][kpar] = newexp
                state["prev_exp"][kpar] = newexp

            def emit_epilogue(p, hip, sq, par, last):
                b = bh_idx(p, hip, sq)
                stg = stage2[par]
                h = 2 * p + hip
                r.wait("tensor", "sAC", state["prev_evac"])
                if last:
                    r.wait("tensor", "sAC", max(state["prev_exp"][0],
                                                state["prev_exp"][1]))
                    slot_off = [0, 128, 256, 384, 512, 640, 768, 896]
                    for i in range(16):
                        so = slot_off[i % 8]
                        r.emit("tensor", lambda e, i=i, so=so: e.transpose(
                            pS[i // 8][:, so:so + 65],
                            outT[:, i * 128:(i + 1) * 128], id_f[0:65, 0:65]))
                    npe_tr = r.cnt["sPE"]
                    r.wait("vector", "sPE", npe_tr)
                    r.wait("vector", "sAC", state["nac_ep"])
                    for h2 in range(2):
                        r.emit("vector", lambda e, b=b, h2=h2: e.tensor_tensor(
                            rdenw16[:, h2 * 8:(h2 + 1) * 8],
                            pS[h2].ap().rearrange("p (a c) -> p a c", a=8)
                            [:, :, 64:65].rearrange("p a c -> p (a c)"),
                            eps_all[:, b * 16 + h2 * 8:b * 16 + (h2 + 1) * 8],
                            ALU.add))
                    ndv_rd = r.emit("vector", lambda e: e.reciprocal(
                        rdenw16[:, :], rdenw16[:, :]))
                    r.wait("scalar", "sDV", ndv_rd)
                    r.wait("scalar", "sSY", state["nst_a2"][par])
                    nac_ep = 0
                    for i in range(16):
                        so = slot_off[i % 8]
                        nac_ep = r.emit("scalar", lambda e, i=i, so=so, stg=stg:
                                        e.activation(
                                            stg[:, i * 64:(i + 1) * 64],
                                            pS[i // 8][:, so:so + 64], AF.Identity,
                                            scale=rdenw16[:, i:i + 1]))
                    state["ndv_ep"] = ndv_rd
                    state["nac_ep"] = nac_ep
                else:
                    r.wait("tensor", "sDV", max(state["phA_dv4"][0],
                                                state["phA_dv4"][1]))

                    def emit_wave(w, gate0=0, gate1=0, npe_tr=None):
                        acts = []
                        if npe_tr is None:
                            if gate0:
                                r.wait("tensor", "sAC", gate0)
                            for sl8 in range(8):
                                i = w * 8 + sl8
                                if gate1 and sl8 == 4:
                                    r.wait("tensor", "sAC", gate1)
                                r.emit("tensor", lambda e, i=i, sl8=sl8:
                                       e.transpose(
                                           pA[:, sl8 * 128:sl8 * 128 + 65],
                                           outT[:, i * 128:(i + 1) * 128],
                                           id_f[0:65, 0:65]))
                            npe_tr = r.cnt["sPE"]
                        r.wait("vector", "sPE", npe_tr)
                        if w == 0:
                            r.wait("vector", "sAC", state["nac_ep"])
                        r.emit("vector", lambda e, b=b, w=w: e.tensor_tensor(
                            rdenw16[:, w * 8:(w + 1) * 8],
                            pA.ap().rearrange("p (a c) -> p a c", a=8)
                            [:, :, 64:65].rearrange("p a c -> p (a c)"),
                            eps_all[:, b * 16 + w * 8:b * 16 + (w + 1) * 8],
                            ALU.add))
                        ndv_rd = r.emit("vector", lambda e, w=w: e.reciprocal(
                            rdenw16[:, w * 8:(w + 1) * 8],
                            rdenw16[:, w * 8:(w + 1) * 8]))
                        r.wait("scalar", "sDV", ndv_rd)
                        if w == 0:
                            r.wait("scalar", "sSY", state["nst_a2"][par])
                        for sl8 in range(8):
                            i = w * 8 + sl8
                            acts.append(r.emit(
                                "scalar", lambda e, i=i, sl8=sl8, stg=stg:
                                e.activation(stg[:, i * 64:(i + 1) * 64],
                                             pA[:, sl8 * 128:sl8 * 128 + 64],
                                             AF.Identity,
                                             scale=rdenw16[:, i:i + 1])))
                        return ndv_rd, acts

                    for sl8 in range(8):
                        r.emit("tensor", lambda e, sl8=sl8: e.transpose(
                            pA[:, sl8 * 128:sl8 * 128 + 65],
                            outT[:, sl8 * 128:(sl8 + 1) * 128], id_f[0:65, 0:65]))
                    npe_tr0 = r.cnt["sPE"]

                    def w1fn(b=b, par=par, stg=stg, h=h, sq=sq, npe_tr0=npe_tr0):
                        _, acts0 = emit_wave(0, npe_tr=npe_tr0)
                        ndv_rd, acts1 = emit_wave(1, gate0=acts0[3],
                                                  gate1=acts0[7])
                        state["ndv_ep"] = ndv_rd
                        state["nac_ep"] = acts1[7]
                        state["pA_free_ac"] = [acts1[3], acts1[7]]
                        finish(b, par, stg, h, sq, acts1[7])

                    state["pending_w1"] = w1fn
                    return

                finish(b, par, stg, h, sq, state["nac_ep"])

            def emit_xf_load(p, hip, sq, par):
                h = 2 * p + hip
                r.wait("sync", "sGC", state["ngp_add"][par])
                r.emit("sync", lambda e, h=h, sq=sq, par=par: e.dma_start(
                    xtile[par].ap().rearrange("p (i d) -> p i d", i=16),
                    x_in.rearrange("(s i pp) c -> s pp i c", i=16, pp=128)
                    [sq][:, :, h * 64:(h + 1) * 64]))
                r.wait("sync", "sPE", state["npe_vtr"])
                state["nld_f"][par] = r.emit(
                    "sync", lambda e, h=h, sq=sq, par=par: e.dma_start(
                        fslice[par].rearrange("p (i d) -> p i d", i=16),
                        part_sc.rearrange("(s i pp) c -> s pp i c", i=16, pp=128)
                        [sq][:, :, h * 64:(h + 1) * 64]))

            def finish(b, par, stg, h, sq, nac_ep):
                r.wait("gpsimdc", "sAC", nac_ep)
                r.wait("gpsimdc", "sSY", state["nld_f"][par])
                r.emit("gpsimdc", lambda e, par=par, stg=stg: e.tensor_tensor(
                    stg[:, :], stg[:, :], xtile[par][:, :], ALU.add))
                state["ngp_add"][par] = r.emit(
                    "gpsimdc", lambda e, par=par, stg=stg: e.tensor_tensor(
                        stg[:, :], stg[:, :], fslice[par], ALU.add))
                r.wait("sync", "sGC", state["ngp_add"][par])
                state["nst_a2"][par] = r.emit(
                    "sync", lambda e, h=h, sq=sq, stg=stg: e.dma_start(
                        out.rearrange("(s i pp) c -> s pp i c", i=16, pp=128)
                        [sq][:, :, h * 64:(h + 1) * 64],
                        stg.ap().rearrange("p (i d) -> p i d", i=16)))

            def emit_projqk_group(p, tag, n):
                pcons = state["pcons"]
                dst = qtp if tag == "q" else ktp
                slot = n % 2
                r.wait("tensor", "sAC",
                       max(pcons[slot], state["pA_free_ac"][slot]))
                r.wait("tensor", "sDV",
                       max(state["phA_dv4"][slot], state["ndv_ep"]))
                for cc in range(8):
                    r.emit("tensor", lambda e, tag=tag, n=n, cc=cc, slot=slot:
                           e.matmul(pA[:, slot * 512:(slot + 1) * 512],
                                    wsl[tag][:, cc * 128:(cc + 1) * 128],
                                    xnT[:, cc * T + n * 512:cc * T + (n + 1) * 512],
                                    start=(cc == 0), stop=(cc == 7)))
                npe_p = r.cnt["sPE"]
                r.wait("scalar", "sPE", npe_p)
                pcons[slot] = r.emit("scalar", lambda e, tag=tag, n=n, slot=slot,
                                     dst=dst, p=p: e.activation(
                                         dst[:, n * 512:(n + 1) * 512],
                                         pA[:, slot * 512:(slot + 1) * 512],
                                         AF.Identity, bias=bcol[tag][:, p:p + 1]))

            state["nls_next"] = 0
            for p in range(8):
                if p == 0:
                    for tag in ("q", "k", "v"):
                        r.emit("sync", lambda e, tag=tag: e.dma_start(
                            wsl[tag].rearrange("p (cc j) -> p cc j", cc=8),
                            wT_sc[tag][0].rearrange("(cc p) j -> p cc j", p=128)))
                    state["nls_next"] = r.cnt["sSY"]
                nls = state["nls_next"]
                pcons = state["pcons"]
                if p == 0:
                    r.wait("scalar", "sPE", r.cnt["sPE"])
                    r.wait("scalar", "sGP", r.cnt["sGP"])
                    r.wait("tensor", "sSY", nls)
                    for tag in ("q", "k"):
                        for n in range(8):
                            emit_projqk_group(0, tag, n)
                    state["nac_qk"] = r.cnt["sAC"]
                bhs = [(p, hip, sq) for hip in range(2) for sq in range(B_CORE)]
                r.wait("gpsimd", "sAC", r.cnt["sAC"])
                emit_prep(*bhs[0], 0, 0, with_bx=True)
                emit_prep_bx(*bhs[1])
                vsl = [(pS[0], 0), (pS[0], 512), (pS[1], 0), (pS[1], 512)]
                vcons = [0] * 4
                pre_units = [(bhs[0], u) for u in range(64)] + \
                            [(bhs[1], u) for u in range(16)]
                upos = [0]

                def pre_phA(slots=(0, 1, 2, 3)):
                    for _ in range(2):
                        if upos[0] < 80:
                            emit_phaseA_half(*pre_units[upos[0]][0],
                                             pre_units[upos[0]][1], slots=slots)
                            upos[0] += 1
                r.wait("tensor", "sDV", state["ndv_ep"])
                r.wait("tensor", "sAC", state["nac_ep"])
                r.wait("scalar", "sGC", max(state["ngp_add"]))
                r.wait("sync", "sGC", max(state["ngp_add"]))
                for half in range(2):
                    vev = [0] * 4
                    for n4 in range(4):
                        n = half * 4 + n4
                        slot = n % 2
                        r.wait("tensor", "sAC",
                               max(pcons[slot], state["pA_free_ac"][slot]))
                        r.wait("tensor", "sDV",
                               max(state["phA_dv4"][slot], state["ndv_ep"]))
                        for cc in range(8):
                            r.emit("tensor", lambda e, n=n, cc=cc, slot=slot:
                                   e.matmul(pA[:, slot * 512:(slot + 1) * 512],
                                            wsl["v"][:, cc * 128:(cc + 1) * 128],
                                            xnT[:, cc * T + n * 512:
                                                cc * T + (n + 1) * 512],
                                            start=(cc == 0), stop=(cc == 7)))
                        npe_p = r.cnt["sPE"]
                        r.wait("scalar", "sPE", npe_p)
                        pcons[slot] = r.emit("scalar", lambda e, n4=n4, slot=slot, p=p:
                                             e.activation(
                                                 vTf[:, n4 * 512:(n4 + 1) * 512],
                                                 pA[:, slot * 512:(slot + 1) * 512],
                                                 AF.Identity,
                                                 bias=bcol["v"][:, p:p + 1]))
                        vev[n4] = pcons[slot]
                        pre_phA(slots=(2, 3))
                    if half == 1 and p < 7:
                        r.wait("sync", "sPE", r.cnt["sPE"])
                        for tag in ("q", "k", "v"):
                            r.emit("sync", lambda e, tag=tag, p=p: e.dma_start(
                                wsl[tag].rearrange("p (cc j) -> p cc j", cc=8),
                                wT_sc[tag][p + 1].rearrange("(cc p) j -> p cc j",
                                                            p=128)))
                        state["nls_next"] = r.cnt["sSY"]
                    for tt4 in range(16):
                        tt = half * 16 + tt4
                        slot = tt % 4
                        vps, voff = vsl[slot]
                        r.wait("tensor", "sAC", max(vcons[slot], vev[tt4 // 4]))
                        npe_v = r.emit("tensor", lambda e, tt4=tt4, vps=vps, voff=voff:
                                       e.matmul(vps[:, voff:voff + 128],
                                                vTf[:, tt4 * 128:(tt4 + 1) * 128],
                                                id_bf[:, :],
                                                start=True, stop=True))
                        pre_phA()
                        r.wait("scalar", "sPE", npe_v)
                        r.emit("scalar", lambda e, tt=tt, vps=vps, voff=voff:
                               e.activation(vtm[:, tt * 130:tt * 130 + 64],
                                            vps[:, voff:voff + 64], AF.Copy))
                        vcons[slot] = r.emit(
                            "scalar", lambda e, tt=tt, vps=vps, voff=voff:
                            e.activation(vtm[:, tt * 130 + 65:tt * 130 + 129],
                                         vps[:, voff + 64:voff + 128], AF.Copy))
                state["npe_vtr"] = r.cnt["sPE"]
                r.wait("tensor", "sAC", r.cnt["sAC"])
                r.wait("gpsimd", "sAC", r.cnt["sAC"])

                for i in range(4):
                    par = i % 2
                    emit_xf_load(*bhs[i], par)
                    if i < 3:
                        emit_prep(*bhs[i + 1], (i + 1) % 2, state["chunk_pe_end"])
                    for kg in range(32):
                        units = []
                        if i == 0:
                            if kg == 16:
                                emit_prep_bx(*bhs[2])
                            if kg < 24:
                                u0 = 16 + 2 * kg
                                units = [(bhs[1], u0), (bhs[1], u0 + 1)]
                            else:
                                u0 = 2 * (kg - 24)
                                units = [(bhs[2], u0), (bhs[2], u0 + 1)]
                        elif i == 1:
                            if kg == 0:
                                emit_prep_bx(*bhs[3])
                            if kg >= 8:
                                u0 = 16 + 2 * (kg - 8)
                                units = [(bhs[2], u0), (bhs[2], u0 + 1)]
                        elif i == 2:
                            if kg >= 8:
                                u0 = 2 * (kg - 8)
                                units = [(bhs[3], u0), (bhs[3], u0 + 1)]
                        else:
                            if 8 <= kg < 16:
                                u0 = 48 + 2 * (kg - 8)
                                units = [(bhs[3], u0), (bhs[3], u0 + 1)]
                            elif kg >= 16 and p < 7:
                                if kg == 16:
                                    r.wait("tensor", "sSY", state["nls_next"])
                                    r.wait("scalar", "sGP", r.cnt["sGP"])
                                tag = "q" if kg < 24 else "k"
                                emit_projqk_group(p + 1, tag, kg % 8)
                                if kg == 31:
                                    state["nac_qk"] = r.cnt["sAC"]
                        emit_chunk_iter(*bhs[i], par, kg, units)
                    emit_av(*bhs[i], par, 31)
                    state["chunk_pe_end"] = r.cnt["sPE"]
                    emit_epilogue(*bhs[i], par, i == 3)
                if debug and p == 0:
                    r.barrier()
                    r.emit("gpsimd", lambda e: e.dma_start(dbg["d_mv"], mv_all[:, 0:32]))
                    r.emit("gpsimd", lambda e: e.dma_start(dbg["d_eps"], eps_all[:, 0:32]))
                    r.emit("gpsimd", lambda e: e.dma_start(dbg["d_qt"], qtp[:, 0:512]))
                    r.emit("gpsimd", lambda e: e.dma_start(dbg["d_ot"], outT[:, 0:1024]))
                    r.emit("gpsimd", lambda e: e.dma_start(dbg["d_stage"], stage[:, :]))
                    r.emit("gpsimd", lambda e: e.dma_start(dbg["d_la"], laT[:, :]))
                r.barrier(("sync", "tensor", "vector"))
            r.barrier()

        r.mark("P2")
        r.wait("gpsimd", "sGP", r.cnt["sGP"])
        r.wait("gpsimd", "sSY", r.cnt["sSY"])

        r.mark("ALL")
        lim = r.seq + 1
        if trunc is not None:
            lim = r.marks.get(trunc, trunc if isinstance(trunc, int) else r.seq + 1)
        with nc.Block() as block:
            @block.sync
            def _(e):
                for sq, f in r.ops["sync"]:
                    if sq <= lim:
                        f(e)

            @block.tensor
            def _(e):
                for sq, f in r.ops["tensor"]:
                    if sq <= lim:
                        f(e)

            @block.vector
            def _(e):
                for sq, f in r.ops["vector"]:
                    if sq <= lim:
                        f(e)

            @block.scalar
            def _(e):
                for sq, f in r.ops["scalar"]:
                    if sq <= lim:
                        f(e)

            @block.gpsimd
            def _(e):
                for sq, f in r.ops["gpsimd"]:
                    if sq <= lim:
                        f(e)

    return nc


def kernel(**inputs):
    X = np.ascontiguousarray(np.asarray(inputs["L0_attn_X"], np.float32))
    cst = _host_consts()
    base = {
        "w_q": np.ascontiguousarray(inputs["L0_attn_Wq"], dtype=np.float32),
        "w_k": np.ascontiguousarray(inputs["L0_attn_Wk"], dtype=np.float32),
        "w_v": np.ascontiguousarray(inputs["L0_attn_Wv"], dtype=np.float32),
        "w_r": np.ascontiguousarray(inputs["L0_ffn_W_r"], dtype=np.float32),
        "w_k2": np.ascontiguousarray(inputs["L0_ffn_W_k"], dtype=np.float32),
        "w_v2": np.ascontiguousarray(inputs["L0_ffn_W_v"], dtype=np.float32),
        "b_q": np.ascontiguousarray(inputs["L0_attn_bq"], dtype=np.float32).reshape(1, C),
        "b_k": np.ascontiguousarray(inputs["L0_attn_bk"], dtype=np.float32).reshape(1, C),
        "b_v": np.ascontiguousarray(inputs["L0_attn_bv"], dtype=np.float32).reshape(1, C),
        "b_mur": np.ascontiguousarray(inputs["L0_ffn_mu_r"], dtype=np.float32).reshape(1, C),
        "b_muk": np.ascontiguousarray(inputs["L0_ffn_mu_k"], dtype=np.float32).reshape(1, C),
    }
    base.update(cst)
    in_maps = []
    for core in range(8):
        m = dict(base)
        m["x"] = np.ascontiguousarray(
            X[2 * core:2 * core + 2].reshape(T, C))
        in_maps.append(m)
    nc = build_program(debug=False)
    res = run_bass_kernel_spmd(nc, in_maps, core_ids=list(range(8)))
    outs = [np.asarray(r["out"], np.float32).reshape(2, S, C) for r in res.results]
    return np.concatenate(outs, axis=0)


# version 25
# speedup vs baseline: 1.0151x; 1.0021x over previous
import contextlib
import numpy as np
import ml_dtypes

import concourse.bass as bass
import concourse.mybir as mybir
from concourse.bass_utils import run_bass_kernel_spmd

bf16 = ml_dtypes.bfloat16
F32 = mybir.dt.float32
BF = mybir.dt.bfloat16
AF = mybir.ActivationFunctionType
ALU = mybir.AluOpType
AX = mybir.AxisListType

B_CORE = 2
S = 2048
T = B_CORE * S
C = 1024
H = 16
HD = 64
BLK = 32
NBH = 32
NUM_BLOCK = 1024
BIGNEG = -239616.0
LN32 = float(np.log(32.0))
LNEPS = float(np.log(1e-6))
N_BISECT = 58
QT = 512

ENGS = ("tensor", "vector", "scalar", "sync", "gpsimd")
ESEM = {"tensor": "sPE", "vector": "sDV", "scalar": "sAC",
        "sync": "sSY", "gpsimd": "sGP", "gpsimdc": "sGC"}


class Rec:
    def __init__(self, nc, stack):
        self.nc = nc
        self.ops = {e: [] for e in ENGS}
        self.sem = {}
        self.cnt = {}
        self.seq = 0
        self.marks = {}
        for n in ("sPE", "sDV", "sAC", "sSY", "sGP", "sGC"):
            self.sem[n] = stack.enter_context(nc.semaphore(n))
            self.cnt[n] = 0

    def _push(self, eng, fn):
        self.seq += 1
        if eng == "gpsimdc":
            eng = "gpsimd"
        self.ops[eng].append((self.seq, fn))

    def mark(self, name):
        self.marks[name] = self.seq

    def emit(self, eng, fn):
        sname = ESEM[eng]
        n = 16 if eng in ("sync", "gpsimd") else 1
        if eng != "tensor" and self.cnt[sname]:
            self.wait(eng, sname, self.cnt[sname])
        self.cnt[sname] += n
        h = self.sem[sname]
        self._push(eng, lambda e, fn=fn, h=h, n=n: fn(e).then_inc(h, n))
        return self.cnt[sname]

    def wait(self, eng, sname, val):
        if not val:
            return
        h = self.sem[sname]
        self._push(eng, lambda e, h=h, v=val: e.wait_ge(h, v))

    def barrier(self, engs=ENGS):
        snap = dict(self.cnt)
        for eng in engs:
            for sname, v in snap.items():
                if sname != ESEM[eng] and v:
                    self.wait(eng, sname, v)


def _host_consts():
    cst = {}
    cst["c_id_bf"] = np.eye(128, dtype=np.float32).astype(bf16)
    cst["c_id_f32"] = np.eye(128, dtype=np.float32)
    ii = np.arange(64)
    cst["c_band"] = (np.abs(ii[:, None] - ii[None, :]) <= 1).astype(np.float32) * 5000.0
    e = np.zeros((64, 2048), np.float32)
    for cc in range(16):
        for j in range(128):
            e[4 * cc + j // 32, 128 * cc + j] = 1.0
    cst["c_eall"] = e.astype(bf16)
    cst["c_ones_row"] = np.ones((1, 128), np.float32).astype(bf16)
    return cst


def build_program(debug=False, trunc=None):
    nc = bass.Bass("TRN2", target_bir_lowering=False, debug=False)

    x_in = nc.dram_tensor("x", [T, C], F32, kind="ExternalInput").ap()
    w_in = {w: nc.dram_tensor(f"w_{w}", [C, C], F32, kind="ExternalInput").ap()
            for w in ("q", "k", "v", "r", "k2", "v2")}
    b_in = {w: nc.dram_tensor(f"b_{w}", [1, C], F32, kind="ExternalInput").ap()
            for w in ("q", "k", "v", "mur", "muk")}
    c_id_bf = nc.dram_tensor("c_id_bf", [128, 128], BF, kind="ExternalInput").ap()
    c_id_f32 = nc.dram_tensor("c_id_f32", [128, 128], F32, kind="ExternalInput").ap()
    c_band = nc.dram_tensor("c_band", [64, 64], F32, kind="ExternalInput").ap()
    c_eall = nc.dram_tensor("c_eall", [64, 2048], BF, kind="ExternalInput").ap()
    c_ones_row = nc.dram_tensor("c_ones_row", [1, 128], BF, kind="ExternalInput").ap()

    out = nc.dram_tensor("out", [T, C], F32, kind="ExternalOutput").ap()

    wT_sc = {w: nc.dram_tensor(f"wT_{w}", [8, C, 128], BF).ap()
             for w in ("q", "k", "v", "r", "k2", "v2")}
    part_sc = nc.dram_tensor("part_sc", [T, C], BF).ap()
    bsm_dram = nc.dram_tensor("bsm_dram", [NBH, 4, 16, 64], F32).ap()
    rm_dram = nc.dram_tensor("rm_dram", [NBH, 4, 16], F32).ap()
    bsm_raw = nc.dram_tensor("bsm_raw", [64, NBH, 64], F32).ap()
    selT_dram = nc.dram_tensor("selT_dram", [64, NBH * 64], F32).ap()
    rm_raw = nc.dram_tensor("rm_raw", [64, NBH], F32).ap()
    attn_sc = nc.dram_tensor("attn_sc", [B_CORE, S, C], F32).ap()

    dbg = {}
    if debug:
        for name, shape in [("d_xnT", [128, 512]), ("d_qhT", [128, 1024]),
                            ("d_norm", [64, 128]), ("d_thr", [128, 32]),
                            ("d_mv", [128, 32]), ("d_qt", [128, 512]),
                            ("d_at", [128, 1024]), ("d_ot", [65, 1024]),
                            ("d_eps", [128, 32]), ("d_la", [64, 64]),
                            ("d_stage", [128, 1024])]:
            dbg[name] = nc.dram_tensor(name, shape, F32, kind="ExternalOutput").ap()

    stack = contextlib.ExitStack()
    with stack:
        r = Rec(nc, stack)

        def sb(name, shape, dt):
            return stack.enter_context(nc.sbuf_tensor(name, shape, dt))

        xnT = sb("xnT", [128, 8 * T], BF)
        id_bf = sb("id_bf", [128, 128], BF)
        id_f = sb("id_f", [128, 128], F32)
        band = sb("band", [64, 64], F32)
        eall = sb("eall", [64, 2048], BF)
        ones_row = sb("ones_row", [1, 128], BF)
        ones128 = sb("ones128", [128, 128], F32)
        bcol = {w: sb(f"bcol_{w}", [128, 8], F32) for w in ("q", "k", "v")}
        mucol = {w: sb(f"mucol_{w}", [128, 8], F32) for w in ("r", "k")}
        mucol1m = {w: sb(f"mucol1m_{w}", [128, 8], F32) for w in ("r", "k")}
        bvrow = sb("bvrow", [1, C], BF)
        norm_all = sb("norm_all", [128, NBH * 64], F32)
        lowT_all = sb("lowT_all", [64, NBH * 64], F32)
        rm_all = sb("rm_all", [64, NBH], F32)
        selT_all = sb("selT_all", [64, NBH * 64], F32)
        bsm_all = sb("bsm_all", [64, NBH * 64], F32)
        vhp_all = sb("vhp_all", [64, NBH * 65], BF)
        thr = sb("thr", [128, 32], F32)
        mv_all = sb("mv_all", [128, NBH * 16], F32)
        eps_all = sb("eps_all", [128, NBH * 16], F32)
        cln32 = sb("cln32", [128, 1], F32)
        clneps = sb("clneps", [128, 1], F32)

        pA = stack.enter_context(nc.psum_tensor("pA", [128, 1024], F32))

        xtile = [sb("xtile0", [128, C], F32), sb("xtile1", [128, C], F32)]
        xnrow = sb("xnrow", [128, C], BF)
        sq_dump = sb("sq_dump", [128, C], F32)
        col_a = sb("col_a", [128, 2], F32)
        col_b = sb("col_b", [128, 2], F32)
        col_c = sb("col_c", [128, 2], F32)
        xnrow2 = sb("xnrow2", [128, C], BF)
        xnh = sb("xnh", [128, 1024], BF)
        small_f = sb("small_f", [64, 64], F32)
        sel_buf = sb("sel_buf", [64, 64], F32)

        for dst, src in [(id_bf[:, :], c_id_bf), (id_f[:, :], c_id_f32),
                         (band[:, :], c_band), (eall[:, :], c_eall),
                         (ones_row[:, :], c_ones_row)]:
            r.emit("sync", lambda e, d=dst, s=src: e.dma_start(d, s))
        ngp_bv = r.emit("gpsimd", lambda e: e.dma_start(bvrow[:, :], b_in["v"]))
        for i, w in enumerate(["q", "k", "v", "mur", "muk"]):
            r.emit("sync", lambda e, i=i, s=b_in[w]: e.dma_start(
                brows[:, i * 128:(i + 1) * 128],
                s.rearrange("o (p j) -> (o p) j", p=8)))
        n_sy_const = r.cnt["sSY"]

        r.emit("vector", lambda e: e.memset(ones128[:, :], 1.0))
        r.emit("vector", lambda e: e.memset(cln32[:, :], LN32))
        r.emit("vector", lambda e: e.memset(clneps[:, :], LNEPS))

        r.wait("tensor", "sSY", n_sy_const)
        for i in range(5):
            r.emit("tensor", lambda e, i=i: e.transpose(
                pA[:, i * 8:(i + 1) * 8], brows[:, i * 128:(i + 1) * 128],
                id_f[0:8, 0:8]))
        npe = r.cnt["sPE"]
        r.wait("vector", "sPE", npe)
        for i, dst in enumerate([bcol["q"], bcol["k"], bcol["v"],
                                 mucol["r"], mucol["k"]]):
            r.emit("vector", lambda e, i=i, d=dst: e.tensor_copy(
                d[:, :], pA[:, i * 8:(i + 1) * 8]))
        for w in ("r", "k"):
            r.emit("vector", lambda e, w=w: e.tensor_scalar(
                mucol1m[w][:, :], mucol[w][:, :], -1.0, 1.0, ALU.mult, ALU.add))
        r.barrier()

        with contextlib.ExitStack() as st_p0:
            pT = st_p0.enter_context(nc.psum_tensor("pT", [128, 1024], BF))
            pTb = st_p0.enter_context(nc.psum_tensor("pTb", [128, 1024], BF))
            pT2 = [pT, pTb]
            brows = st_p0.enter_context(nc.sbuf_tensor("brows", [8, 5 * 128], F32))
            qhT = st_p0.enter_context(nc.sbuf_tensor("qhT", [128, 1024], F32))
            khT = st_p0.enter_context(nc.sbuf_tensor("khT", [128, 1024], F32))
            vhT = st_p0.enter_context(nc.sbuf_tensor("vhT", [128, 1024], F32))
            prev_consume = [0, 0]
            prev_pt_dv = [0, 0]
            xnrows = None
            for t in range(32):
                par = t % 2
                ca = col_a[:, par:par + 1]
                cb = col_b[:, par:par + 1]
                cc_ = col_c[:, par:par + 1]
                xr_ = [xnrow, xnrow2][par]
                sq_ = sq_dump[:, par * 512:par * 512 + 512]
                r.wait("sync", "sAC", prev_consume[par])
                nld = r.emit("sync", lambda e, t=t, p=par: e.dma_start(
                    xtile[p][:, :], x_in[t * 128:(t + 1) * 128, :]))
                r.wait("scalar", "sSY", nld)
                r.emit("scalar", lambda e, p=par, ca=ca: e.activation(
                    sq_dump[:, :], xtile[p][:, :], AF.Square, accum_out=ca))
                nac = r.cnt["sAC"]
                r.wait("vector", "sAC", nac)
                r.emit("vector", lambda e, p=par, cb=cb: e.tensor_reduce(
                    cb, xtile[p][:, :], axis=AX.X, op=ALU.add))
                r.emit("vector", lambda e, cb=cb: e.tensor_scalar(
                    cb, cb, 1.0 / C, None, ALU.mult))
                r.emit("vector", lambda e, ca=ca: e.tensor_scalar(
                    ca, ca, 1.0 / C, None, ALU.mult))
                r.emit("vector", lambda e, cb=cb, cc_=cc_: e.tensor_tensor(
                    cc_, cb, cb, ALU.mult))
                r.emit("vector", lambda e, ca=ca, cc_=cc_: e.tensor_tensor(
                    ca, ca, cc_, ALU.subtract))
                ndv = r.emit("vector", lambda e, ca=ca: e.tensor_scalar(
                    ca, ca, 1e-5, None, ALU.add))
                r.wait("scalar", "sDV", ndv)
                nac2 = r.emit("scalar", lambda e, ca=ca: e.activation(
                    ca, ca, AF.Sqrt))
                r.wait("vector", "sAC", nac2)
                r.emit("vector", lambda e, ca=ca, cc_=cc_: e.reciprocal(cc_, ca))
                r.emit("vector", lambda e, cb=cb, cc_=cc_: e.tensor_tensor(
                    cb, cb, cc_, ALU.mult))
                ndv2 = r.emit("vector", lambda e, cb=cb: e.tensor_scalar(
                    cb, cb, -1.0, None, ALU.mult))
                r.wait("scalar", "sDV", ndv2)
                r.wait("scalar", "sPE", prev_pt_dv[par] and 0 or 0)
                nac3 = r.emit("scalar", lambda e, p=par, xr_=xr_, cb=cb, cc_=cc_:
                              e.activation(xr_[:, :], xtile[p][:, :], AF.Identity,
                                           bias=cb, scale=cc_))
                prev_consume[par] = nac3
                r.wait("tensor", "sAC", nac3)
                r.wait("tensor", "sDV", prev_pt_dv[par])
                for c in range(8):
                    r.emit("tensor", lambda e, c=c, xr_=xr_, p=par: e.transpose(
                        pT2[p][:, c * 128:(c + 1) * 128],
                        xr_[:, c * 128:(c + 1) * 128], id_bf[:, :]))
                npe_t = r.cnt["sPE"]
                r.wait("vector", "sPE", npe_t)
                prev_pt_dv[par] = r.emit("vector", lambda e, t=t, p=par: e.tensor_copy(
                    xnT.rearrange("p (c t) -> p c t", c=8)[:, :, t * 128:(t + 1) * 128],
                    pT2[p].ap().rearrange("p (c x) -> p c x", c=8)))
                r.wait("scalar", "sPE", npe_t)
            r.barrier()

            r.mark("P0b")
            for c in range(8):
                r.emit("vector", lambda e, c=c: e.tensor_reduce(
                    sq_dump[:, c * 128:(c + 1) * 128],
                    xnT[:, c * T:(c + 1) * T].rearrange("p (n b) -> p n b", b=BLK),
                    axis=AX.X, op=ALU.add))
            ndvh = r.emit("vector", lambda e: e.tensor_scalar(
                xnh[:, :], sq_dump[:, :], 1.0 / BLK, None, ALU.mult))

            wbuf2 = [st_p0.enter_context(nc.sbuf_tensor(f"wbuf{j}", [128, C], BF))
                     for j in range(2)]
            wTsb2 = st_p0.enter_context(nc.sbuf_tensor("wTsb2", [128, C], BF))
            wp = {"nld": {}, "npe": {}, "nst": 0, "wc": [0, 0], "g": 0}

            def wpd_load(tag, d):
                g = wp["g"]
                wp["g"] = g + 1
                r.wait("gpsimd", "sPE", wp["npe"].get(g - 2, 0))
                wp["nld"][(tag, d)] = r.emit(
                    "gpsimd", lambda e, tag=tag, d=d, g=g: e.dma_start(
                        wbuf2[g % 2][:, :], w_in[tag][d * 128:(d + 1) * 128, :]))
                wp["nld"][(tag, d)] = (wp["nld"][(tag, d)], g)

            def wpd_proc(tag, d):
                nld, g = wp["nld"][(tag, d)]
                r.wait("tensor", "sGP", nld)
                r.wait("tensor", "sAC", r.cnt["sAC"])
                r.wait("tensor", "sDV", r.cnt["sDV"])
                wpt = pT2[g % 2]
                for cc in range(8):
                    r.emit("tensor", lambda e, cc=cc, wpt=wpt, g=g: e.transpose(
                        wpt[:, cc * 128:(cc + 1) * 128],
                        wbuf2[g % 2][:, cc * 128:(cc + 1) * 128],
                        id_bf[:, :]))
                npe_w = r.cnt["sPE"]
                wp["npe"][g] = npe_w
                r.wait("vector", "sPE", npe_w)
                r.wait("vector", "sSY", wp["nst"])
                nwc = r.emit("vector", lambda e, wpt=wpt: e.tensor_copy(
                    wTsb2[:, :], wpt[:, :]))
                r.wait("sync", "sDV", nwc)
                wp["nst"] = r.emit("sync", lambda e, tag=tag, d=d: e.dma_start(
                    wT_sc[tag][d].rearrange("(cc p) j -> p cc j", p=128),
                    wTsb2.rearrange("p (cc x) -> p cc x", cc=8)))
                if tag in ("q", "k", "v"):
                    dstT = {"q": qhT, "k": khT, "v": vhT}[tag]
                    r.wait("tensor", "sDV", max(ndvh, nwc))
                    for cc in range(8):
                        r.emit("tensor", lambda e, cc=cc: e.matmul(
                            pA[:, 0:128],
                            wTsb2[:, cc * 128:(cc + 1) * 128],
                            xnh[:, cc * 128:(cc + 1) * 128],
                            start=(cc == 0), stop=(cc == 7)))
                    npe_m = r.cnt["sPE"]
                    r.wait("vector", "sPE", npe_m)
                    r.emit("vector", lambda e, d=d, dstT=dstT, tag=tag:
                           e.tensor_scalar(
                               dstT[:, d * 128:(d + 1) * 128], pA[:, 0:128],
                               bcol[tag][:, d:d + 1], None, ALU.add))
                r.wait("tensor", "sDV", r.cnt["sDV"])

            for tag in ("r", "k2", "v2"):
                wpd_load(tag, 0)
                wpd_load(tag, 1)
                for d in range(8):
                    wpd_proc(tag, d)
                    if d + 2 < 8:
                        wpd_load(tag, d + 2)
            r.mark("P0c")
            nb128 = st_p0.enter_context(nc.sbuf_tensor("nb128", [128, 1024], F32))
            bis_lo = st_p0.enter_context(nc.sbuf_tensor("bis_lo", [128, 32], F32))
            bis_hi = st_p0.enter_context(nc.sbuf_tensor("bis_hi", [128, 32], F32))
            bis_mid = st_p0.enter_context(nc.sbuf_tensor("bis_mid", [128, 32], F32))
            bis_ge = st_p0.enter_context(nc.sbuf_tensor("bis_ge", [128, 1024], BF))
            bis_cnt = st_p0.enter_context(nc.sbuf_tensor("bis_cnt", [128, 32], F32))
            bis_t1 = st_p0.enter_context(nc.sbuf_tensor("bis_t1", [128, 32], F32))
            bis_t2 = st_p0.enter_context(nc.sbuf_tensor("bis_t2", [128, 32], F32))

            def bh_idx(p, hip, s):
                return (2 * p + hip) * 2 + s

            def emit_low_bh(p, hip, sx):
                b = bh_idx(p, hip, sx)
                p0 = 64 * hip
                cl = p * 128 + sx * 64
                r.wait("tensor", "sAC", r.cnt["sAC"])
                r.wait("tensor", "sDV", r.cnt["sDV"])
                r.emit("tensor", lambda e, p0=p0, cl=cl: e.matmul(
                    pA[0:64, 0:64], qhT[p0:p0 + 64, cl:cl + 64],
                    khT[p0:p0 + 64, cl:cl + 64], start=True, stop=True))
                npe_l = r.emit("tensor", lambda e, p0=p0, cl=cl: e.matmul(
                    pA[0:64, 64:128], khT[p0:p0 + 64, cl:cl + 64],
                    qhT[p0:p0 + 64, cl:cl + 64], start=True, stop=True))
                r.wait("vector", "sPE", npe_l)
                r.emit("vector", lambda e: e.tensor_reduce(
                    col_a[0:64, 0:1], pA[0:64, 0:64], axis=AX.X, op=ALU.max))
                r.emit("vector", lambda e: e.tensor_scalar(
                    small_f[:, :], pA[0:64, 0:64], col_a[0:64, 0:1], 0.125,
                    ALU.subtract, ALU.mult))
                r.emit("vector", lambda e, b=b: e.tensor_tensor(
                    norm_all[0:64, b * 64:(b + 1) * 64], small_f[:, :],
                    band[:, :], ALU.add))
                r.emit("vector", lambda e, b=b: e.tensor_copy(
                    lowT_all[:, b * 64:(b + 1) * 64], pA[0:64, 64:128]))
                ndv_l = r.emit("vector", lambda e, b=b: e.tensor_scalar(
                    rm_all[:, b:b + 1], col_a[0:64, 0:1], 0.125, None, ALU.mult))
                r.wait("tensor", "sDV", ndv_l)

            def emit_vhp_bh(p, hip, sx):
                b = bh_idx(p, hip, sx)
                p0 = 64 * hip
                cl = p * 128 + sx * 64
                r.wait("tensor", "sAC", r.cnt["sAC"])
                r.wait("tensor", "sDV", r.cnt["sDV"])
                idsl = id_f[0:64, 0:64] if hip == 0 else id_f[64:128, 64:128]
                npe_v = r.emit("tensor", lambda e, p0=p0, cl=cl, idsl=idsl:
                               e.transpose(pA[0:64, 0:64],
                                           vhT[p0:p0 + 64, cl:cl + 64], idsl))
                r.wait("vector", "sPE", npe_v)
                r.emit("vector", lambda e, b=b: e.tensor_copy(
                    vhp_all[:, b * 65:b * 65 + 64], pA[0:64, 0:64]))
                ndv_v = r.emit("vector", lambda e, b=b: e.memset(
                    vhp_all[:, b * 65 + 64:b * 65 + 65], 1.0))
                r.wait("tensor", "sDV", ndv_v)

            def emit_nb128():
                ndv_n = r.cnt["sDV"]
                r.wait("gpsimd", "sDV", ndv_n)
                for b in range(NBH):
                    r.emit("gpsimd", lambda e, b=b: e.dma_start(
                        nb128[0:64, b * 32:(b + 1) * 32],
                        norm_all[0:64, b * 64:b * 64 + 32]))
                    r.emit("gpsimd", lambda e, b=b: e.dma_start(
                        nb128[64:128, b * 32:(b + 1) * 32],
                        norm_all[0:64, b * 64 + 32:b * 64 + 64]))
                ngp_n = r.cnt["sGP"]
                r.emit("vector", lambda e: e.memset(bis_lo[:, :], -100000.0))
                r.emit("vector", lambda e: e.memset(bis_hi[:, :], 100000.0))
                r.wait("vector", "sGP", ngp_n)

            pB1 = st_p0.enter_context(nc.psum_tensor("pB1", [128, 32], F32))
            pB1s = st_p0.enter_context(nc.psum_tensor("pB1s", [64, 64], F32))

            def emit_bisect_iter(it):
                r.emit("vector", lambda e: e.tensor_tensor(
                    bis_mid[:, :], bis_lo[:, :], bis_hi[:, :], ALU.add))
                r.emit("vector", lambda e: e.tensor_scalar(
                    bis_mid[:, :], bis_mid[:, :], 0.5, None, ALU.mult))
                r.emit("vector", lambda e: e.tensor_tensor(
                    bis_ge[:, :], nb128[:, :],
                    bis_mid.ap().rearrange("p (h o) -> p h o", o=1)
                    .broadcast_to([128, 32, 32]), ALU.is_ge))
                ndv_c = r.emit("vector", lambda e: e.tensor_reduce(
                    bis_cnt[:, :], bis_ge.ap().rearrange("p (h e) -> p h e", e=32),
                    axis=AX.X, op=ALU.add))
                r.wait("tensor", "sDV", ndv_c)
                npe_c = r.emit("tensor", lambda e: e.matmul(
                    pB1[:, :], ones128[:, :], bis_cnt[:, :], start=True, stop=True))
                r.wait("vector", "sPE", npe_c)
                r.emit("vector", lambda e: e.tensor_scalar(
                    bis_t1[:, :], pB1[:, :], float(NUM_BLOCK), None, ALU.is_ge))
                r.emit("vector", lambda e: e.tensor_scalar(
                    bis_t2[:, :], pB1[:, :], float(NUM_BLOCK), None, ALU.is_lt))
                r.emit("vector", lambda e: e.tensor_tensor(
                    bis_cnt[:, :], bis_t1[:, :], bis_mid[:, :], ALU.mult))
                r.emit("vector", lambda e: e.tensor_tensor(
                    bis_lo[:, :], bis_t2[:, :], bis_lo[:, :], ALU.mult))
                r.emit("vector", lambda e: e.tensor_tensor(
                    bis_lo[:, :], bis_lo[:, :], bis_cnt[:, :], ALU.add))
                r.emit("vector", lambda e: e.tensor_tensor(
                    bis_cnt[:, :], bis_t2[:, :], bis_mid[:, :], ALU.mult))
                r.emit("vector", lambda e: e.tensor_tensor(
                    bis_hi[:, :], bis_t1[:, :], bis_hi[:, :], ALU.mult))
                ndv_u = r.emit("vector", lambda e: e.tensor_tensor(
                    bis_hi[:, :], bis_hi[:, :], bis_cnt[:, :], ALU.add))
                r.wait("tensor", "sDV", ndv_u)

            def emit_sel_thr():
                r.emit("vector", lambda e: e.tensor_copy(thr[:, :], bis_lo[:, :]))

            def emit_sel_b(b):
                r.emit("vector", lambda e, b=b: e.tensor_scalar(
                    sel_buf[:, :], norm_all[0:64, b * 64:(b + 1) * 64],
                    thr[0:64, b:b + 1], None, ALU.is_ge))
                ndv_s = r.emit("vector", lambda e, b=b: e.tensor_scalar(
                    bsm_all[:, b * 64:(b + 1) * 64], sel_buf[:, :], 1.0, -BIGNEG,
                    ALU.subtract, ALU.mult))
                r.wait("tensor", "sDV", ndv_s)
                r.wait("tensor", "sAC", r.cnt["sAC"])
                npe_s = r.emit("tensor", lambda e: e.transpose(
                    pB1s[0:64, 0:64], sel_buf[:, :], id_f[0:64, 0:64]))
                r.wait("vector", "sPE", npe_s)
                ndv_s2 = r.emit("vector", lambda e, b=b: e.tensor_copy(
                    selT_all[:, b * 64:(b + 1) * 64], pB1s[0:64, 0:64]))
                r.wait("tensor", "sDV", ndv_s2)

            def emit_sel_post():
                r.wait("sync", "sDV", r.cnt["sDV"])
                nT1 = r.emit("sync", lambda e: e.dma_start(selT_dram, selT_all[:, :]))
                r.wait("sync", "sSY", nT1)
                r.emit("sync", lambda e: e.dma_start(norm_all[64:128, :], selT_dram))
                nraw = r.emit("sync", lambda e: e.dma_start(
                    bsm_raw, bsm_all.ap().rearrange("q (b k) -> q b k", b=NBH)))
                nraw2 = r.emit("sync", lambda e: e.dma_start(rm_raw, rm_all[:, :]))
                r.wait("sync", "sSY", nraw2)
                for j in range(4):
                    r.emit("sync", lambda e, j=j: e.dma_start(
                        bsm_dram[:, j],
                        bsm_raw.rearrange("(t j) b k -> j b t k", j=4)[j]))
                    r.emit("sync", lambda e, j=j: e.dma_start(
                        rm_dram[:, j],
                        rm_raw.rearrange("(t j) b -> j b t", j=4)[j],
                        allow_slow_non_contiguous=True))

            r.mark("P05")
            fA = st_p0.enter_context(nc.sbuf_tensor("fA", [128, 8 * QT], BF))
            fB = st_p0.enter_context(nc.sbuf_tensor("fB", [128, 8 * QT], BF))
            fR = st_p0.enter_context(nc.sbuf_tensor("fR", [128, 8 * QT], BF))
            fF = st_p0.enter_context(nc.sbuf_tensor("fF", [128, 8 * QT], BF))
            fW = [st_p0.enter_context(nc.sbuf_tensor(f"fW{j}", [128, 8 * 128], BF))
                  for j in range(3)]
            ptile = st_p0.enter_context(nc.sbuf_tensor("ptile", [128, C], BF))
            ptileb = st_p0.enter_context(nc.sbuf_tensor("ptileb", [128, C], BF))
            ptile2 = [ptile, ptileb]

            def mu_mix(dst, mu_key, t0):
                for c in range(8):
                    base = c * T + t0
                    if t0 % S == 0:
                        r.emit("vector", lambda e, d=dst, c=c: e.memset(
                            d[:, c * QT:c * QT + 1], 0.0))
                        r.emit("vector", lambda e, d=dst, c=c, base=base, m=mu_key:
                               e.tensor_scalar(
                                   d[:, c * QT + 1:(c + 1) * QT],
                                   xnT[:, base:base + QT - 1],
                                   mucol1m[m][:, c:c + 1], None, ALU.mult))
                    else:
                        r.emit("vector", lambda e, d=dst, c=c, base=base, m=mu_key:
                               e.tensor_scalar(
                                   d[:, c * QT:(c + 1) * QT],
                                   xnT[:, base - 1:base + QT - 1],
                                   mucol1m[m][:, c:c + 1], None, ALU.mult))
                    r.emit("vector", lambda e, d=dst, c=c, base=base, m=mu_key:
                           e.scalar_tensor_tensor(
                               d[:, c * QT:(c + 1) * QT],
                               xnT[:, base:base + QT],
                               mucol[m][:, c:c + 1],
                               d[:, c * QT:(c + 1) * QT],
                               ALU.mult, ALU.add))

            def load_wslice(buf, tag, d):
                return r.emit("sync", lambda e, buf=buf, tag=tag, d=d: e.dma_start(
                    buf.rearrange("p (cc j) -> p cc j", cc=8),
                    wT_sc[tag][d].rearrange("(cc p) j -> p cc j", p=128)))

            def mm_quarter(tag, rhs_buf, d, wbuf_par, slot):
                for cc in range(8):
                    r.emit("tensor", lambda e, cc=cc, w=wbuf_par, rb=rhs_buf, sl=slot:
                           e.matmul(pA[:, sl * 512:sl * 512 + QT],
                                    w[:, cc * 128:(cc + 1) * 128],
                                    rb[:, cc * QT:(cc + 1) * QT],
                                    start=(cc == 0), stop=(cc == 7)))
                return r.cnt["sPE"]

            wsched = [(tag, d) for _Q in range(8)
                      for tag in ("r", "k2", "v2") for d in range(8)]
            wst = {"ld": {}, "mm": {}}

            def wload(i):
                if i >= len(wsched):
                    return
                if i - 3 >= 0:
                    r.wait("sync", "sPE", wst["mm"][i - 3])
                tag, d = wsched[i]
                wst["ld"][i] = load_wslice(fW[i % 3], tag, d)

            for i in range(3):
                wload(i)

            bh_list = [(p, hip, sx) for p in range(8) for hip in range(2)
                       for sx in range(B_CORE)]
            site_work = {}

            def addw(si, fn):
                site_work.setdefault(si, []).append(fn)

            for j, tag in enumerate(("q", "k", "v")):
                base = 2 + j * 2
                for ch in range(8):
                    d = ch
                    addw(base + ch // 4, lambda tag=tag, d=d: wpd_proc(tag, d))
                    if d + 2 < 8:
                        addw(base + ch // 4,
                             lambda tag=tag, d=d: wpd_load(tag, d + 2))
                nxt = {"q": "k", "k": "v", "v": None}[tag]
                if nxt is not None:
                    addw(base + 1, lambda nxt=nxt: wpd_load(nxt, 0))
                    addw(base + 1, lambda nxt=nxt: wpd_load(nxt, 1))
            for j, si in enumerate((6, 7, 8, 9)):
                for bb in bh_list[j * 8:(j + 1) * 8]:
                    addw(si, lambda bb=bb: emit_low_bh(*bb))
            for bb in bh_list:
                addw(10, lambda bb=bb: emit_vhp_bh(*bb))
            addw(10, emit_nb128)
            for it in range(N_BISECT):
                addw(11 + it // 8, lambda it=it: emit_bisect_iter(it))
            addw(18, emit_sel_thr)
            for b2 in range(NBH):
                addw(18 + (2 + b2) // 6, lambda b2=b2: emit_sel_b(b2))

            def do_site(si):
                for fn in site_work.get(si, ()):
                    fn()

            wpd_load("q", 0)
            wpd_load("q", 1)
            prev_dma = [0, 0]
            prev_add = [0, 0]
            r.wait("vector", "sPE", r.cnt["sPE"])
            for Q in range(8):
                t0 = Q * QT
                r.wait("vector", "sPE", r.cnt["sPE"])
                mu_mix(fA, "r", t0)
                ndv_mixr = r.cnt["sDV"]
                cons = [r.cnt["sAC"], r.cnt["sAC"]]
                r.wait("tensor", "sDV", r.cnt["sDV"])
                for d in range(8):
                    par = d % 2
                    gi = Q * 24 + d
                    r.wait("tensor", "sSY", wst["ld"][gi])
                    if d == 0:
                        r.wait("tensor", "sDV", ndv_mixr)
                    r.wait("tensor", "sAC", cons[par])
                    npe_q = mm_quarter("r", fA, d, fW[gi % 3], par)
                    wst["mm"][gi] = npe_q
                    wload(gi + 3)
                    r.wait("scalar", "sPE", npe_q)
                    cons[par] = r.emit("scalar", lambda e, d=d, par=par: e.activation(
                        fR[:, d * QT:(d + 1) * QT], pA[:, par * 512:par * 512 + QT],
                        AF.Sigmoid))
                do_site(Q * 3 + 0)
                r.wait("vector", "sPE", r.cnt["sPE"])
                mu_mix(fA, "k", t0)
                ndv_mixk = r.cnt["sDV"]
                consd = [r.cnt["sDV"], r.cnt["sDV"]]
                r.wait("tensor", "sAC", r.cnt["sAC"])
                for d in range(8):
                    par = d % 2
                    gi = Q * 24 + 8 + d
                    r.wait("tensor", "sSY", wst["ld"][gi])
                    if d == 0:
                        r.wait("tensor", "sDV", ndv_mixk)
                    r.wait("tensor", "sDV", consd[par])
                    npe_q = mm_quarter("k2", fA, d, fW[gi % 3], par)
                    wst["mm"][gi] = npe_q
                    wload(gi + 3)
                    r.wait("vector", "sPE", npe_q)
                    consd[par] = r.emit("vector", lambda e, d=d, par=par: e.tensor_scalar(
                        fB[:, d * QT:(d + 1) * QT], pA[:, par * 512:par * 512 + QT],
                        0.0, None, ALU.max))
                do_site(Q * 3 + 1)
                r.wait("scalar", "sDV", r.cnt["sDV"])
                r.wait("scalar", "sPE", r.cnt["sPE"])
                nac_sq = None
                for d in range(8):
                    nac_sq = r.emit("scalar", lambda e, d=d: e.activation(
                        fA[:, d * QT:(d + 1) * QT], fB[:, d * QT:(d + 1) * QT],
                        AF.Square))
                r.wait("tensor", "sAC", nac_sq)
                consd = [r.cnt["sDV"], r.cnt["sDV"]]
                r.wait("tensor", "sAC", r.cnt["sAC"])
                for d in range(8):
                    par = d % 2
                    gi = Q * 24 + 16 + d
                    r.wait("tensor", "sSY", wst["ld"][gi])
                    r.wait("tensor", "sDV", consd[par])
                    npe_q = mm_quarter("v2", fA, d, fW[gi % 3], par)
                    wst["mm"][gi] = npe_q
                    wload(gi + 3)
                    r.wait("vector", "sPE", npe_q)
                    consd[par] = r.emit("vector", lambda e, d=d, par=par: e.tensor_tensor(
                        fF[:, d * QT:(d + 1) * QT], pA[:, par * 512:par * 512 + QT],
                        fR[:, d * QT:(d + 1) * QT], ALU.mult))
                r.wait("tensor", "sDV", r.cnt["sDV"])
                for tt in range(QT // 128):
                    par = tt % 2
                    wpt = pT2[par]
                    r.wait("tensor", "sDV", prev_add[par])
                    for d in range(8):
                        r.emit("tensor", lambda e, d=d, tt=tt, wpt=wpt: e.transpose(
                            wpt[:, d * 128:(d + 1) * 128],
                            fF[:, d * QT + tt * 128:d * QT + (tt + 1) * 128],
                            id_bf[:, :]))
                    npe_tt = r.cnt["sPE"]
                    r.wait("vector", "sPE", npe_tt)
                    r.wait("vector", "sGP", prev_dma[par])
                    ndv_tt = r.emit("vector", lambda e, p=par, wpt=wpt:
                                    e.tensor_copy(ptile2[p][:, :], wpt[:, :]))
                    prev_add[par] = ndv_tt
                    r.wait("gpsimd", "sDV", ndv_tt)
                    prev_dma[par] = r.emit("gpsimd", lambda e, t0=t0, tt=tt, p=par:
                                           e.dma_start(part_sc[t0 + tt * 128:
                                                               t0 + (tt + 1) * 128, :],
                                                       ptile2[p][:, :]))
                do_site(Q * 3 + 2)
            emit_sel_post()
            r.barrier()
            if debug:
                r.emit("gpsimd", lambda e: e.dma_start(dbg["d_norm"], norm_all[0:64, 0:128]))
                r.emit("gpsimd", lambda e: e.dma_start(dbg["d_thr"], thr[:, :]))
                r.emit("gpsimd", lambda e: e.dma_start(dbg["d_xnT"], xnT[:, 0:512]))
                r.emit("gpsimd", lambda e: e.dma_start(dbg["d_qhT"], qhT[:, :]))
                r.barrier()

        r.mark("P1")
        with contextlib.ExitStack() as st_p2:
            def ps2(name, shape, dt=F32):
                return st_p2.enter_context(nc.psum_tensor(name, shape, dt))
            pS = [ps2("pS0", [128, 1024]), ps2("pS1", [128, 1024])]
            pO = ps2("pO", [128, 1024])

            qtp = sb("qtp", [128, T], BF)
            ktp = sb("ktp", [128, T], BF)
            vtm = sb("vtm", [128, 32 * 130], BF)
            wsl = {w: sb(f"wsl_{w}", [128, 8 * 128], BF) for w in ("q", "k", "v")}
            lhsc = sb("lhsc", [128, 2048], BF)
            rhsc = sb("rhsc", [128, 2048], BF)
            aT = [sb("aT0", [128, 1024], BF), sb("aT1", [128, 1024], BF)]
            laT = sb("laT", [64, 64], BF)
            la_t1 = sb("la_t1", [64, 64], F32)
            la_t2 = sb("la_t2", [64, 64], F32)
            bm = sb("bm", [128, 64], F32)
            bm2c = [bm, sb("bm1", [128, 64], F32)]
            bmm = sb("bmm", [128, 64], F32)
            bx2 = [sb("bx0", [128, 1024], BF), sb("bx1", [128, 1024], BF)]
            rmb2 = [sb("rmb0", [128, 16], F32), sb("rmb1", [128, 16], F32)]
            epst = sb("epst", [128, 16], F32)
            outT = sb("outT", [65, 2048], F32)
            stage = sb("stage", [128, 1024], F32)
            stage2 = [stage, sb("stage1", [128, 1024], F32)]
            rden = sb("rden", [128, 1], F32)
            rdenw16 = sb("rdenw16", [128, 16], F32)
            vTf = sb("vTf", [128, 2048], BF)
            fslice = [vTf[:, 0:1024], vTf[:, 1024:2048]]
            dcol = sb("dcol", [128, 1], F32)

            r.emit("vector", lambda e: e.memset(
                vtm.ap().rearrange("p (k h o) -> p k h o", k=32, h=2)[:, :, :, 64:65], 1.0))
            ndv_pre = r.emit("gpsimd", lambda e: e.dma_start(lhsc[64:128, :], eall[:, :]))
            r.barrier()

            def bh_idx(p, hip, s):
                return (2 * p + hip) * 2 + s

            rhs2 = [rhsc, st_p2.enter_context(nc.sbuf_tensor("rhsc1", [128, 2048], BF))]
            lhs2 = [lhsc, st_p2.enter_context(nc.sbuf_tensor("lhsc1", [128, 2048], BF))]
            laT2 = [laT, st_p2.enter_context(nc.sbuf_tensor("laT1", [64, 64], BF))]
            r.emit("gpsimd", lambda e: e.dma_start(lhs2[1][64:128, :], eall[:, :]))
            r.barrier()

            state = {"ndv_ep": 0, "prev_evac": 0, "chunk_pe_end": 0,
                     "nac_la": [0, 0], "ngp_prep": [0, 0], "ndv_prep": [0, 0],
                     "ngp_bx": [0, 0], "nac_eps": 0, "ngc_mv": [0, 0],
                     "nac_la_last": 0, "ngp_prep_last": 0,
                     "prev_exp": [0, 0], "phA_dv": 0, "phA_done_dv": [0, 0], "pO_dv": 0,
                     "nst_a": 0, "ngp_rmb": [0, 0], "prev_exp2": [0, 0], "phA_dv4": [0, 0, 0, 0], "nac_ep": 0,
                     "pcons": [0, 0], "nac_qk": 0, "nsy_part": [0, 0], "ngp_add": [0, 0],
                     "nst_a2": [0, 0], "pA_free_ac": [0, 0], "pending_w1": None, "npe_vtr": 0,
                     "nld_f": [0, 0]}

            def emit_prep_bx(p, hip, sq):
                b = bh_idx(p, hip, sq)
                bpar = b % 2
                r.wait("gpsimd", "sDV", state["phA_done_dv"][bpar])
                state["ngp_bx"][bpar] = r.emit(
                    "gpsimd", lambda e, b=b, bpar=bpar: e.dma_start(
                        bx2[bpar][:, :],
                        bsm_dram[b].rearrange("j t k -> j (t k)")
                        .unsqueeze(1).broadcast_to([4, 32, 1024])))
                state["ngp_rmb"][bpar] = r.emit(
                    "gpsimd", lambda e, b=b, bpar=bpar: e.dma_start(
                        rmb2[bpar][:, :],
                        rm_dram[b].unsqueeze(1).broadcast_to([4, 32, 16])))

            def emit_prep(p, hip, sq, par, prev_chunk_pe, with_bx=False):
                b = bh_idx(p, hip, sq)
                p0 = 64 * hip
                ts0 = sq * S
                r.wait("gpsimd", "sPE", prev_chunk_pe)
                r.emit("gpsimd", lambda e, p0=p0, ts0=ts0, par=par: e.dma_start(
                    rhs2[par][0:64, :], qtp[p0:p0 + 64, ts0:ts0 + S]))
                r.emit("gpsimd", lambda e, p0=p0, ts0=ts0, par=par: e.dma_start(
                    lhs2[par][0:64, :], ktp[p0:p0 + 64, ts0:ts0 + S]))
                ngp_qk = r.cnt["sGP"]
                if with_bx:
                    emit_prep_bx(p, hip, sq)
                r.wait("vector", "sPE", prev_chunk_pe)
                ngp5 = r.emit("vector", lambda e, b=b, par=par: e.tensor_scalar(
                    rhs2[par][64:128, :],
                    norm_all[64:128, b * 64:(b + 1) * 64]
                    .rearrange("p (q o) -> p q o", o=1)
                    .broadcast_to([64, 64, 32]),
                    1.0, -BIGNEG, ALU.subtract, ALU.mult))
                r.wait("vector", "sAC", state["nac_la_last"])
                r.emit("vector", lambda e, b=b: e.tensor_scalar(
                    la_t1[:, :], lowT_all[:, b * 64:(b + 1) * 64],
                    0.125, None, ALU.mult))
                r.emit("vector", lambda e, b=b: e.tensor_scalar(
                    la_t2[:, :], selT_all[:, b * 64:(b + 1) * 64],
                    10000.0, None, ALU.mult))
                ndv_la = r.emit("vector", lambda e: e.tensor_tensor(
                    la_t1[:, :], la_t1[:, :], la_t2[:, :], ALU.subtract))
                r.wait("scalar", "sDV", ndv_la)
                state["nac_la"][par] = r.emit("scalar", lambda e, par=par: e.activation(
                    laT2[par][:, :], la_t1[:, :], AF.Exp, bias=cln32[0:64, :]))
                state["nac_la_last"] = state["nac_la"][par]
                state["ngp_prep"][par] = ngp_qk
                state["ndv_prep"][par] = ngp5

            def emit_phaseA_half(p, hip, sq, u, slots=(0, 1)):
                b = bh_idx(p, hip, sq)
                bpar = b % 2
                bmp = bm2c[bpar]
                p0 = 64 * hip
                ts0 = sq * S
                t, kq = u // 4, u % 4
                sl4 = slots[u % len(slots)]
                pap = [pA[:, 0:512], pA[:, 512:1024],
                       pO[:, 0:512], pO[:, 512:1024]][sl4]
                r.wait("tensor", "sDV", state["phA_dv4"][sl4])
                if sl4 < 2:
                    r.wait("tensor", "sAC", state["pcons"][sl4])
                    r.wait("tensor", "sAC", state["pA_free_ac"][sl4])
                else:
                    r.wait("tensor", "sAC", state["prev_evac"])
                r.wait("tensor", "sDV", state["ndv_ep"])
                if u == 0:
                    r.wait("tensor", "sGP", state["ngp_bx"][bpar])
                    r.wait("tensor", "sAC", state["nac_qk"])
                r.emit("tensor", lambda e, p0=p0, ts0=ts0, t=t, kq=kq, pap=pap:
                       e.matmul(pap,
                                qtp[p0:p0 + 64,
                                    ts0 + t * 128:ts0 + (t + 1) * 128],
                                ktp[p0:p0 + 64,
                                    ts0 + kq * 512:ts0 + (kq + 1) * 512],
                                start=True, stop=True))
                npe_a = r.cnt["sPE"]
                r.wait("vector", "sPE", npe_a)
                if kq == 0:
                    r.wait("vector", "sGC", state["ngc_mv"][bpar])
                nred = r.emit("vector", lambda e, kq=kq, pap=pap, bmp=bmp:
                              e.tensor_reduce(
                                  bmp[:, kq * 16:(kq + 1) * 16],
                                  pap.rearrange("p (k e) -> p k e", e=32),
                                  axis=AX.X, op=ALU.max))
                state["phA_dv4"][sl4] = nred
                if sl4 >= 2:
                    state["pO_dv"] = nred
                if kq == 3:
                    r.emit("vector", lambda e, t=t, bmp=bmp, bpar=bpar:
                           e.tensor_tensor(bmm[:, :], bmp[:, :],
                                           bx2[bpar][:, t * 64:(t + 1) * 64],
                                           ALU.add))
                    state["ngc_mv"][bpar] = 0
                    r.emit("vector", lambda e, b=b, t=t:
                           e.tensor_reduce(
                               mv_all[:, b * 16 + t:b * 16 + t + 1],
                               bmm[:, :], axis=AX.X, op=ALU.max))
                if u == 63:
                    r.wait("vector", "sGP", state["ngp_rmb"][bpar])
                    r.wait("vector", "sGC", state["ngc_mv"][bpar])
                    r.wait("vector", "sAC", state["nac_eps"])
                    r.emit("vector", lambda e, b=b: e.tensor_scalar(
                        epst[:, :], mv_all[:, b * 16:(b + 1) * 16],
                        0.125, None, ALU.mult))
                    ndv_e = r.emit("vector", lambda e, bpar=bpar: e.tensor_tensor(
                        epst[:, :], epst[:, :], rmb2[bpar][:, :], ALU.max))
                    r.wait("scalar", "sDV", ndv_e)
                    state["nac_eps"] = r.emit("scalar", lambda e, b=b: e.activation(
                        eps_all[:, b * 16:(b + 1) * 16], epst[:, :],
                        AF.Exp, bias=clneps[:, :]))
                    state["phA_done_dv"][bpar] = ndv_e

            def emit_av(p, hip, sq, par, kg):
                b = bh_idx(p, hip, sq)
                qh, k = kg // 16, kg % 16
                kpar = kg % 2
                r.wait("tensor", "sAC", state["prev_exp2"][kpar])
                if k == 0:
                    r.wait("tensor", "sAC", state["prev_evac"])
                    r.wait("tensor", "sDV", state["pO_dv"])
                for sl in range(2):
                    r.emit("tensor", lambda e, sq=sq, k=k, sl=sl, bpar=kpar, hip=hip:
                           e.matmul(pO[0:65, sl * 512:(sl + 1) * 512],
                                    vtm[:, (sq * 16 + k) * 130 + hip * 65:
                                        (sq * 16 + k) * 130 + hip * 65 + 65],
                                    aT[bpar][:, sl * 512:(sl + 1) * 512],
                                    start=(k == 0), stop=False))
                if k == 15:
                    r.wait("tensor", "sAC", state["nac_la"][par])
                    for sl in range(2):
                        r.emit("tensor", lambda e, b=b, qh=qh, sl=sl, par=par:
                               e.matmul(pO[0:65, sl * 512:(sl + 1) * 512],
                                        vhp_all[:, b * 65:(b + 1) * 65],
                                        laT2[par][:, qh * 32 + sl * 16:
                                                  qh * 32 + (sl + 1) * 16]
                                        .rearrange("p (q o) -> p q o", o=1)
                                        .broadcast_to([64, 16, 32]),
                                        start=False, stop=True))
                    npe_o = r.cnt["sPE"]
                    r.wait("scalar", "sPE", npe_o)
                    state["prev_evac"] = r.emit("scalar", lambda e, qh=qh:
                                                e.activation(
                                                    outT[:, qh * 1024:(qh + 1) * 1024],
                                                    pO[0:65, :], AF.Copy))

            def emit_chunk_iter(p, hip, sq, par, kg, units):
                qh, k = kg // 16, kg % 16
                kpar = kg % 2
                r.wait("tensor", "sAC", state["prev_exp"][kpar])
                if kg == 0:
                    r.wait("tensor", "sGP", state["ngp_prep"][par])
                    r.wait("tensor", "sDV", state["ndv_prep"][par])
                for sl in range(2):
                    r.emit("tensor", lambda e, k=k, qh=qh, sl=sl, par=par, bpar=kpar:
                           e.matmul(pS[bpar][:, sl * 512:(sl + 1) * 512],
                                    lhs2[par][:, k * 128:(k + 1) * 128],
                                    rhs2[par][:, qh * 1024 + sl * 512:
                                              qh * 1024 + (sl + 1) * 512],
                                    start=True, stop=True))
                npe_s = r.cnt["sPE"]
                if kg > 0:
                    emit_av(p, hip, sq, par, kg - 1)
                if kg == 4 and state["pending_w1"] is not None:
                    state["pending_w1"]()
                    state["pending_w1"] = None
                for bhu, u in units:
                    emit_phaseA_half(*bhu, u)
                r.wait("scalar", "sPE", npe_s)
                newexp = r.emit("scalar", lambda e, bpar=kpar:
                                e.activation(aT[bpar][:, :], pS[bpar][:, :],
                                             AF.Exp, scale=0.125))
                state["prev_exp2"][kpar] = newexp
                state["prev_exp"][kpar] = newexp

            def emit_epilogue(p, hip, sq, par, last):
                b = bh_idx(p, hip, sq)
                stg = stage2[par]
                h = 2 * p + hip
                r.wait("tensor", "sAC", state["prev_evac"])
                if last:
                    r.wait("tensor", "sAC", max(state["prev_exp"][0],
                                                state["prev_exp"][1]))
                    slot_off = [0, 128, 256, 384, 512, 640, 768, 896]
                    for i in range(16):
                        so = slot_off[i % 8]
                        r.emit("tensor", lambda e, i=i, so=so: e.transpose(
                            pS[i // 8][:, so:so + 65],
                            outT[:, i * 128:(i + 1) * 128], id_f[0:65, 0:65]))
                    npe_tr = r.cnt["sPE"]
                    r.wait("vector", "sPE", npe_tr)
                    r.wait("vector", "sAC", state["nac_ep"])
                    for h2 in range(2):
                        r.emit("vector", lambda e, b=b, h2=h2: e.tensor_tensor(
                            rdenw16[:, h2 * 8:(h2 + 1) * 8],
                            pS[h2].ap().rearrange("p (a c) -> p a c", a=8)
                            [:, :, 64:65].rearrange("p a c -> p (a c)"),
                            eps_all[:, b * 16 + h2 * 8:b * 16 + (h2 + 1) * 8],
                            ALU.add))
                    ndv_rd = r.emit("vector", lambda e: e.reciprocal(
                        rdenw16[:, :], rdenw16[:, :]))
                    r.wait("scalar", "sDV", ndv_rd)
                    r.wait("scalar", "sSY", state["nst_a2"][par])
                    nac_ep = 0
                    for i in range(16):
                        so = slot_off[i % 8]
                        nac_ep = r.emit("scalar", lambda e, i=i, so=so, stg=stg:
                                        e.activation(
                                            stg[:, i * 64:(i + 1) * 64],
                                            pS[i // 8][:, so:so + 64], AF.Identity,
                                            scale=rdenw16[:, i:i + 1]))
                    state["ndv_ep"] = ndv_rd
                    state["nac_ep"] = nac_ep
                else:
                    r.wait("tensor", "sDV", max(state["phA_dv4"][0],
                                                state["phA_dv4"][1]))

                    def emit_wave(w, gate0=0, gate1=0, npe_tr=None):
                        acts = []
                        if npe_tr is None:
                            if gate0:
                                r.wait("tensor", "sAC", gate0)
                            for sl8 in range(8):
                                i = w * 8 + sl8
                                if gate1 and sl8 == 4:
                                    r.wait("tensor", "sAC", gate1)
                                r.emit("tensor", lambda e, i=i, sl8=sl8:
                                       e.transpose(
                                           pA[:, sl8 * 128:sl8 * 128 + 65],
                                           outT[:, i * 128:(i + 1) * 128],
                                           id_f[0:65, 0:65]))
                            npe_tr = r.cnt["sPE"]
                        r.wait("vector", "sPE", npe_tr)
                        if w == 0:
                            r.wait("vector", "sAC", state["nac_ep"])
                        r.emit("vector", lambda e, b=b, w=w: e.tensor_tensor(
                            rdenw16[:, w * 8:(w + 1) * 8],
                            pA.ap().rearrange("p (a c) -> p a c", a=8)
                            [:, :, 64:65].rearrange("p a c -> p (a c)"),
                            eps_all[:, b * 16 + w * 8:b * 16 + (w + 1) * 8],
                            ALU.add))
                        ndv_rd = r.emit("vector", lambda e, w=w: e.reciprocal(
                            rdenw16[:, w * 8:(w + 1) * 8],
                            rdenw16[:, w * 8:(w + 1) * 8]))
                        r.wait("scalar", "sDV", ndv_rd)
                        if w == 0:
                            r.wait("scalar", "sSY", state["nst_a2"][par])
                        for sl8 in range(8):
                            i = w * 8 + sl8
                            acts.append(r.emit(
                                "scalar", lambda e, i=i, sl8=sl8, stg=stg:
                                e.activation(stg[:, i * 64:(i + 1) * 64],
                                             pA[:, sl8 * 128:sl8 * 128 + 64],
                                             AF.Identity,
                                             scale=rdenw16[:, i:i + 1])))
                        return ndv_rd, acts

                    for sl8 in range(8):
                        r.emit("tensor", lambda e, sl8=sl8: e.transpose(
                            pA[:, sl8 * 128:sl8 * 128 + 65],
                            outT[:, sl8 * 128:(sl8 + 1) * 128], id_f[0:65, 0:65]))
                    npe_tr0 = r.cnt["sPE"]

                    def w1fn(b=b, par=par, stg=stg, h=h, sq=sq, npe_tr0=npe_tr0):
                        _, acts0 = emit_wave(0, npe_tr=npe_tr0)
                        ndv_rd, acts1 = emit_wave(1, gate0=acts0[3],
                                                  gate1=acts0[7])
                        state["ndv_ep"] = ndv_rd
                        state["nac_ep"] = acts1[7]
                        state["pA_free_ac"] = [acts1[3], acts1[7]]
                        finish(b, par, stg, h, sq, acts1[7])

                    state["pending_w1"] = w1fn
                    return

                finish(b, par, stg, h, sq, state["nac_ep"])

            def emit_xf_load(p, hip, sq, par):
                h = 2 * p + hip
                r.wait("sync", "sGC", state["ngp_add"][par])
                r.emit("sync", lambda e, h=h, sq=sq, par=par: e.dma_start(
                    xtile[par].ap().rearrange("p (i d) -> p i d", i=16),
                    x_in.rearrange("(s i pp) c -> s pp i c", i=16, pp=128)
                    [sq][:, :, h * 64:(h + 1) * 64]))
                r.wait("sync", "sPE", state["npe_vtr"])
                state["nld_f"][par] = r.emit(
                    "sync", lambda e, h=h, sq=sq, par=par: e.dma_start(
                        fslice[par].rearrange("p (i d) -> p i d", i=16),
                        part_sc.rearrange("(s i pp) c -> s pp i c", i=16, pp=128)
                        [sq][:, :, h * 64:(h + 1) * 64]))

            def finish(b, par, stg, h, sq, nac_ep):
                r.wait("gpsimdc", "sAC", nac_ep)
                r.wait("gpsimdc", "sSY", state["nld_f"][par])
                r.emit("gpsimdc", lambda e, par=par, stg=stg: e.tensor_tensor(
                    stg[:, :], stg[:, :], xtile[par][:, :], ALU.add))
                state["ngp_add"][par] = r.emit(
                    "gpsimdc", lambda e, par=par, stg=stg: e.tensor_tensor(
                        stg[:, :], stg[:, :], fslice[par], ALU.add))
                r.wait("sync", "sGC", state["ngp_add"][par])
                state["nst_a2"][par] = r.emit(
                    "sync", lambda e, h=h, sq=sq, stg=stg: e.dma_start(
                        out.rearrange("(s i pp) c -> s pp i c", i=16, pp=128)
                        [sq][:, :, h * 64:(h + 1) * 64],
                        stg.ap().rearrange("p (i d) -> p i d", i=16)))

            def emit_projqk_group(p, tag, n):
                pcons = state["pcons"]
                dst = qtp if tag == "q" else ktp
                slot = n % 2
                r.wait("tensor", "sAC",
                       max(pcons[slot], state["pA_free_ac"][slot]))
                r.wait("tensor", "sDV",
                       max(state["phA_dv4"][slot], state["ndv_ep"]))
                for cc in range(8):
                    r.emit("tensor", lambda e, tag=tag, n=n, cc=cc, slot=slot:
                           e.matmul(pA[:, slot * 512:(slot + 1) * 512],
                                    wsl[tag][:, cc * 128:(cc + 1) * 128],
                                    xnT[:, cc * T + n * 512:cc * T + (n + 1) * 512],
                                    start=(cc == 0), stop=(cc == 7)))
                npe_p = r.cnt["sPE"]
                r.wait("scalar", "sPE", npe_p)
                pcons[slot] = r.emit("scalar", lambda e, tag=tag, n=n, slot=slot,
                                     dst=dst, p=p: e.activation(
                                         dst[:, n * 512:(n + 1) * 512],
                                         pA[:, slot * 512:(slot + 1) * 512],
                                         AF.Identity, bias=bcol[tag][:, p:p + 1]))

            state["nls_next"] = 0
            for p in range(8):
                if p == 0:
                    for tag in ("q", "k", "v"):
                        r.emit("sync", lambda e, tag=tag: e.dma_start(
                            wsl[tag].rearrange("p (cc j) -> p cc j", cc=8),
                            wT_sc[tag][0].rearrange("(cc p) j -> p cc j", p=128)))
                    state["nls_next"] = r.cnt["sSY"]
                nls = state["nls_next"]
                pcons = state["pcons"]
                if p == 0:
                    r.wait("scalar", "sPE", r.cnt["sPE"])
                    r.wait("scalar", "sGP", r.cnt["sGP"])
                    r.wait("tensor", "sSY", nls)
                    for tag in ("q", "k"):
                        for n in range(8):
                            emit_projqk_group(0, tag, n)
                    state["nac_qk"] = r.cnt["sAC"]
                bhs = [(p, hip, sq) for hip in range(2) for sq in range(B_CORE)]
                r.wait("gpsimd", "sAC", r.cnt["sAC"])
                emit_prep(*bhs[0], 0, 0, with_bx=True)
                emit_prep_bx(*bhs[1])
                vsl = [(pS[0], 0), (pS[0], 512), (pS[1], 0), (pS[1], 512)]
                vcons = [0] * 4
                pre_units = [(bhs[0], u) for u in range(64)] + \
                            [(bhs[1], u) for u in range(16)]
                upos = [0]

                def pre_phA(slots=(0, 1, 2, 3)):
                    for _ in range(2):
                        if upos[0] < 80:
                            emit_phaseA_half(*pre_units[upos[0]][0],
                                             pre_units[upos[0]][1], slots=slots)
                            upos[0] += 1
                r.wait("tensor", "sDV", state["ndv_ep"])
                r.wait("tensor", "sAC", state["nac_ep"])
                r.wait("scalar", "sGC", max(state["ngp_add"]))
                r.wait("sync", "sGC", max(state["ngp_add"]))
                for half in range(2):
                    vev = [0] * 4
                    for n4 in range(4):
                        n = half * 4 + n4
                        slot = n % 2
                        r.wait("tensor", "sAC",
                               max(pcons[slot], state["pA_free_ac"][slot]))
                        r.wait("tensor", "sDV",
                               max(state["phA_dv4"][slot], state["ndv_ep"]))
                        for cc in range(8):
                            r.emit("tensor", lambda e, n=n, cc=cc, slot=slot:
                                   e.matmul(pA[:, slot * 512:(slot + 1) * 512],
                                            wsl["v"][:, cc * 128:(cc + 1) * 128],
                                            xnT[:, cc * T + n * 512:
                                                cc * T + (n + 1) * 512],
                                            start=(cc == 0), stop=(cc == 7)))
                        npe_p = r.cnt["sPE"]
                        r.wait("scalar", "sPE", npe_p)
                        pcons[slot] = r.emit("scalar", lambda e, n4=n4, slot=slot, p=p:
                                             e.activation(
                                                 vTf[:, n4 * 512:(n4 + 1) * 512],
                                                 pA[:, slot * 512:(slot + 1) * 512],
                                                 AF.Identity,
                                                 bias=bcol["v"][:, p:p + 1]))
                        vev[n4] = pcons[slot]
                        pre_phA(slots=(2, 3))
                    if half == 1 and p < 7:
                        r.wait("sync", "sPE", r.cnt["sPE"])
                        for tag in ("q", "k", "v"):
                            r.emit("sync", lambda e, tag=tag, p=p: e.dma_start(
                                wsl[tag].rearrange("p (cc j) -> p cc j", cc=8),
                                wT_sc[tag][p + 1].rearrange("(cc p) j -> p cc j",
                                                            p=128)))
                        state["nls_next"] = r.cnt["sSY"]
                    for tt4 in range(16):
                        tt = half * 16 + tt4
                        slot = tt % 4
                        vps, voff = vsl[slot]
                        r.wait("tensor", "sAC", max(vcons[slot], vev[tt4 // 4]))
                        npe_v = r.emit("tensor", lambda e, tt4=tt4, vps=vps, voff=voff:
                                       e.matmul(vps[:, voff:voff + 128],
                                                vTf[:, tt4 * 128:(tt4 + 1) * 128],
                                                id_bf[:, :],
                                                start=True, stop=True))
                        pre_phA()
                        r.wait("scalar", "sPE", npe_v)
                        r.emit("scalar", lambda e, tt=tt, vps=vps, voff=voff:
                               e.activation(vtm[:, tt * 130:tt * 130 + 64],
                                            vps[:, voff:voff + 64], AF.Copy))
                        vcons[slot] = r.emit(
                            "scalar", lambda e, tt=tt, vps=vps, voff=voff:
                            e.activation(vtm[:, tt * 130 + 65:tt * 130 + 129],
                                         vps[:, voff + 64:voff + 128], AF.Copy))
                state["npe_vtr"] = r.cnt["sPE"]
                r.wait("tensor", "sAC", r.cnt["sAC"])
                r.wait("gpsimd", "sAC", r.cnt["sAC"])

                for i in range(4):
                    par = i % 2
                    emit_xf_load(*bhs[i], par)
                    if i < 3:
                        emit_prep(*bhs[i + 1], (i + 1) % 2, state["chunk_pe_end"])
                    for kg in range(32):
                        units = []
                        if i == 0:
                            if kg == 16:
                                emit_prep_bx(*bhs[2])
                            if kg < 24:
                                u0 = 16 + 2 * kg
                                units = [(bhs[1], u0), (bhs[1], u0 + 1)]
                            else:
                                u0 = 2 * (kg - 24)
                                units = [(bhs[2], u0), (bhs[2], u0 + 1)]
                        elif i == 1:
                            if kg == 0:
                                emit_prep_bx(*bhs[3])
                            if kg >= 8:
                                u0 = 16 + 2 * (kg - 8)
                                units = [(bhs[2], u0), (bhs[2], u0 + 1)]
                        elif i == 2:
                            if kg >= 8:
                                u0 = 2 * (kg - 8)
                                units = [(bhs[3], u0), (bhs[3], u0 + 1)]
                        else:
                            if 8 <= kg < 16:
                                u0 = 48 + 2 * (kg - 8)
                                units = [(bhs[3], u0), (bhs[3], u0 + 1)]
                            elif kg >= 16 and p < 7:
                                if kg == 16:
                                    r.wait("tensor", "sSY", state["nls_next"])
                                    r.wait("scalar", "sGP", r.cnt["sGP"])
                                tag = "q" if kg < 24 else "k"
                                emit_projqk_group(p + 1, tag, kg % 8)
                                if kg == 31:
                                    state["nac_qk"] = r.cnt["sAC"]
                        emit_chunk_iter(*bhs[i], par, kg, units)
                    emit_av(*bhs[i], par, 31)
                    state["chunk_pe_end"] = r.cnt["sPE"]
                    emit_epilogue(*bhs[i], par, i == 3)
                if debug and p == 0:
                    r.barrier()
                    r.emit("gpsimd", lambda e: e.dma_start(dbg["d_mv"], mv_all[:, 0:32]))
                    r.emit("gpsimd", lambda e: e.dma_start(dbg["d_eps"], eps_all[:, 0:32]))
                    r.emit("gpsimd", lambda e: e.dma_start(dbg["d_qt"], qtp[:, 0:512]))
                    r.emit("gpsimd", lambda e: e.dma_start(dbg["d_ot"], outT[:, 0:1024]))
                    r.emit("gpsimd", lambda e: e.dma_start(dbg["d_stage"], stage[:, :]))
                    r.emit("gpsimd", lambda e: e.dma_start(dbg["d_la"], laT[:, :]))
                r.barrier(("sync", "tensor", "vector"))
            r.barrier()

        r.mark("P2")
        r.wait("gpsimd", "sGP", r.cnt["sGP"])
        r.wait("gpsimd", "sSY", r.cnt["sSY"])

        r.mark("ALL")
        lim = r.seq + 1
        if trunc is not None:
            lim = r.marks.get(trunc, trunc if isinstance(trunc, int) else r.seq + 1)
        with nc.Block() as block:
            @block.sync
            def _(e):
                for sq, f in r.ops["sync"]:
                    if sq <= lim:
                        f(e)

            @block.tensor
            def _(e):
                for sq, f in r.ops["tensor"]:
                    if sq <= lim:
                        f(e)

            @block.vector
            def _(e):
                for sq, f in r.ops["vector"]:
                    if sq <= lim:
                        f(e)

            @block.scalar
            def _(e):
                for sq, f in r.ops["scalar"]:
                    if sq <= lim:
                        f(e)

            @block.gpsimd
            def _(e):
                for sq, f in r.ops["gpsimd"]:
                    if sq <= lim:
                        f(e)

    return nc


def kernel(**inputs):
    X = np.ascontiguousarray(np.asarray(inputs["L0_attn_X"], np.float32))
    cst = _host_consts()
    base = {
        "w_q": np.ascontiguousarray(inputs["L0_attn_Wq"], dtype=np.float32),
        "w_k": np.ascontiguousarray(inputs["L0_attn_Wk"], dtype=np.float32),
        "w_v": np.ascontiguousarray(inputs["L0_attn_Wv"], dtype=np.float32),
        "w_r": np.ascontiguousarray(inputs["L0_ffn_W_r"], dtype=np.float32),
        "w_k2": np.ascontiguousarray(inputs["L0_ffn_W_k"], dtype=np.float32),
        "w_v2": np.ascontiguousarray(inputs["L0_ffn_W_v"], dtype=np.float32),
        "b_q": np.ascontiguousarray(inputs["L0_attn_bq"], dtype=np.float32).reshape(1, C),
        "b_k": np.ascontiguousarray(inputs["L0_attn_bk"], dtype=np.float32).reshape(1, C),
        "b_v": np.ascontiguousarray(inputs["L0_attn_bv"], dtype=np.float32).reshape(1, C),
        "b_mur": np.ascontiguousarray(inputs["L0_ffn_mu_r"], dtype=np.float32).reshape(1, C),
        "b_muk": np.ascontiguousarray(inputs["L0_ffn_mu_k"], dtype=np.float32).reshape(1, C),
    }
    base.update(cst)
    in_maps = []
    for core in range(8):
        m = dict(base)
        m["x"] = np.ascontiguousarray(
            X[2 * core:2 * core + 2].reshape(T, C))
        in_maps.append(m)
    nc = build_program(debug=False)
    res = run_bass_kernel_spmd(nc, in_maps, core_ids=list(range(8)))
    outs = [np.asarray(r["out"], np.float32).reshape(2, S, C) for r in res.results]
    return np.concatenate(outs, axis=0)
